# version 04
# baseline (speedup 1.0000x reference)
import numpy as np
from contextlib import ExitStack
import concourse.bass as bass
import concourse.mybir as mybir
from concourse.bass import IndirectOffsetOnAxis
from concourse.bass_utils import run_bass_kernel_spmd

F32 = mybir.dt.float32
BF16 = mybir.dt.bfloat16
U32 = mybir.dt.uint32
AF = mybir.ActivationFunctionType
ALU = mybir.AluOpType
AX = mybir.AxisListType

NCORE = 8
B = 4
SEQ = 4096
D = 1024
NLOC = SEQ // NCORE
T = B * NLOC
TT = T // 128
DT = D // 128
CIN = 2048
CT = CIN // 128
CSH = CIN // NCORE
NG = 32
Q = 64
NCH = NLOC // Q
NBI = B * NCH
NST = 16
TPAD = 144
LN_EPS = 1e-5
NSL = 8
CSL = CIN // NSL

PKH, PKK, NEXP, DIN0 = 8, 128, 16384, 2048
MH, MNE, MSE, MD = 8, 16, 256, 128
MES = MNE * MSE
RG8 = [[0, 1, 2, 3, 4, 5, 6, 7]]
A2A = CSH * NST * NBI


class St:

    def __init__(self, sem, step=1):
        self.e = None
        self.sem = sem
        self.step = step
        self.n = 0
        self.dma = False

    def put(self, instr, k=None):
        k = self.step if k is None else k
        instr.then_inc(self.sem, k)
        self.n += k

    def evs(self):
        return [(self.sem, self.n)]


class DmaSt:

    def __init__(self, sems):
        self.e = None
        self.pool = sems
        self.P = len(sems)
        self.idx = 0
        self.uses = [0] * self.P
        self.dma = True

    def sem_of(self, i):
        return self.pool[i % self.P], (i // self.P + 1) * 16

    def evs(self):
        return [(self.pool[j], self.uses[j] * 16) for j in range(self.P)
                if self.uses[j] > 0]


class Sched:

    KEYS = ("q", "g", "c", "t", "v", "s", "q2")

    def __init__(self, ST):
        self.ST = ST
        self.ops = {k: [] for k in self.KEYS}
        self.lbl = {}

    _AUTO = [0]
    DMA_KEYS = ("q", "g", "q2")

    def add(self, key, fn, deps=(), label=None):
        idx = len(self.ops[key])
        if label is None:
            Sched._AUTO[0] += 1
            label = f"_auto{Sched._AUTO[0]}"
        assert label not in self.lbl, f"dup label {label}"
        self.lbl[label] = (key, idx)
        flat = []
        for d in deps:
            if isinstance(d, list):
                flat.extend(d)
            else:
                flat.append(d)
        snap = {k: len(self.ops[k]) for k in self.DMA_KEYS}
        self.ops[key].append((fn, tuple(flat), label, snap))
        return label

    def emit(self, blk, ext_deps=()):
        openers = {"q": blk.sync, "g": blk.gpsimd, "c": blk.gpsimd,
                   "t": blk.tensor, "v": blk.vector, "s": blk.scalar,
                   "q2": blk.scalar}
        ST = self.ST
        bases = {k: (ST[k].idx if k in self.DMA_KEYS else ST[k].n)
                 for k in self.KEYS}

        def val_of(d, snap):
            if not isinstance(d, str):
                return d
            k, i = self.lbl[d]
            if k in self.DMA_KEYS:
                return ST[k].sem_of(bases[k] + i)
            return (ST[k].sem, bases[k] + i + 1)

        for k in self.KEYS:
            if not self.ops[k]:
                continue
            ops = self.ops[k]

            def body(eng, k=k, ops=ops):
                st = ST[k]
                st.e = eng
                waited = {}
                for ev in ext_deps:
                    if ev is not None and ev[1] > 0:
                        eng.wait_ge(ev[0], ev[1])
                        waited[id(ev[0])] = ev[1]
                for fn, deps, label, snap in ops:
                    for d in deps:
                        if d is None:
                            continue
                        sem, v = val_of(d, snap)
                        if waited.get(id(sem), -1) < v:
                            eng.wait_ge(sem, v)
                            waited[id(sem)] = v
                    if st.dma:
                        sem = st.pool[st.idx % st.P]
                        prior = st.uses[st.idx % st.P]
                        if prior > 0 and waited.get(id(sem), -1) < prior * 16:
                            eng.wait_ge(sem, prior * 16)
                            waited[id(sem)] = prior * 16
                        r = fn(eng)
                        r.then_inc(sem, 16)
                        st.uses[st.idx % st.P] += 1
                        st.idx += 1
                    else:
                        r = fn(eng)
                        if r is not None:
                            st.put(r)
                        else:
                            eng.sem_inc(st.sem, st.step)
                            st.n += st.step
            openers[k](body)
        evs = {}
        for k in self.KEYS:
            evs[k] = ST[k].evs()
        return evs


def build_program(dbg=None, nblocks=4, phases="ABCDEFGHI", csub=4):
    nc = bass.Bass("TRN2", target_bir_lowering=False, enable_partition_id=False)

    def din(name, shape, dt=F32):
        return nc.dram_tensor(name, list(shape), dt, kind="ExternalInput").ap()

    def dint(name, shape, dt=F32, shared=False):
        kw = dict(kind="Internal")
        if shared:
            kw["addr_space"] = "Shared"
        return nc.dram_tensor(name, list(shape), dt, **kw).ap()

    x_in = din("x_slice", (T, D))
    cid_in = din("core_id", (1, 1), U32)
    w_in = din("w_in", (D, CIN))
    b_in = din("b_in", (1, CIN))
    w_gate = din("w_gate", (D, CIN))
    b_gate = din("b_gate", (1, CIN))
    w_out = din("w_out", (CIN, D))
    b_out = din("b_out", (1, D))
    a_sh = din("A_shard", (CSH, NST, NST))
    bssm_sh = din("B_shard", (CSH, NST))
    cssm_sh = din("C_shard", (CSH, NST))

    identf = din("c_identf", (128, 128))

    pk_keysT = din("pk_keysT", (2, PKH, 512, PKK))
    pk_wq = din("pk_wq", (2 * PKH * 512, D))
    pk_w_inT = din("pk_w_inT", (128, DT, DIN0))
    pk_w_outT = din("pk_w_outT", (DT, 128, DIN0 // 128, 128))
    pk_in_a = din("pk_in_a", (NEXP, D))
    pk_in_b = din("pk_in_b", (NEXP, DIN0))
    pk_out_a = din("pk_out_a", (NEXP, DIN0))
    pk_out_b = din("pk_out_b", (NEXP, D))

    la_wqT = din("la_wqT", (D, D))
    la_wkT = din("la_wkT", (D, D))
    la_wvT = din("la_wvT", (D, D))
    la_woT = din("la_woT", (DT, 128, DT, 128))
    la_bq = din("la_bq", (1, D))
    la_bk = din("la_bk", (1, D))
    la_bv = din("la_bv", (1, D))
    la_bo = din("la_bo", (1, D))
    la_beta = din("la_beta", (1, 16))

    mo_wmhT = din("mo_wmhT", (128, DT, D))
    mo_wrT = din("mo_wrT", (D, MNE))
    mo_kfT = din("mo_kfT", (MH, 128, MES))
    mo_vfT = din("mo_vfT", (MH, 128, 32, 128))
    mo_wmgT = din("mo_wmgT", (DT, 128, DT, 128))

    onesf_r = din("c_onesf_r", (1, 512))
    onesf_c = din("c_onesf_c", (128, 1))
    bdones = din("c_bdones", (128, 8))

    out_t = nc.dram_tensor("out", [T, D], F32, kind="ExternalOutput").ap()
    dbg_t = None
    if dbg is not None:
        dbg_t = nc.dram_tensor("dbg", [128, DT * T], F32,
                               kind="ExternalOutput").ap()

    MATC = Q + 2 * Q * NST
    ct_dram = dint("ct_dram", (NST, CSH))
    bt_dram = dint("bt_dram", (NST, CSH))
    mat_sh = dint("mat_sh", (1, MATC * CSH), BF16)
    mat_all = dint("mat_all", (NCORE, MATC * CSH), BF16, shared=True)
    k_dram = dint("k_dram", (Q, CIN), BF16)
    kx_dram = dint("kx_dram", (Q, Q, CIN), BF16)
    v_dram = dint("v_dram", (Q, NST, CIN), BF16)
    u_dram = dint("u_dram", (Q, NST, CIN), BF16)
    t_dram = dint("t_dram", (Q, TPAD, CIN), BF16)
    xpf_dram = dint("xpf_dram", (Q, B, NCH, CIN), BF16)
    gate_dram = dint("gate_dram", (CT, 128, T), BF16)
    y_dram = dint("y_dram", (CT, 128, T), BF16)
    yl_dram = dint("yl_dram", (128, 512, Q), BF16)
    ga_in = dint("ga_in", (NCORE, A2A), BF16)
    ga_all = dint("ga_all", (NCORE, NCORE * A2A), BF16, shared=True)
    ga_out = dint("ga_out", (NCORE, A2A), BF16)
    ha_in = dint("ha_in", (NCORE, A2A), BF16)
    ha_all = dint("ha_all", (NCORE, NCORE * A2A), BF16, shared=True)
    ha_out = dint("ha_out", (NCORE, A2A), BF16)
    xln2_dram = dint("xln2_dram", (TT, 128, D), BF16)
    scb_dram = dint("scb_dram", (1, 16))
    scb2_dram = dint("scb2_dram", (1, D))
    wt_dram = dint("wt_dram", (MNE, T))

    _es1 = ExitStack()
    R = _es1.enter_context(nc.sbuf_tensor("R", [128, DT, T], F32))
    IDF = _es1.enter_context(nc.sbuf_tensor("IDF", [128, 128], F32))
    IDB = _es1.enter_context(nc.sbuf_tensor("IDB", [128, 128], BF16))
    ONESR = _es1.enter_context(nc.sbuf_tensor("ONESR", [1, 512], BF16))
    ONESRF = _es1.enter_context(nc.sbuf_tensor("ONESRF", [1, 512], F32))
    ONESCF = _es1.enter_context(nc.sbuf_tensor("ONESCF", [128, 1], F32))
    ONESCB = _es1.enter_context(nc.sbuf_tensor("ONESCB", [128, 1], BF16))
    POOLQ = [_es1.enter_context(nc.semaphore(f"pq{i}")) for i in range(20)]
    POOLG = [_es1.enter_context(nc.semaphore(f"pg{i}")) for i in range(8)]
    POOLQ2 = [_es1.enter_context(nc.semaphore(f"pq2_{i}")) for i in range(12)]
    CIDT = _es1.enter_context(nc.sbuf_tensor("CIDT", [1, 1], U32))
    JNK = _es1.enter_context(nc.sbuf_tensor("JNK", [1, 32], F32))
    BDONES = _es1.enter_context(nc.sbuf_tensor("BDONES", [128, 8], BF16))
    BIASR = _es1.enter_context(nc.sbuf_tensor("BIASR", [1, 5, 2048], BF16))
    _esBD = ExitStack()
    BD64 = _esBD.enter_context(nc.sbuf_tensor("BD64", [128, NG * 128], BF16))
    BD512 = _esBD.enter_context(nc.sbuf_tensor("BD512", [128, NG * 128], BF16))
    sem_q = _es1.enter_context(nc.semaphore("sq"))
    sem_g = _es1.enter_context(nc.semaphore("sg"))
    sem_c = _es1.enter_context(nc.semaphore("scc"))
    sem_t = _es1.enter_context(nc.semaphore("stt"))
    sem_v = _es1.enter_context(nc.semaphore("svv"))
    sem_s = _es1.enter_context(nc.semaphore("sss"))
    sem_q2 = _es1.enter_context(nc.semaphore("sq2"))
    blk = _es1.enter_context(nc.Block())
    if True:
        ST = {"q": DmaSt(POOLQ), "g": DmaSt(POOLG), "c": St(sem_c, 1),
              "t": St(sem_t, 1), "v": St(sem_v, 1), "s": St(sem_s, 1),
              "q2": DmaSt(POOLQ2)}

        BSLOT = {"b_in": (0, 0), "b_gate": (1, 0), "b_out": (2, 0),
                 "bq": (2, 1024), "bk": (3, 0), "bv": (3, 1024), "bo": (4, 0)}

        def brow(nm, lo, w):
            sl, off = BSLOT[nm]
            return BIASR.ap()[0:1, sl, off + lo:off + lo + w]

        def dbg_dump(sc, src_ap_fn, deps, nelem=DT * T):
            sc.add("q", lambda e: e.dma_start(
                out=dbg_t[:, 0:nelem], in_=src_ap_fn()), deps=deps,
                label="dbgdump")

        _es2 = ExitStack()
        XTOK = _es2.enter_context(nc.sbuf_tensor("XTOK", [128, 8 * D], F32))
        BDAF = _es2.enter_context(nc.sbuf_tensor("BDAF", [128, NG * 128], F32))
        BCOLF = _es2.enter_context(nc.sbuf_tensor("BCOLF", [128, NG], F32))
        BDONESF = _es2.enter_context(nc.sbuf_tensor("BDONESF", [128, 8], F32))
        BSCR = _es2.enter_context(nc.sbuf_tensor("BSCR", [1, 2048], F32))
        PSTR0 = _es2.enter_context(nc.psum_tensor("PSTR0", [128, 128], F32))
        PSTR1 = _es2.enter_context(nc.psum_tensor("PSTR1", [128, 128], F32))
        BDA = _es2.enter_context(nc.sbuf_tensor("BDA", [128, NG * 128], BF16))
        CCOLF = _es2.enter_context(nc.sbuf_tensor("CCOLF", [128, NG], F32))
        CCOLB = _es2.enter_context(nc.sbuf_tensor("CCOLB", [128, NG], BF16))
        BCOLB = _es2.enter_context(nc.sbuf_tensor("BCOLB", [128, NG], BF16))
        BDN = _es2.enter_context(nc.sbuf_tensor("BDN", [128, 128], BF16))
        BDT = _es2.enter_context(nc.sbuf_tensor("BDT", [128, 128], BF16))
        USTK = _es2.enter_context(nc.sbuf_tensor("USTK", [128, Q], BF16))
        VSTK = _es2.enter_context(nc.sbuf_tensor("VSTK", [128, Q], BF16))
        KTMP = _es2.enter_context(nc.sbuf_tensor("KTMP", [128, Q], BF16))
        KT8 = _es2.enter_context(nc.sbuf_tensor("KT8", [8, Q], BF16))
        KSB = _es2.enter_context(nc.sbuf_tensor("KSB", [Q, CSH], BF16))
        VSB = _es2.enter_context(nc.sbuf_tensor("VSB", [Q, NST * CSH], BF16))
        USB = _es2.enter_context(nc.sbuf_tensor("USB", [Q, NST * CSH], BF16))
        PSMA = _es2.enter_context(nc.psum_tensor("PSMA", [128, 128], F32))
        PSMB = _es2.enter_context(nc.psum_tensor("PSMB", [128, 128], F32))
        PSMT = _es2.enter_context(nc.psum_tensor("PSMT", [128, 128], BF16))
        if True:
            sc = Sched(ST)
            l_idf = sc.add("q", lambda e: e.dma_start(out=IDF[:], in_=identf))
            l_xh = []
            l_xh.append(sc.add("q", lambda e: e.dma_start(
                out=XTOK.ap().rearrange("p (a b) -> p a b", a=8),
                in_=x_in.rearrange("(a p) b -> p a b", p=128)
                [:, 0:8, :]), deps=[], label="l_x0"))
            l_idb = sc.add("v", lambda e: e.tensor_copy(out=IDB[:], in_=IDF[:]),
                           deps=[l_idf])
            sc.add("q", lambda e: e.dma_start(out=ONESRF[:], in_=onesf_r),
                   label="l_orf")
            sc.add("v", lambda e: e.tensor_copy(out=ONESR[:], in_=ONESRF[:]),
                   deps=["l_orf"], label="l_or")
            sc.add("q", lambda e: e.dma_start(out=ONESCF[:], in_=onesf_c),
                   label="l_ocf")
            l_bdo0 = sc.add("q", lambda e: e.dma_start(out=BDONESF[:],
                                                       in_=bdones))
            l_bdo = sc.add("v", lambda e: e.tensor_copy(out=BDONES[:],
                                                        in_=BDONESF[:]),
                           deps=[l_bdo0])
            sc.add("v", lambda e: e.tensor_copy(out=ONESCB[:], in_=ONESCF[:]),
                   deps=["l_ocf"], label="l_ocb")
            sc.add("q", lambda e: e.dma_start(out=CIDT[:], in_=cid_in),
                   label="l_cid")
            sc.add("v", lambda e: e.memset(JNK[:], 0.0), label="l_jnk")
            bias_l = []
            prevb = None
            for nm, src, w in [("b_in", b_in, 2048), ("b_gate", b_gate, 2048),
                               ("b_out", b_out, 1024), ("bq", la_bq, 1024),
                               ("bk", la_bk, 1024), ("bv", la_bv, 1024),
                               ("bo", la_bo, 1024)]:
                lb = sc.add("q", lambda e, src=src, w=w: e.dma_start(
                    out=BSCR.ap()[0:1, 0:w], in_=src),
                    deps=[prevb] if prevb else [], label=f"lbf_{nm}")
                prevb = sc.add("v", lambda e, nm=nm, w=w: e.tensor_copy(
                    out=brow(nm, 0, w), in_=BSCR.ap()[0:1, 0:w]),
                    deps=[lb], label=f"l_{nm}")
                bias_l.append(prevb)
            l_xh.append(sc.add("q", lambda e: e.dma_start(
                out=XTOK.ap().rearrange("p (a b) -> p a b", a=8),
                in_=x_in.rearrange("(a p) b -> p a b", p=128)
                [:, 8:16, :]), deps=["trv63"], label="l_x1"))

            tr_l = []
            for i in range(TT * DT):
                tt, k = divmod(i, DT)
                ps = PSTR0 if i % 2 == 0 else PSTR1
                deps = [l_xh[i // 64], l_idf] + ([f"trv{i - 2}"] if i >= 2
                                                 else [])
                sc.add("t", lambda e, ps=ps, i=i: e.transpose(
                    ps.ap(), XTOK.ap()[:, (i % 64) * 128:(i % 64 + 1) * 128],
                    IDF[:]), deps=deps, label=f"trt{i}")
                sc.add("v", lambda e, ps=ps, tt=tt, k=k: e.tensor_copy(
                    out=R.ap()[:, k, tt * 128:(tt + 1) * 128], in_=ps.ap()),
                    deps=[f"trt{i}"], label=f"trv{i}")
                tr_l.append(f"trv{i}")

            m_z = sc.add("v", lambda e: e.memset(BDAF[:], 0.0))
            bd_loads0 = []
            for gsub in range(8):
                for c4 in range(4):
                    bd_loads0.append(sc.add("q", lambda e, gsub=gsub, c4=c4:
                        e.dma_start(
                            out=BDAF.ap().rearrange("p (g f) -> p g f", g=NG)
                            [gsub * 16:gsub * 16 + 16, c4::4,
                             gsub * 16:gsub * 16 + 16],
                            in_=a_sh[4 * gsub + c4::32, :, :]
                            .rearrange("gb n m -> n gb m")), deps=[m_z]))
            bdc = sc.add("v", lambda e: e.tensor_copy(out=BDA[:], in_=BDAF[:]),
                         deps=bd_loads0 + [m_z], label="l_bdc")
            colq = []
            from concourse.ap import AP as _APc
            ctr = sc.add("q", lambda e: e.dma_start(
                out=_APc(ct_dram.tensor, 0, [[CSH, NST], [1, CSH]]),
                in_=_APc(cssm_sh.tensor, 0, [[1, NST], [NST, CSH]]),
                allow_slow_non_contiguous=True))
            btr = sc.add("q", lambda e: e.dma_start(
                out=_APc(bt_dram.tensor, 0, [[CSH, NST], [1, CSH]]),
                in_=_APc(bssm_sh.tensor, 0, [[1, NST], [NST, CSH]]),
                allow_slow_non_contiguous=True))
            for gsub in range(8):
                for c4 in range(4):
                    colq.append(sc.add("q", lambda e, gsub=gsub, c4=c4:
                        e.dma_start(
                            out=_APc(CCOLF, gsub * 16 * NG + c4,
                                     [[NG, 16], [4, 8]]),
                            in_=_APc(ct_dram.tensor, 4 * gsub + c4,
                                     [[CSH, NST], [32, 8]]),
                            allow_slow_non_contiguous=True),
                        deps=[ctr]))
                    colq.append(sc.add("q", lambda e, gsub=gsub, c4=c4:
                        e.dma_start(
                            out=_APc(BCOLF, gsub * 16 * NG + c4,
                                     [[NG, 16], [4, 8]]),
                            in_=_APc(bt_dram.tensor, 4 * gsub + c4,
                                     [[CSH, NST], [32, 8]]),
                            allow_slow_non_contiguous=True),
                        deps=[btr]))
            ccb = sc.add("v", lambda e: e.tensor_copy(out=CCOLB[:],
                                                      in_=CCOLF[:]),
                         deps=colq, label="l_ccb")
            bcb = sc.add("v", lambda e: e.tensor_copy(out=BCOLB[:],
                                                      in_=BCOLF[:]),
                         deps=colq, label="l_bcb")
            bd_loads = [bdc, ccb, bcb] + colq

            mat_end = []
            for g_ in range(NG):
                P = f"m{g_}"
                pre = ([f"m{g_ - 1}end"] if g_ > 0
                       else bd_loads + [l_idb, l_bdo, "l_ocf"])
                BDAg = BDA.ap()[:, g_ * 128:(g_ + 1) * 128]

                tp0 = sc.add("t", lambda e, BDAg=BDAg: e.transpose(
                    PSMT.ap(), BDAg, IDB[:]), deps=pre, label=P + "tp0")
                bdt0 = sc.add("v", lambda e: e.tensor_copy(
                    out=BDT[:], in_=PSMT.ap()), deps=[tp0], label=P + "bdt0")
                bdn0 = sc.add("v", lambda e, BDAg=BDAg: e.tensor_copy(
                    out=BDN[:], in_=BDAg), deps=pre, label=P + "bdn0")
                v0 = sc.add("v", lambda e, g_=g_: e.tensor_copy(
                    out=VSTK.ap()[:, 0:1], in_=BCOLB.ap()[:, g_:g_ + 1]),
                    deps=pre, label=P + "v0")
                u1m = sc.add("t", lambda e, g_=g_, BDAg=BDAg: e.matmul(
                    PSMA.ap()[:, 0:1], BDAg, CCOLB.ap()[:, g_:g_ + 1],
                    start=True, stop=True), deps=pre + [tp0], label=P + "u1m")
                u0 = sc.add("v", lambda e: e.tensor_copy(
                    out=USTK.ap()[:, 0:1], in_=PSMA.ap()[:, 0:1]),
                    deps=[u1m], label=P + "u0")
                lastU, lastV, lastN, lastT = u0, v0, bdn0, bdt0
                lastSQN = lastSQT = None
                for k in range(6):
                    m = 1 << k
                    us = sc.add("t", lambda e, m=m: e.matmul(
                        PSMA.ap()[:, 0:m], BDN[:], USTK.ap()[:, 0:m],
                        start=True, stop=True),
                        deps=[lastN, lastU, u0], label=P + f"us{k}")
                    vs = sc.add("t", lambda e, m=m: e.matmul(
                        PSMB.ap()[:, 0:m], BDT[:], VSTK.ap()[:, 0:m],
                        start=True, stop=True),
                        deps=[lastT, lastV], label=P + f"vs{k}")
                    ue = sc.add("v", lambda e, m=m: e.tensor_copy(
                        out=USTK.ap()[:, m:2 * m], in_=PSMA.ap()[:, 0:m]),
                        deps=[us], label=P + f"ue{k}")
                    ve = sc.add("v", lambda e, m=m: e.tensor_copy(
                        out=VSTK.ap()[:, m:2 * m], in_=PSMB.ap()[:, 0:m]),
                        deps=[vs], label=P + f"ve{k}")
                    lastU, lastV = ue, ve
                    sqn = sc.add("t", lambda e: e.matmul(
                        PSMA.ap(), BDT[:], BDN[:], start=True, stop=True),
                        deps=[us, vs, ue, lastN, lastT], label=P + f"sqn{k}")
                    sqt = sc.add("t", lambda e: e.matmul(
                        PSMB.ap(), BDN[:], BDT[:], start=True, stop=True),
                        deps=[us, vs, ve, sqn], label=P + f"sqt{k}")
                    nn_ = sc.add("v", lambda e: e.tensor_copy(
                        out=BDN[:], in_=PSMA.ap()), deps=[sqn, sqt],
                        label=P + f"nn{k}")
                    tt_ = sc.add("v", lambda e: e.tensor_copy(
                        out=BDT[:], in_=PSMB.ap()), deps=[sqn, sqt],
                        label=P + f"tt{k}")
                    if k == 5:
                        sv64 = sc.add("v", lambda e, g_=g_: e.tensor_copy(
                            out=BD64.ap()[:, g_ * 128:(g_ + 1) * 128],
                            in_=PSMB.ap()), deps=[sqn, sqt], label=P + "sv64")
                    lastN, lastT = nn_, tt_
                for k2 in range(2):
                    sqn = sc.add("t", lambda e: e.matmul(
                        PSMA.ap(), BDT[:], BDN[:], start=True, stop=True),
                        deps=[lastN, lastT] + ([sv64] if k2 == 0 else []),
                        label=P + f"p2n{k2}")
                    sqt = sc.add("t", lambda e: e.matmul(
                        PSMB.ap(), BDN[:], BDT[:], start=True, stop=True),
                        deps=[lastN, lastT, sqn] +
                             ([sv64] if k2 == 0 else []),
                        label=P + f"p2t{k2}")
                    nn_ = sc.add("v", lambda e: e.tensor_copy(
                        out=BDN[:], in_=PSMA.ap()), deps=[sqn, sqt],
                        label=P + f"p2nn{k2}")
                    tt_ = sc.add("v", lambda e: e.tensor_copy(
                        out=BDT[:], in_=PSMB.ap()), deps=[sqn, sqt],
                        label=P + f"p2tt{k2}")
                    lastN, lastT = nn_, tt_
                sq512 = sc.add("t", lambda e: e.matmul(
                    PSMB.ap(), BDN[:], BDT[:], start=True, stop=True),
                    deps=[lastN, lastT], label=P + "sq512")
                sv512 = sc.add("v", lambda e, g_=g_: e.tensor_copy(
                    out=BD512.ap()[:, g_ * 128:(g_ + 1) * 128],
                    in_=PSMB.ap()), deps=[sq512], label=P + "sv512")

                ktm = sc.add("v", lambda e, g_=g_: e.tensor_scalar(
                    out=KTMP[:], in0=VSTK[:], scalar1=CCOLF.ap()[:, g_:g_ + 1],
                    scalar2=None, op0=ALU.mult), deps=[lastV], label=P + "ktm")
                kmm = sc.add("t", lambda e: e.matmul(
                    PSMA.ap()[0:8, 0:Q], BDONES[:], KTMP[:],
                    start=True, stop=True), deps=[ktm, sq512], label=P + "kmm")
                kev = sc.add("v", lambda e: e.tensor_copy(
                    out=KT8[:], in_=PSMA.ap()[0:8, 0:Q]), deps=[kmm],
                    label=P + "kev")
                ktr = sc.add("t", lambda e: e.transpose(
                    PSMT.ap()[0:Q, 0:8], KT8[:], IDB.ap()[0:8, 0:8]),
                    deps=[kev], label=P + "ktr")
                gb, c4 = divmod(g_, 4)
                off = 32 * gb + c4
                kse = sc.add("v", lambda e, off=off: e.tensor_copy(
                    out=KSB.ap()[:, off:off + 29:4],
                    in_=PSMT.ap()[0:Q, 0:8]), deps=[ktr], label=P + "kse")
                utr = sc.add("t", lambda e: e.transpose(
                    PSMT.ap()[0:Q, :], USTK[:], IDB[:]),
                    deps=[lastU, ktr, kse], label=P + "utr")
                use = sc.add("v", lambda e, off=off: e.tensor_copy(
                    out=USB.ap().rearrange("p (n c) -> p n c", n=NST)
                    [:, :, off:off + 29:4],
                    in_=PSMT.ap()[0:Q, :].rearrange("p (g n) -> p g n", g=8)
                    .transpose([0, 2, 1])), deps=[utr], label=P + "use")
                vtr = sc.add("t", lambda e: e.transpose(
                    PSMT.ap()[0:Q, :], VSTK[:], IDB[:]),
                    deps=[ktm, use], label=P + "vtr")
                vse = sc.add("v", lambda e, off=off: e.tensor_copy(
                    out=VSB.ap().rearrange("p (n c) -> p n c", n=NST)
                    [:, :, off:off + 29:4],
                    in_=PSMT.ap()[0:Q, :].rearrange("p (g n) -> p g n", g=8)
                    .transpose([0, 2, 1])), deps=[vtr], label=P + "vse")
                sc.add("v", lambda e: e.tensor_copy(
                    out=KT8.ap()[0:1, 0:1], in_=KT8.ap()[0:1, 0:1]),
                    deps=[kse, use, vse, sv64, sv512], label=f"m{g_}end")
                mat_end.append(f"m{g_}end")

            mk = sc.add("q", lambda e: e.dma_start(
                out=mat_sh[0:1, 0:Q * CSH]
                .rearrange("o (p c) -> (o p) c", p=Q), in_=KSB.ap()),
                deps=mat_end)
            mv = sc.add("q", lambda e: e.dma_start(
                out=mat_sh[0:1, Q * CSH:(Q + Q * NST) * CSH]
                .rearrange("o (p c) -> (o p) c", p=Q), in_=VSB.ap()),
                deps=mat_end)
            mu = sc.add("q", lambda e: e.dma_start(
                out=mat_sh[0:1, (Q + Q * NST) * CSH:]
                .rearrange("o (p c) -> (o p) c", p=Q), in_=USB.ap()),
                deps=mat_end)
            ag = sc.add("c", lambda e: e.collective_compute(
                "AllGather", ALU.bypass, replica_groups=RG8,
                ins=[mat_sh], outs=[mat_all]), deps=[mk, mv, mu])
            KO, VO = Q * CSH, (Q + Q * NST) * CSH
            mre = []
            for r in range(NCORE):
                mre.append(sc.add("q", lambda e, r=r: e.dma_start(
                    out=k_dram[:, r * CSH:(r + 1) * CSH],
                    in_=mat_all[r:r + 1, 0:KO]
                    .rearrange("o (j c) -> (o j) c", j=Q)), deps=[ag]))
                mre.append(sc.add("q", lambda e, r=r: e.dma_start(
                    out=v_dram.rearrange("j n c -> (j n) c")
                    [:, r * CSH:(r + 1) * CSH],
                    in_=mat_all[r:r + 1, KO:VO]
                    .rearrange("o (j c) -> (o j) c", j=Q * NST)), deps=[ag]))
                mre.append(sc.add("q", lambda e, r=r: e.dma_start(
                    out=u_dram.rearrange("j n c -> (j n) c")
                    [:, r * CSH:(r + 1) * CSH],
                    in_=mat_all[r:r + 1, VO:]
                    .rearrange("o (j c) -> (o j) c", j=Q * NST)), deps=[ag]))
            if dbg == "R0":
                dbg_dump(sc, lambda: R.ap().rearrange("p a b -> p (a b)"), tr_l)
            evA = sc.emit(blk)
            _es2.close()
        prev = [e for v in evA.values() for e in v]
        ev_mats = evA["q"]
        ev_R0 = evA["v"]

        _es3 = ExitStack()
        XLN = _es3.enter_context(nc.sbuf_tensor("XLN", [128, DT, T], BF16))
        SQH = _es3.enter_context(nc.sbuf_tensor("SQH", [128, DT, 512], BF16))
        ROWS = _es3.enter_context(nc.sbuf_tensor("ROWS", [1, 3, T], F32))
        LNA = _es3.enter_context(nc.sbuf_tensor("LNA", [128, 2, 512], F32))
        LNB = _es3.enter_context(nc.sbuf_tensor("LNB", [128, 2, 512], F32))
        WSB = _es3.enter_context(nc.sbuf_tensor("WSB", [128, DT, 1024], BF16))
        XTM = _es3.enter_context(nc.sbuf_tensor("XTM", [128, 2, CIN], BF16))
        GTM = _es3.enter_context(nc.sbuf_tensor("GTM", [128, 2, 512], BF16))
        PSS0 = _es3.enter_context(nc.psum_tensor("PSS0", [1, 512], F32))
        PSS1 = _es3.enter_context(nc.psum_tensor("PSS1", [1, 512], F32))
        PSQ0 = _es3.enter_context(nc.psum_tensor("PSQ0", [1, 512], F32))
        PSQ1 = _es3.enter_context(nc.psum_tensor("PSQ1", [1, 512], F32))
        PSB0 = _es3.enter_context(nc.psum_tensor("PSB0", [128, 512], F32))
        PSB1 = _es3.enter_context(nc.psum_tensor("PSB1", [128, 512], F32))
        if True:
            sc = Sched(ST)
            emit_ln(sc, R, XLN, SQH, ROWS, LNA, LNB, PSS0, PSS1, PSQ0, PSQ1,
                    PSB0, PSB1, ONESCF, ONESCB, ONESRF, "ln0")
            for half in range(2):
                lw = sc.add("g", lambda e, half=half: e.dma_start(
                    out=WSB.ap(),
                    in_=w_in.rearrange("(a p) c -> p a c", p=128)
                    [:, :, half * 1024:(half + 1) * 1024]),
                    deps=([f"xpev{half - 1}_{TT - 1}_{2 * half - 1}"]
                          if half > 0 else []),
                    label=f"lwin{half}")
                for tt in range(TT):
                    strips = []
                    for s2 in range(2):
                        pb = PSB0 if (2 * tt + s2) % 2 == 0 else PSB1
                        s4 = half * 2 + s2
                        dep = [lw, f"ln0_ap{(tt // 4) * DT + DT - 1}"]
                        if tt == 0 and half == 0:
                            dep += ["ln0_bav2", "ln0_bbv2", "ln0_bav3",
                                    "ln0_bbv3"]
                        if tt >= 1:
                            dep += [f"xpev{half}_{tt - 1}_{s2}"]
                        if tt >= 2:
                            dep += [f"xpdma{half}_{tt - 2}_0",
                                    f"xpdma{half}_{tt - 2}_1"]
                        sc.add("t", lambda e, s4=s4, pb=pb: e.matmul(
                            pb.ap(), ONESR.ap()[0:1, 0:128],
                            brow("b_in", s4 * 512, 512), start=True,
                            stop=False), deps=dep, label=f"xpb{half}_{tt}_{s2}")
                        for k in range(DT):
                            sc.add("t", lambda e, s2=s2, k=k, tt=tt, pb=pb:
                                e.matmul(
                                    pb.ap(),
                                    XLN.ap()[:, k, tt * 128:(tt + 1) * 128],
                                    WSB.ap()[:, k, s2 * 512:(s2 + 1) * 512],
                                    start=False, stop=(k == DT - 1)),
                                deps=[f"xpb{half}_{tt}_{s2}"],
                                label=f"xpm{half}_{tt}_{s2}_{k}")
                        strips.append(sc.add("s",
                            lambda e, s4=s4, tt=tt, pb=pb: e.copy(
                                out=XTM.ap()[:, tt % 2,
                                             s4 * 512:(s4 + 1) * 512],
                                in_=pb.ap()),
                            deps=[f"xpm{half}_{tt}_{s2}_{DT - 1}"],
                            label=f"xpev{half}_{tt}_{s2}"))
                    b, i2 = divmod(tt, 4)
                    for h in range(2):
                        sc.add("q", lambda e, tt=tt, b=b, i2=i2, h=h,
                               half=half: e.dma_start(
                            out=xpf_dram[:, b, i2 * 2 + h,
                                         half * 1024:(half + 1) * 1024],
                            in_=XTM.ap()[h * 64:(h + 1) * 64, tt % 2,
                                         half * 1024:(half + 1) * 1024]),
                            deps=strips, label=f"xpdma{half}_{tt}_{h}")
            xp_last = f"xpdma1_{TT - 1}_1"
            for half in range(2):
                lg = sc.add("g", lambda e, half=half: e.dma_start(
                    out=WSB.ap(),
                    in_=w_gate.rearrange("(a p) c -> p a c", p=128)
                    [:, :, half * 1024:(half + 1) * 1024]),
                    deps=[xp_last] if half == 0 else
                         [f"gtdma{8 * half - 1}_3"],
                    label=f"lwg{half}")
                for ct8 in range(8):
                    ct = half * 8 + ct8
                    for tc in range(4):
                        i = ct * 4 + tc
                        pb = PSB0 if i % 2 == 0 else PSB1
                        dep0 = [lg, xp_last]
                        if i >= 2:
                            pi = i - 2
                            dep0 += [f"gtev{pi // 4}_{pi % 4}",
                                     f"gtdma{pi // 4}_{pi % 4}"]
                        sc.add("t", lambda e, ct=ct, pb=pb: e.matmul(
                            pb.ap(), brow("b_gate", ct * 128, 128),
                            ONESR.ap()[0:1, 0:512], start=True, stop=False),
                            deps=dep0 + [f"ln0_ap{tc * DT + DT - 1}"],
                            label=f"gtb{ct}_{tc}")
                        for k in range(DT):
                            sc.add("t", lambda e, ct8=ct8, tc=tc, k=k, pb=pb:
                                e.matmul(
                                    pb.ap(),
                                    WSB.ap()[:, k, ct8 * 128:(ct8 + 1) * 128],
                                    XLN.ap()[:, k, tc * 512:(tc + 1) * 512],
                                    start=False, stop=(k == DT - 1)),
                                deps=[f"gtb{ct}_{tc}"],
                                label=f"gtm{ct}_{tc}_{k}")
                        sc.add("s", lambda e, ct=ct, tc=tc, pb=pb:
                            e.activation(
                                out=GTM.ap()[:, (ct * 4 + tc) % 2, :],
                                in_=pb.ap(), func=AF.Sigmoid),
                            deps=[f"gtm{ct}_{tc}_{DT - 1}"],
                            label=f"gtev{ct}_{tc}")
                        sc.add("q", lambda e, ct=ct, tc=tc: e.dma_start(
                            out=gate_dram[ct, :, tc * 512:(tc + 1) * 512],
                            in_=GTM.ap()[:, (ct * 4 + tc) % 2, :]),
                            deps=[f"gtev{ct}_{tc}"], label=f"gtdma{ct}_{tc}")
            if dbg == "XLN0":
                dbg_dump(sc, lambda: R.ap().rearrange("p a b -> p (a b)"),
                         [f"ln0_ap{i}" for i in range(32)])
            evB = sc.emit(blk, ext_deps=prev)
            _es3.close()
        prev = [e for v in evB.values() for e in v]
        ev_xp = evB["q2"]

        if "C" in phases:
            _es4 = ExitStack()
            ZSB = _es4.enter_context(nc.sbuf_tensor("ZSB", [128, 2048], BF16))
            TWIN = _es4.enter_context(nc.sbuf_tensor("TWIN", [128, 80 * CSL], BF16))
            XPW = _es4.enter_context(nc.sbuf_tensor("XPW", [128, NBI * CSL], BF16))
            YLT = _es4.enter_context(nc.sbuf_tensor("YLT", [128, 8, Q], BF16))
            GA = _es4.enter_context(nc.sbuf_tensor("GA", [128, 512, NST], BF16))
            GAT = _es4.enter_context(nc.sbuf_tensor("GAT", [128, Q * 128], BF16))
            PCS = [_es4.enter_context(nc.psum_tensor(f"PC{i}", [128, 80], F32))
                   for i in range(4)]
            PGT = _es4.enter_context(nc.psum_tensor("PGT", [128, 128], BF16))
            PGT1 = _es4.enter_context(nc.psum_tensor("PGT1", [128, 128], BF16))
            if True:
                sc = Sched(ST)
                zv = sc.add("v", lambda e: e.memset(ZSB[:], 0.0))
                zga = sc.add("v", lambda e: e.memset(
                    GA.ap().rearrange("p g n -> p (g n)"), 0.0), label="zga")
                zgat = sc.add("v", lambda e: e.memset(GAT[:], 0.0), label="zgat")
                zd = []
                TDTOT = Q * TPAD * CIN
                ZCH = 128 * 2048
                nz = (TDTOT + ZCH - 1) // ZCH
                from concourse.ap import AP as _AP
                for zi in range(nz):
                    lo = zi * ZCH
                    cnt = min(ZCH, TDTOT - lo)
                    zd.append(sc.add("q", lambda e, lo=lo, cnt=cnt: e.dma_start(
                        out=_AP(t_dram.tensor, lo, [[1, cnt]]),
                        in_=ZSB.ap()[:, 0:cnt // 128]), deps=[zv]))
                kx0 = sc.add("q", lambda e: e.dma_start(
                    out=kx_dram[0:1, :, :].rearrange("o j c -> (o j) c"),
                    in_=k_dram), deps=[ev_mats])
                kxl = kx0
                for m in range(6):
                    w = 1 << m
                    kxl = sc.add("q", lambda e, w=w: e.dma_start(
                        out=kx_dram[w:2 * w, :, :].rearrange("s j c -> (s j) c"),
                        in_=kx_dram[0:w, :, :].rearrange("s j c -> (s j) c")),
                        deps=[kxl])
                tb_diag = sc.add("q", lambda e: e.dma_start(
                    out=bass_diag_dst(t_dram), in_=kx_dram),
                    deps=[kxl, ev_mats] + zd)
                from concourse.ap import AP as _APg2
                tb_g = sc.add("q", lambda e: e.dma_start(
                    out=_APg2(t_dram.tensor, 128 * CIN,
                              [[CIN, NST], [TPAD * CIN, Q], [1, CIN]]),
                    in_=_APg2(v_dram.tensor, (Q - 1) * NST * CIN,
                              [[CIN, NST], [-NST * CIN, Q], [1, CIN]])),
                    deps=zd + [ev_mats])
                conv_slices = range(NSL) if csub >= 2 else range(0)
                conv_mm = csub >= 3
                for cs in conv_slices:
                    ldeps = [tb_diag, tb_g, ev_xp] \
                        + ([f"c1ev{cs - 1}_63", f"c1gv{cs - 1}_63"]
                           if (cs > 0 and conv_mm) else [])
                    lt_parts = []
                    for tg8 in range(8):
                        lt_parts.append(sc.add("q", lambda e, cs=cs, tg8=tg8:
                            e.dma_start(
                                out=TWIN.ap()
                                .rearrange("p (tg c) -> p tg c", tg=80)
                                [0:Q, tg8 * 8:(tg8 + 1) * 8, :],
                                in_=t_dram[:, tg8 * 8:(tg8 + 1) * 8,
                                           cs * CSL:(cs + 1) * CSL]),
                            deps=ldeps, label=f"ctl{cs}_{tg8}"))
                    lt2_parts = []
                    for tg2 in range(2):
                        lt2_parts.append(sc.add("q", lambda e, cs=cs, tg2=tg2:
                            e.dma_start(
                                out=TWIN.ap()
                                .rearrange("p (tg c) -> p tg c", tg=80)
                                [0:Q, Q + tg2 * 8:Q + (tg2 + 1) * 8, :],
                                in_=t_dram[:, 128 + tg2 * 8:128 + (tg2 + 1) * 8,
                                           cs * CSL:(cs + 1) * CSL]),
                            deps=ldeps, label=f"ctl2{cs}_{tg2}"))
                    lx_parts = []
                    for bq_ in range(4):
                        lx_parts.append(sc.add("q", lambda e, cs=cs, bq_=bq_:
                            e.dma_start(
                                out=XPW.ap()
                                .rearrange("p (bi c) -> p bi c", bi=NBI)
                                [0:Q, bq_ * 8:(bq_ + 1) * 8, :],
                                in_=xpf_dram.rearrange("s b i c -> s (b i) c")
                                [:, bq_ * 8:(bq_ + 1) * 8,
                                 cs * CSL:(cs + 1) * CSL]),
                            deps=ldeps, label=f"cxl{cs}_{bq_}"))
                    lt = lt_parts
                    lt2 = lt2_parts
                    lx = lx_parts
                    for g4 in (range(64) if conv_mm else range(0)):
                        ps = PCS[g4 % 4]
                        g512 = cs * 64 + g4
                        mms = []
                        for j in range(4):
                            cl = g4 * 4 + j
                            dep = [lt, lt2, lx]
                            if g4 >= 4:
                                dep += [f"c1ev{cs}_{g4 - 4}", f"c1gv{cs}_{g4 - 4}"]
                            mms.append(sc.add("t", lambda e, cl=cl, j=j, ps=ps:
                                e.matmul(
                                    ps.ap()[32 * j:32 * (j + 1), :],
                                    XPW.ap().rearrange("p (bi c) -> p bi c",
                                                       bi=NBI)[0:Q, :, cl],
                                    TWIN.ap().rearrange("p (tg c) -> p tg c",
                                                        tg=80)[0:Q, :, cl],
                                    start=True, stop=True,
                                    tile_position=(0, 32 * j)),
                                deps=dep, label=f"c1m{cs}_{g4}_{j}"))
                        sc.add("s", lambda e, g4=g4, ps=ps: e.copy(
                            out=YLT.ap()[:, g4 % 8, :], in_=ps.ap()[:, 0:Q]),
                            deps=mms + ([f"yld{cs}_{g4 // 8 - 1}"]
                                        if g4 >= 8 else
                                        ([f"yld{cs - 1}_7"] if cs > 0 else [])),
                            label=f"c1ev{cs}_{g4}")
                        sc.add("s", lambda e, g512=g512, ps=ps: e.copy(
                            out=GA.ap()[:, g512, :], in_=ps.ap()[:, Q:80]),
                            deps=mms + ["zga"], label=f"c1gv{cs}_{g4}")
                        if g4 % 8 == 7:
                            sc.add("q", lambda e, cs=cs, g4=g4: e.dma_start(
                                out=yl_dram[:, cs * 64 + g4 - 7:cs * 64 + g4 + 1, :],
                                in_=YLT.ap()),
                                deps=[f"c1ev{cs}_{gg}" for gg in
                                      range(g4 - 7, g4 + 1)],
                                label=f"yld{cs}_{g4 // 8}")
                if csub < 3:
                    ga_tr = None
                ga_tr = []
                ga_range = range(Q) if csub >= 4 else range(0)
                for gb_ in ga_range:
                    pg = PGT if gb_ % 2 == 0 else PGT1
                    tr = sc.add("t", lambda e, gb_=gb_, pg=pg: e.transpose(
                        pg.ap(),
                        GA.ap().rearrange("p g n -> p (g n)")
                        [:, gb_ * 128:(gb_ + 1) * 128], IDB[:]),
                        deps=[f"c1gv{(gb_ * 8) // 64}_{(gb_ * 8) % 64 + 7}",
                              (f"gatv{gb_ - 2}" if gb_ >= 2 else None)],
                        label=f"gatt{gb_}")
                    ga_tr.append(sc.add("v", lambda e, gb_=gb_, pg=pg: e.tensor_copy(
                        out=GAT.ap()[:, gb_ * 128:(gb_ + 1) * 128], in_=pg.ap()),
                        deps=[tr, "zgat"], label=f"gatv{gb_}"))
                ga_send = []
                from concourse.ap import AP as _APg
                for gs_ in (range(8) if csub >= 4 else range(0)):
                    ga_send.append(sc.add("q", lambda e, gs_=gs_: e.dma_start(
                        out=_APg(ga_in.tensor, gs_ * 1024,
                                 [[8192, NST], [A2A, NCORE], [128, 8], [1, 128]]),
                        in_=GAT.ap()[gs_ * 16:(gs_ + 1) * 16, :]
                        .rearrange("p (q gbl f) -> p q gbl f", q=NCORE, gbl=8)),
                        deps=ga_tr, label=f"gas{gs_}"))
                a2a1 = None
                if csub >= 4:
                    a2a1 = sc.add("c", lambda e: e.collective_compute(
                        "AllGather", ALU.bypass, replica_groups=RG8,
                        ins=[ga_in.rearrange("r f -> () (r f)")],
                        outs=[ga_all]), deps=ga_send)
                from concourse.ap import AP as _AP2

                def slice_my(e, src_all, dst):
                    qv = e.value_load(in_tensor=CIDT.ap()[0:1, 0:1],
                                      min_val=0, max_val=NCORE - 1)
                    return e.dma_start(
                        out=dst,
                        in_=_AP2(src_all.tensor, qv * A2A,
                                 [[NCORE * A2A, NCORE], [1, A2A]]))
                a2a1s = sc.add("q", lambda e: slice_my(e, ga_all, ga_out),
                               deps=[a2a1], label="ga_slice")
                evC = sc.emit(blk, ext_deps=prev)
                _es4.close()
            prev = [e for v in evC.values() for e in v]

        ev_a2a1 = evC["c"] if ("C" in phases and csub >= 4) else None

        if "D" in phases:
            _es5 = ExitStack()
            GSH = _es5.enter_context(nc.sbuf_tensor("GSH", [128, NCORE * 1024], BF16))
            SHISTL = _es5.enter_context(nc.sbuf_tensor("SHISTL", [128, 4, 9, NBI], BF16))
            HSUPHL = _es5.enter_context(nc.sbuf_tensor("HSUPHL", [128, 4, 9, B], BF16))
            TCCL = _es5.enter_context(nc.sbuf_tensor("TCCL", [128, 4, NBI], BF16))
            HINS = _es5.enter_context(nc.sbuf_tensor("HINS", [128, NCORE * 1024], BF16))
            PDS = [_es5.enter_context(nc.psum_tensor(f"PD{i}", [128, NBI], F32))
                   for i in range(8)]
            if True:
                sc = Sched(ST)
                zhins = sc.add("v", lambda e: e.memset(HINS[:], 0.0),
                               label="zhins")
                gsh_l = []
                from concourse.ap import AP as _APh
                for gs_ in range(8):
                    gsh_l.append(sc.add("q", lambda e, gs_=gs_: e.dma_start(
                        out=GSH.ap()[gs_ * 16:(gs_ + 1) * 16, :]
                        .rearrange("p (r f) -> p r f", r=NCORE),
                        in_=_APh(ga_out.tensor, gs_ * 1024,
                                 [[8192, NST], [A2A, NCORE], [1, 1024]])),
                        deps=[], label=f"gshl{gs_}"))
                ha_sends = []
                for g_ in range(NG):
                    Pn = f"d{g_}"
                    ln_ = g_ % 4
                    SHIST = SHISTL.ap()[:, ln_]
                    HSUPH = HSUPHL.ap()[:, ln_]
                    TCC = TCCL.ap()[:, ln_]
                    PD0 = PDS[ln_ * 2]
                    PD1 = PDS[ln_ * 2 + 1]
                    gb, c4 = divmod(g_, 4)
                    A64g = BD64.ap()[:, g_ * 128:(g_ + 1) * 128]
                    A512g = BD512.ap()[:, g_ * 128:(g_ + 1) * 128]

                    def gcols(i, g_=g_):
                        gb_, c4_ = divmod(g_, 4)
                        off = (gb_ * 4 + c4_) * 32 + i
                        return GSH.ap().rearrange(
                            "p (r gb c4 b i) -> p r gb c4 b i",
                            r=NCORE, gb=8, c4=4, b=B)[:, :, gb_, c4_, :, i] \
                            .transpose([0, 2, 1])
                    pre = gsh_l + ([f"d{g_ - 4}done"] if g_ >= 4 else [])
                    z0 = sc.add("v", lambda e, SHIST=SHIST: e.memset(
                        SHIST[:, 0, :], 0.0), deps=pre, label=Pn + "z0")
                    zh = sc.add("v", lambda e, HSUPH=HSUPH: e.memset(
                        HSUPH[:, 0, :], 0.0), deps=pre, label=Pn + "zh")
                    lastv = z0
                    for i in range(1, 9):
                        ps = PD0 if i % 2 == 0 else PD1
                        mm = sc.add("t", lambda e, i=i, ps=ps, A64g=A64g,
                                    SHIST=SHIST: e.matmul(
                            ps.ap(), A64g, SHIST[:, i - 1, :],
                            start=True, stop=True), deps=[lastv], label=Pn + f"am{i}")
                        lastv = sc.add("v", lambda e, i=i, ps=ps, g_=g_,
                                       SHIST=SHIST: e.tensor_add(
                            SHIST[:, i, :], ps.ap(), gcols(i - 1, g_)),
                            deps=[mm], label=Pn + f"av{i}")
                    lastb = zh
                    for r in range(1, 9):
                        ps = PD0 if r % 2 == 0 else PD1
                        mm = sc.add("t", lambda e, r=r, ps=ps, A512g=A512g,
                                    HSUPH=HSUPH: e.matmul(
                                ps.ap()[:, 0:B], A512g,
                                HSUPH[:, r - 1, :],
                                start=True, stop=True),
                            deps=[lastb, lastv], label=Pn + f"bm{r}")
                        lastb = sc.add("v", lambda e, r=r, ps=ps, HSUPH=HSUPH,
                                       SHIST=SHIST: e.tensor_add(
                            HSUPH[:, r, :], ps.ap()[:, 0:B],
                            SHIST[:, 8, r - 1::8]), deps=[mm],
                            label=Pn + f"bv{r}")
                    tc0 = sc.add("v", lambda e, TCC=TCC, HSUPH=HSUPH:
                        e.tensor_copy(
                            out=TCC.rearrange("p (b r) -> p b r", b=B),
                            in_=HSUPH[:, 0:8, :].transpose([0, 2, 1])),
                        deps=[lastb], label=Pn + "tc0")
                    lastc = tc0
                    hoff = (gb * 4 + c4) * 32
                    for i in range(8):
                        he = sc.add("v", lambda e, i=i, hoff=hoff, TCC=TCC,
                                    SHIST=SHIST: e.tensor_add(
                            HINS.ap().rearrange(
                                "p (r gb c4 b i) -> p r gb c4 b i",
                                r=NCORE, gb=8, c4=4, b=B)
                            [:, :, gb, c4, :, i].transpose([0, 2, 1]),
                            TCC.rearrange("p (b r) -> p b r", b=B),
                            SHIST[:, i, :]
                            .rearrange("p (b r) -> p b r", b=B)),
                            deps=[lastc, "zhins"], label=Pn + f"he{i}")
                        if i < 7:
                            ps = PD0 if i % 2 == 0 else PD1
                            mm = sc.add("t", lambda e, ps=ps, A64g=A64g,
                                        TCC=TCC: e.matmul(
                                ps.ap(), A64g, TCC, start=True, stop=True),
                                deps=[he], label=Pn + f"cm{i}")
                            lastc = sc.add("v", lambda e, ps=ps, TCC=TCC:
                                e.tensor_copy(out=TCC, in_=ps.ap()),
                                deps=[mm], label=Pn + f"cv{i}")
                        else:
                            lastc = he
                    sc.add("v", lambda e, TCC=TCC: e.tensor_copy(
                        out=TCC[0:1, 0:1], in_=TCC[0:1, 0:1]),
                        deps=[lastc] + [Pn + f"he{i}" for i in range(8)],
                        label=f"d{g_}done")
                hdone = [f"d{g_}done" for g_ in range(NG)]
                for gs_ in range(8):
                    ha_sends.append(sc.add("q", lambda e, gs_=gs_: e.dma_start(
                        out=_APh(ha_in.tensor, gs_ * 1024,
                                 [[8192, NST], [A2A, NCORE], [1, 1024]]),
                        in_=HINS.ap()[gs_ * 16:(gs_ + 1) * 16, :]
                        .rearrange("p (r f) -> p r f", r=NCORE)),
                        deps=hdone, label=f"has{gs_}"))
                a2a2 = sc.add("c", lambda e: e.collective_compute(
                    "AllGather", ALU.bypass, replica_groups=RG8,
                    ins=[ha_in.rearrange("r f -> () (r f)")], outs=[ha_all]),
                    deps=ha_sends)
                from concourse.ap import AP as _AP3

                def slice_my2(e):
                    qv = e.value_load(in_tensor=CIDT.ap()[0:1, 0:1],
                                      min_val=0, max_val=NCORE - 1)
                    return e.dma_start(
                        out=ha_out,
                        in_=_AP3(ha_all.tensor, qv * A2A,
                                 [[NCORE * A2A, NCORE], [1, A2A]]))
                a2a2s = sc.add("q", lambda e: slice_my2(e),
                               deps=[a2a2],
                               label="ha_slice")
                evD = sc.emit(blk, ext_deps=prev)
                _es5.close()
            _esBD.close()
            prev = [e for v in evD.values() for e in v]

        ev_a2a2 = evD["c"] if "D" in phases else None

        if "E" in phases:
            _es6 = ExitStack()
            HF = _es6.enter_context(nc.sbuf_tensor("HF", [128, 8192], BF16))
            UWIN = _es6.enter_context(nc.sbuf_tensor("UWIN", [128, Q * CSL], BF16))
            YLS0 = _es6.enter_context(nc.sbuf_tensor("YLS0", [128, 64, Q], BF16))
            YLS1 = _es6.enter_context(nc.sbuf_tensor("YLS1", [128, 64, Q], BF16))
            CARR = _es6.enter_context(nc.sbuf_tensor("CARR", [128, 4, Q], F32))
            PES = [_es6.enter_context(nc.psum_tensor(f"PE{i}", [128, Q], F32))
                   for i in range(4)]
            if True:
                from concourse.ap import AP as _AP
                sc = Sched(ST)
                for cs in range(NSL):
                    hfl = sc.add("q", lambda e, cs=cs: e.dma_start(
                        out=HF.ap()[64:64 + NST, :],
                        in_=ha_out[cs:cs + 1, :]
                        .rearrange("o (n f) -> (o n) f", n=NST)),
                        deps=[(f"c2ev{cs - 1}_63" if cs > 0 else None)],
                        label=f"hfl{cs}")
                    YLS = YLS0 if cs % 2 == 0 else YLS1
                    ul = sc.add("q", lambda e, cs=cs: e.dma_start(
                        out=UWIN.ap().rearrange("p (t c) -> p t c", t=Q)
                        [64:64 + NST, :, :],
                        in_=u_dram[:, :, cs * CSL:(cs + 1) * CSL]
                        .transpose([1, 0, 2])),
                        deps=[(f"c2ev{cs - 1}_63" if cs > 0 else None)],
                        label=f"ul{cs}")
                    yll = sc.add("q", lambda e, cs=cs, YLS=YLS: e.dma_start(
                        out=YLS.ap(), in_=yl_dram[:, cs * 64:(cs + 1) * 64, :]),
                        deps=[(f"yrf{2 * (cs - 2) + 1}" if cs >= 2 else None)],
                        label=f"yll{cs}")
                    for g4 in range(64):
                        ps = PES[g4 % 4]
                        mms = []
                        for j in range(4):
                            c = cs * CSL + g4 * 4 + j
                            rr = c % CSH
                            gbv, r2 = divmod(rr, 32)
                            gsv, c4v = divmod(r2, 4)
                            hoff = ((gsv * 8 + gbv) * 4 + c4v) * 32
                            dep = [hfl, ul, yll]
                            if g4 >= 4:
                                dep += [f"c2sc{cs}_{g4 - 4}"]
                            mms.append(sc.add("t", lambda e, j=j, ps=ps,
                                              hoff=hoff, cl=g4 * 4 + j: e.matmul(
                                ps.ap()[32 * j:32 * (j + 1), :],
                                HF.ap().rearrange("p (f b i) -> p f b i",
                                                  b=B, i=NCH)
                                [64:64 + NST, hoff // 32, :, :],
                                UWIN.ap().rearrange("p (t c) -> p t c", t=Q)
                                [64:64 + NST, :, cl],
                                start=True, stop=True,
                                tile_position=(64, 32 * j)),
                                deps=dep, label=f"c2m{cs}_{g4}_{j}"))
                        sc.add("s", lambda e, g4=g4, ps=ps: e.copy(
                            out=CARR.ap()[:, g4 % 4, :], in_=ps.ap()),
                            deps=mms + ([f"c2ev{cs}_{g4 - 4}"] if g4 >= 4 else
                                        ([f"c2ev{cs - 1}_{g4 + 60}"]
                                         if cs > 0 else [])),
                            label=f"c2sc{cs}_{g4}")
                        sc.add("v", lambda e, g4=g4, YLS=YLS: e.tensor_add(
                            YLS.ap()[:, g4, :], CARR.ap()[:, g4 % 4, :],
                            YLS.ap()[:, g4, :]),
                            deps=[f"c2sc{cs}_{g4}"], label=f"c2ev{cs}_{g4}")
                    for h2 in range(2):
                        ct = cs * 2 + h2
                        FR = 64 * Q
                        for c4 in range(4):
                            sc.add("q", lambda e, ct=ct, h2=h2, YLS=YLS, FR=FR,
                                   c4=c4: e.dma_start(
                                out=_AP(y_dram.tensor, ct * 128 * T + c4 * T,
                                        [[Q, 32], [4 * T, 32], [1, Q]]),
                                in_=_AP(YLS, c4 * 32 * FR + h2 * 32 * Q,
                                        [[FR, 32], [Q, 32], [1, Q]])),
                                deps=[f"c2ev{cs}_{32 * h2 + gg}"
                                      for gg in range(32)],
                                label=f"yrf{ct}_{c4}")
                        sc.add("v", lambda e, ct=ct: e.tensor_copy(
                            out=JNK.ap()[0:1, ct:ct + 1],
                            in_=JNK.ap()[0:1, ct + 16:ct + 17]),
                            deps=[f"yrf{ct}_{cc}" for cc in range(4)],
                            label=f"yrf{ct}")
                evE = sc.emit(blk, ext_deps=prev)
                _es6.close()
            prev = [e for v in evE.values() for e in v]

        ev_yrf = evE["q2"] if "E" in phases else None

        if "F" in phases:
            _es7 = ExitStack()
            YGM = _es7.enter_context(nc.sbuf_tensor("YGM", [128, CT, T], BF16))
            GSL = _es7.enter_context(nc.sbuf_tensor("GSL", [128, T], BF16))
            WOUT = _es7.enter_context(nc.sbuf_tensor("WOUT", [128, CT, D], BF16))
            PF0 = _es7.enter_context(nc.psum_tensor("PF0", [128, 512], F32))
            PF1 = _es7.enter_context(nc.psum_tensor("PF1", [128, 512], F32))
            PF2 = _es7.enter_context(nc.psum_tensor("PF2", [128, 512], F32))
            PF3 = _es7.enter_context(nc.psum_tensor("PF3", [128, 512], F32))
            PF4 = _es7.enter_context(nc.psum_tensor("PF4", [128, 512], F32))
            PF5 = _es7.enter_context(nc.psum_tensor("PF5", [128, 512], F32))
            PF6 = _es7.enter_context(nc.psum_tensor("PF6", [128, 512], F32))
            PF7 = _es7.enter_context(nc.psum_tensor("PF7", [128, 512], F32))
            if True:
                PFS = [PF0, PF1, PF2, PF3, PF4, PF5, PF6, PF7]
                sc = Sched(ST)
                lwo = sc.add("g", lambda e: e.dma_start(
                    out=WOUT.ap(),
                    in_=w_out.rearrange("(a p) d -> p a d", p=128)),
                    label="lwo")
                for ct in range(CT):
                    ly = sc.add("q", lambda e, ct=ct: e.dma_start(
                        out=YGM.ap()[:, ct, :], in_=y_dram[ct, :, :]),
                        deps=[], label=f"ly{ct}")
                    lgt = sc.add("q", lambda e, ct=ct: e.dma_start(
                        out=GSL.ap(), in_=gate_dram[ct, :, :]),
                        deps=[f"ygm{ct - 1}" if ct > 0 else None],
                        label=f"lg{ct}")
                    sc.add("v", lambda e, ct=ct: e.tensor_mul(
                        YGM.ap()[:, ct, :], YGM.ap()[:, ct, :], GSL.ap()),
                        deps=[ly, lgt], label=f"ygm{ct}")
                for tc in range(4):
                    for dt_ in range(DT):
                        ps = PFS[dt_]
                        sc.add("t", lambda e, dt_=dt_, ps=ps: e.matmul(
                            ps.ap(), brow("b_out", dt_ * 128, 128),
                            ONESR.ap()[0:1, :], start=True, stop=False),
                            deps=[lwo, (f"rup{tc - 1}_{dt_}" if tc > 0 else None)] +
                                 ([f"ygm{ct}" for ct in range(CT)] if dt_ == 0
                                  else []),
                            label=f"wob{tc}_{dt_}")
                        for ct in range(CT):
                            sc.add("t", lambda e, ct=ct, dt_=dt_, tc=tc, ps=ps:
                                e.matmul(
                                    ps.ap(),
                                    WOUT.ap()[:, ct, dt_ * 128:(dt_ + 1) * 128],
                                    YGM.ap()[:, ct, tc * 512:(tc + 1) * 512],
                                    start=False, stop=(ct == CT - 1)),
                                deps=[f"wob{tc}_{dt_}"],
                                label=f"wom{tc}_{dt_}_{ct}")
                        sc.add("v", lambda e, tc=tc, dt_=dt_, ps=ps: e.tensor_add(
                            R.ap()[:, dt_, tc * 512:(tc + 1) * 512], ps.ap(),
                            R.ap()[:, dt_, tc * 512:(tc + 1) * 512]),
                            deps=[f"wom{tc}_{dt_}_{CT - 1}"],
                            label=f"rup{tc}_{dt_}")
                r1_done = [f"rup{tc}_{dt_}" for tc in range(4)
                           for dt_ in range(DT)]
                if dbg == "R1":
                    dbg_dump(sc, lambda: R.ap().rearrange("p a b -> p (a b)"),
                             r1_done)
                evF = sc.emit(blk, ext_deps=prev)
                _es7.close()
            prev = [e for v in evF.values() for e in v]

        if "G" in phases:
            _esG0 = ExitStack()
            XLN2 = _esG0.enter_context(nc.sbuf_tensor("XLN2", [128, DT, T], BF16))
            SQH2 = _esG0.enter_context(nc.sbuf_tensor("SQH2", [128, DT, 512], BF16))
            ROWS2 = _esG0.enter_context(nc.sbuf_tensor("ROWS2", [1, 3, T], F32))
            LNA2 = _esG0.enter_context(nc.sbuf_tensor("LNA2", [128, 2, 512], F32))
            LNB2 = _esG0.enter_context(nc.sbuf_tensor("LNB2", [128, 2, 512], F32))
            PGS0 = _esG0.enter_context(nc.psum_tensor("PGS0", [1, 512], F32))
            PGS1 = _esG0.enter_context(nc.psum_tensor("PGS1", [1, 512], F32))
            PGQ0 = _esG0.enter_context(nc.psum_tensor("PGQ0", [1, 512], F32))
            PGQ1 = _esG0.enter_context(nc.psum_tensor("PGQ1", [1, 512], F32))
            PGB0 = _esG0.enter_context(nc.psum_tensor("PGB0", [128, 512], F32))
            PGB1 = _esG0.enter_context(nc.psum_tensor("PGB1", [128, 512], F32))
            if True:
                sc = Sched(ST)
                emit_ln(sc, R, XLN2, SQH2, ROWS2, LNA2, LNB2, PGS0, PGS1,
                        PGQ0, PGQ1, PGB0, PGB1, ONESCF, ONESCB, ONESRF, "ln1")
                for tt in range(TT):
                    sc.add("q", lambda e, tt=tt: e.dma_start(
                        out=xln2_dram[tt].rearrange("p (a b) -> p a b", a=DT),
                        in_=XLN2.ap()[:, :, tt * 128:(tt + 1) * 128]),
                        deps=[f"ln1_ap{(tt // 4) * DT + k}" for k in range(DT)],
                        label=f"x2s{tt}")
                evG0 = sc.emit(blk, ext_deps=prev)
                _esG0.close()
            prev = [e for v in evG0.values() for e in v]

            _esG23 = ExitStack()
            IDXS = _esG23.enter_context(nc.sbuf_tensor("IDXS", [128, TT, PKH], U32))
            GATES = _esG23.enter_context(nc.sbuf_tensor("GATES", [128, TT, PKH], F32))
            LINS = _esG23.enter_context(nc.sbuf_tensor("LINS", [128, TT, PKH], F32))
            _esG12 = ExitStack()
            WKT = _esG12.enter_context(
                nc.sbuf_tensor("WKT", [128, DT, 16, PKK], BF16))

            _esG1 = ExitStack()
            KSTS = _esG1.enter_context(nc.sbuf_tensor("KSTS", [128, 2, 4, 128], BF16))
            WQS = _esG1.enter_context(nc.sbuf_tensor("WQS", [128, 2, 4, D], BF16))
            WKN = _esG1.enter_context(nc.sbuf_tensor("WKN", [128, 2, D], BF16))
            PWK = _esG1.enter_context(nc.psum_tensor("PWK", [128, 2, 512], F32))
            PTK0 = _esG1.enter_context(nc.psum_tensor("PTK0", [128, 128], BF16))
            PTK1 = _esG1.enter_context(nc.psum_tensor("PTK1", [128, 128], BF16))
            if True:
                sc = Sched(ST)
                for ph in range(16):
                    pp, hh = divmod(ph, PKH)
                    pb = ph % 2
                    sdep = [f"wkm{ph - 2}_3_1"] if ph >= 2 else []
                    lk = sc.add("g", lambda e, pp=pp, hh=hh, pb=pb: e.dma_start(
                        out=KSTS.ap()[:, pb, :, :],
                        in_=pk_keysT[pp, hh]
                        .rearrange("(a p) k -> p a k", p=128)),
                        deps=sdep, label=f"wklk{ph}")
                    lw = []
                    for pch in range(4):
                        lw.append(sc.add("g", lambda e, ph=ph, pch=pch, pb=pb:
                            e.dma_start(
                                out=WQS.ap()[:, pb, pch, :],
                                in_=pk_wq[ph * 512 + pch * 128:
                                          ph * 512 + (pch + 1) * 128, :]),
                            deps=sdep, label=f"wklw{ph}_{pch}"))
                    for pch in range(4):
                        for nch in range(2):
                            dep = [lk, lw[pch]]
                            if pch == 0 and ph >= 1:
                                dep += [f"wknv{ph - 1}_0", f"wknv{ph - 1}_1"]
                            sc.add("t", lambda e, pch=pch, nch=nch, pb=pb:
                                e.matmul(
                                    PWK.ap()[:, nch, :],
                                    KSTS.ap()[:, pb, pch, :],
                                    WQS.ap()[:, pb, pch,
                                             nch * 512:(nch + 1) * 512],
                                    start=(pch == 0), stop=(pch == 3)),
                                deps=dep, label=f"wkm{ph}_{pch}_{nch}")
                    for nch in range(2):
                        sc.add("v", lambda e, pb=pb, nch=nch: e.tensor_copy(
                            out=WKN.ap()[:, pb, nch * 512:(nch + 1) * 512],
                            in_=PWK.ap()[:, nch, :]),
                            deps=[f"wkm{ph}_3_{nch}"] +
                                 ([f"wktt{ph - 2}_{4 * nch + 3}"]
                                  if ph >= 2 else []),
                            label=f"wknv{ph}_{nch}")
                    for dch in range(DT):
                        pt = PTK0 if dch % 2 == 0 else PTK1
                        pdep = []
                        if dch >= 2:
                            pdep = [f"wkte{ph}_{dch - 2}"]
                        elif ph >= 1:
                            pdep = [f"wkte{ph - 1}_{6 + dch}"]
                        tr = sc.add("t", lambda e, pt=pt, pb=pb, dch=dch:
                            e.transpose(
                                pt.ap(),
                                WKN.ap()[:, pb, dch * 128:(dch + 1) * 128],
                                IDB[:]),
                            deps=[f"wknv{ph}_{dch // 4}"] + pdep,
                            label=f"wktt{ph}_{dch}")
                        sc.add("s", lambda e, pt=pt, dch=dch, ph=ph: e.copy(
                            out=WKT.ap()[:, dch, ph, :], in_=pt.ap()),
                            deps=[tr], label=f"wkte{ph}_{dch}")
                evG1 = sc.emit(blk, ext_deps=prev)
                _esG1.close()
            prev = [e for v in evG1.values() for e in v]

            _esG2 = ExitStack()
            XL = _esG2.enter_context(nc.sbuf_tensor("XL", [128, 2, DT, 128], BF16))
            SIMB = _esG2.enter_context(nc.sbuf_tensor("SIMB", [128, 2048], F32))
            X2TT = _esG2.enter_context(nc.sbuf_tensor("X2TT", [128, 2, D], BF16))
            AINH = _esG2.enter_context(nc.sbuf_tensor("AINH", [128, 2, D], BF16))
            TMPD = _esG2.enter_context(nc.sbuf_tensor("TMPD", [128, D], BF16))
            M8ALL = _esG2.enter_context(nc.sbuf_tensor("M8ALL", [128, 16, 8], F32))
            MIALL = _esG2.enter_context(nc.sbuf_tensor("MIALL", [128, 16, 8], U32))
            MIFALL = _esG2.enter_context(nc.sbuf_tensor("MIFALL", [128, 16], F32))
            SCALL = _esG2.enter_context(nc.sbuf_tensor("SCALL", [128, 8], F32))
            IDXF8 = _esG2.enter_context(nc.sbuf_tensor("IDXF8", [128, 8], F32))
            DOTA = _esG2.enter_context(nc.sbuf_tensor("DOTA", [128, 1], F32))
            IDXF = _esG2.enter_context(nc.sbuf_tensor("IDXF", [128, 1], F32))
            PSIM = _esG2.enter_context(nc.psum_tensor("PSIM", [128, 4, 512], F32))
            PXT0 = _esG2.enter_context(nc.psum_tensor("PXT0", [128, 128], BF16))
            PXT1 = _esG2.enter_context(nc.psum_tensor("PXT1", [128, 128], BF16))
            if True:
                sc = Sched(ST)
                for tt in range(TT):
                    sl = tt % 2
                    lx = sc.add("q", lambda e, tt=tt, sl=sl: e.dma_start(
                        out=XL.ap()[:, sl, :, :].rearrange("p a b -> p (a b)"),
                        in_=xln2_dram[tt]),
                        deps=([f"simm{tt - 2}_7_3", f"x2tt{tt - 2}_7"]
                              if tt >= 2 else []),
                        label=f"xl{tt}")
                    for dch in range(DT):
                        for q4 in range(4):
                            dep = [lx]
                            if dch == 0 and tt >= 1:
                                dep.append(f"sime{tt - 1}")
                            sc.add("t", lambda e, dch=dch, q4=q4, sl=sl:
                                e.matmul(
                                    PSIM.ap()[:, q4, :],
                                    XL.ap()[:, sl, dch, :],
                                    WKT.ap()[:, dch, 4 * q4:4 * (q4 + 1), :]
                                    .rearrange("p a b -> p (a b)"),
                                    start=(dch == 0), stop=(dch == DT - 1)),
                                deps=dep, label=f"simm{tt}_{dch}_{q4}")
                    dep = [f"simm{tt}_{DT - 1}_{q4}" for q4 in range(4)]
                    if tt >= 1:
                        dep.append(f"mif{tt - 1}")
                    sc.add("v", lambda e: e.tensor_copy(
                        out=SIMB.ap(),
                        in_=PSIM.ap().rearrange("p a b -> p (a b)")),
                        deps=dep, label=f"sime{tt}")
                    for dch in range(DT):
                        px = PXT0 if dch % 2 == 0 else PXT1
                        pdep = []
                        if dch >= 2:
                            pdep = [f"x2t{tt}_{dch - 2}"]
                        elif tt >= 1:
                            pdep = [f"x2t{tt - 1}_{6 + dch}"]
                        tr = sc.add("t", lambda e, px=px, sl=sl, dch=dch:
                            e.transpose(px.ap(), XL.ap()[:, sl, dch, :],
                                        IDB[:]),
                            deps=[lx] + pdep, label=f"x2tt{tt}_{dch}")
                        vdep = [tr]
                        if tt >= 2:
                            vdep.append(f"dota{tt - 2}_7")
                        sc.add("v", lambda e, px=px, sl=sl, dch=dch:
                            e.tensor_copy(
                                out=X2TT.ap()[:, sl, dch * 128:(dch + 1) * 128],
                                in_=px.ap()),
                            deps=vdep, label=f"x2t{tt}_{dch}")
                    topdep0 = ([f"gate{tt - 1}", f"idxu{tt - 1}"]
                               if tt >= 1 else [])
                    for h16 in range(16):
                        mx = sc.add("v", lambda e, h16=h16: e.max(
                            out=M8ALL.ap()[:, h16, :],
                            in_=SIMB.ap()[:, h16 * 128:(h16 + 1) * 128]),
                            deps=[f"sime{tt}"] +
                                 (topdep0 if h16 == 0 else []),
                            label=f"mxa{tt}_{h16}")
                        sc.add("v", lambda e, h16=h16: e.max_index(
                            out=MIALL.ap()[:, h16, :],
                            in_max=M8ALL.ap()[:, h16, :],
                            in_values=SIMB.ap()[:, h16 * 128:(h16 + 1) * 128]),
                            deps=[mx], label=f"mia{tt}_{h16}")
                    cast = sc.add("v", lambda e: e.tensor_copy(
                        out=MIFALL.ap(), in_=MIALL.ap()[:, :, 0]),
                        deps=[f"mia{tt}_{h_}" for h_ in range(16)],
                        label=f"mif{tt}")
                    fi = sc.add("v", lambda e: e.scalar_tensor_tensor(
                        out=IDXF8.ap(), in0=MIFALL.ap()[:, 0:8], scalar=128.0,
                        in1=MIFALL.ap()[:, 8:16], op0=ALU.mult, op1=ALU.add),
                        deps=[cast], label=f"idxf{tt}")
                    sc.add("v", lambda e, tt=tt: e.tensor_copy(
                        out=IDXS.ap()[:, tt, :], in_=IDXF8.ap()),
                        deps=[fi] + ([f"gain{tt - 2}_7"] if tt >= 2 else []),
                        label=f"idxu{tt}")
                    scca = sc.add("v", lambda e: e.tensor_add(
                        SCALL.ap(), M8ALL.ap()[:, 0:8, 0],
                        M8ALL.ap()[:, 8:16, 0]),
                        deps=[f"mxa{tt}_{h_}" for h_ in range(16)] + [fi],
                        label=f"scc{tt}")
                    sc.add("v", lambda e, tt=tt: e.tensor_scalar(
                        out=GATES.ap()[:, tt, :], in0=SCALL.ap(),
                        scalar1=0.0, scalar2=None, op0=ALU.max),
                        deps=[scca], label=f"gate{tt}")
                    for h in range(PKH):
                        ga = sc.add("g", lambda e, tt=tt, h=h:
                            e.indirect_dma_start(
                                out=AINH.ap()[:, h % 2, :], out_offset=None,
                                in_=pk_in_a,
                                in_offset=IndirectOffsetOnAxis(
                                    ap=IDXS.ap()[:, tt, h:h + 1], axis=0)),
                            deps=[f"idxu{tt}",
                                  (f"dota{tt}_{h - 2}" if h >= 2 else
                                   (f"dota{tt - 1}_{6 + h}" if tt >= 1
                                    else None))],
                            label=f"gain{tt}_{h}")
                        dt_ = sc.add("v", lambda e, h=h, sl=sl:
                            e.scalar_tensor_tensor(
                                out=TMPD.ap(), in0=AINH.ap()[:, h % 2, :],
                                scalar=1.0, in1=X2TT.ap()[:, sl, :],
                                op0=ALU.mult, op1=ALU.mult,
                                accum_out=DOTA.ap()),
                            deps=[ga, f"gate{tt}",
                                  (f"lin{tt}_{h - 1}" if h > 0 else
                                   (f"lin{tt - 1}_7" if tt >= 1 else None))]
                                 + [f"x2t{tt}_{dc}" for dc in range(DT)],
                            label=f"dota{tt}_{h}")
                        sc.add("v", lambda e, tt=tt, h=h: e.tensor_mul(
                            LINS.ap()[:, tt, h:h + 1], DOTA.ap(),
                            GATES.ap()[:, tt, h:h + 1]),
                            deps=[dt_],
                            label=f"lin{tt}_{h}")
                evG2 = sc.emit(blk, ext_deps=prev)
                _esG2.close()
                _esG12.close()
            prev = [e for v in evG2.values() for e in v]

            _esG3 = ExitStack()
            WINT = _esG3.enter_context(nc.sbuf_tensor("WINT", [128, DT, DIN0], BF16))
            WOTS = _esG3.enter_context(
                nc.sbuf_tensor("WOTS", [128, 2, DIN0 // 128, 128], BF16))
            HIDTM = _esG3.enter_context(nc.sbuf_tensor("HIDTM", [128, 4, DIN0], BF16))
            HIT = _esG3.enter_context(nc.sbuf_tensor("HIT", [128, DIN0 // 128, 512], BF16))
            OPK = _esG3.enter_context(nc.sbuf_tensor("OPK", [128, 4, D], BF16))
            BINH = _esG3.enter_context(nc.sbuf_tensor("BINH", [128, 2, DIN0], BF16))
            AOUTH = _esG3.enter_context(nc.sbuf_tensor("AOUTH", [128, 2, DIN0], BF16))
            BOUTH = _esG3.enter_context(nc.sbuf_tensor("BOUTH", [128, 2, D], BF16))
            XLG = _esG3.enter_context(nc.sbuf_tensor("XLG", [128, 2, DT, 128], BF16))
            TMP3 = _esG3.enter_context(nc.sbuf_tensor("TMP3", [128, DIN0], BF16))
            DOTO = _esG3.enter_context(nc.sbuf_tensor("DOTO", [128, 1], F32))
            LOUTH = _esG3.enter_context(nc.sbuf_tensor("LOUTH", [128, PKH], F32))
            HIDP = _esG3.enter_context(nc.psum_tensor("HIDP", [128, 4, 512], F32))
            POUT0 = _esG3.enter_context(nc.psum_tensor("POUT0", [128, 512], F32))
            POUT1 = _esG3.enter_context(nc.psum_tensor("POUT1", [128, 512], F32))
            PTR0 = _esG3.enter_context(nc.psum_tensor("PTR0", [128, 128], BF16))
            PTR1 = _esG3.enter_context(nc.psum_tensor("PTR1", [128, 128], BF16))
            if True:
                sc = Sched(ST)
                lwin = []
                for wh in range(2):
                    lwin.append(sc.add("g", lambda e, wh=wh: e.dma_start(
                        out=WINT.ap()[:, wh * 4:(wh + 1) * 4, :],
                        in_=pk_w_inT[:, wh * 4:(wh + 1) * 4, :]),
                        label=f"lwin{wh}"))
                NDI = DIN0 // 128
                trseq = [0]
                for qt in range(4):
                    tts = list(range(4 * qt, 4 * qt + 4))
                    for ti, tt in enumerate(tts):
                        lx3 = sc.add("q", lambda e, tt=tt: e.dma_start(
                            out=XLG.ap()[:, tt % 2, :, :]
                            .rearrange("p a b -> p (a b)"),
                            in_=xln2_dram[tt]),
                            deps=([f"wim{tt - 2}_{DT - 1}_3"] if tt >= 2
                                  else []),
                            label=f"xlg{tt}")
                        for pch in range(DT):
                            for nch in range(4):
                                dep = [lx3, lwin[pch // 4]]
                                if pch == 0 and tt >= 1:
                                    dep.append(f"gel{tt - 1}")
                                sc.add("t", lambda e, pch=pch, nch=nch, tt=tt:
                                    e.matmul(
                                        HIDP.ap()[:, nch, :],
                                        XLG.ap()[:, tt % 2, pch, :],
                                        WINT.ap()[:, pch,
                                                  nch * 512:(nch + 1) * 512],
                                        start=(pch == 0), stop=(pch == DT - 1)),
                                    deps=dep, label=f"wim{tt}_{pch}_{nch}")
                        for h in range(PKH):
                            gd = [None]
                            if h >= 2:
                                gd = [f"sct{tt}_{h - 2}"]
                            elif tt >= 1:
                                gd = [f"sct{tt - 1}_{6 + h}"]
                            sc.add("g", lambda e, tt=tt, h=h:
                                e.indirect_dma_start(
                                    out=BINH.ap()[:, h % 2, :],
                                    out_offset=None,
                                    in_=pk_in_b,
                                    in_offset=IndirectOffsetOnAxis(
                                        ap=IDXS.ap()[:, tt, h:h + 1],
                                        axis=0)),
                                deps=gd, label=f"gbin{tt}_{h}")
                        for h in range(PKH):
                            dep = [f"wim{tt}_{DT - 1}_3", f"gbin{tt}_{h}"]
                            if h > 0:
                                dep.append(f"sct{tt}_{h - 1}")
                            sc.add("v", lambda e, tt=tt, h=h:
                                e.scalar_tensor_tensor(
                                    out=HIDP.ap()
                                    .rearrange("p a b -> p (a b)"),
                                    in0=BINH.ap()[:, h % 2, :],
                                    scalar=LINS.ap()[:, tt, h:h + 1],
                                    in1=HIDP.ap()
                                    .rearrange("p a b -> p (a b)"),
                                    op0=ALU.mult, op1=ALU.add),
                                deps=dep, label=f"sct{tt}_{h}")
                        gdep = [f"sct{tt}_7"]
                        if qt > 0:
                            gdep.append(f"qjoin{qt - 1}")
                        sc.add("s", lambda e, ti=ti: e.activation(
                            out=HIDTM.ap()[:, ti, :],
                            in_=HIDP.ap().rearrange("p a b -> p (a b)"),
                            func=AF.Gelu),
                            deps=gdep, label=f"gel{tt}")
                        for h in range(PKH):
                            gd = [None]
                            if h >= 2:
                                gd = [f"doto{tt}_{h - 2}"]
                            elif ti >= 1:
                                gd = [f"doto{tts[ti - 1]}_{6 + h}"]
                            elif qt > 0:
                                gd = [f"doto{4 * qt - 1}_{6 + h}"]
                            sc.add("g", lambda e, tt=tt, h=h:
                                e.indirect_dma_start(
                                    out=AOUTH.ap()[:, h % 2, :],
                                    out_offset=None,
                                    in_=pk_out_a,
                                    in_offset=IndirectOffsetOnAxis(
                                        ap=IDXS.ap()[:, tt, h:h + 1],
                                        axis=0)),
                                deps=gd, label=f"gaout{tt}_{h}")
                            gd2 = [None]
                            if h >= 2:
                                gd2 = [f"opk{tt}_{h - 2}"]
                            elif ti >= 1:
                                gd2 = [f"opk{tts[ti - 1]}_{6 + h}"]
                            elif qt > 0:
                                gd2 = [f"opk{4 * qt - 1}_{6 + h}"]
                            sc.add("g", lambda e, tt=tt, h=h:
                                e.indirect_dma_start(
                                    out=BOUTH.ap()[:, h % 2, :],
                                    out_offset=None,
                                    in_=pk_out_b,
                                    in_offset=IndirectOffsetOnAxis(
                                        ap=IDXS.ap()[:, tt, h:h + 1], axis=0)),
                                deps=gd2, label=f"gbout{tt}_{h}")
                            ddep = [f"gaout{tt}_{h}", f"gel{tt}"]
                            if h > 0:
                                ddep.append(f"lout{tt}_{h - 1}")
                            elif ti >= 1:
                                ddep.append(f"lout{tts[ti - 1]}_7")
                            elif qt > 0:
                                ddep.append(f"lout{4 * qt - 1}_7")
                            dt2 = sc.add("v", lambda e, ti=ti, h=h:
                                e.scalar_tensor_tensor(
                                    out=TMP3.ap(),
                                    in0=AOUTH.ap()[:, h % 2, :], scalar=1.0,
                                    in1=HIDTM.ap()[:, ti, :],
                                    op0=ALU.mult, op1=ALU.mult,
                                    accum_out=DOTO.ap()),
                                deps=ddep, label=f"doto{tt}_{h}")
                            lo = sc.add("v", lambda e, tt=tt, h=h: e.tensor_mul(
                                LOUTH.ap()[:, h:h + 1], DOTO.ap(),
                                GATES.ap()[:, tt, h:h + 1]),
                                deps=[dt2], label=f"lout{tt}_{h}")
                            if h == 0:
                                odep = [lo, f"gbout{tt}_{h}"]
                                if qt > 0 and ti == 0:
                                    odep.append(f"qjoin{qt - 1}")
                                sc.add("v", lambda e, ti=ti, h=h:
                                    e.tensor_scalar(
                                        out=OPK.ap()[:, ti, :],
                                        in0=BOUTH.ap()[:, h % 2, :],
                                        scalar1=LOUTH.ap()[:, h:h + 1],
                                        scalar2=None, op0=ALU.mult),
                                    deps=odep, label=f"opk{tt}_{h}")
                            else:
                                sc.add("v", lambda e, ti=ti, h=h:
                                    e.scalar_tensor_tensor(
                                        out=OPK.ap()[:, ti, :],
                                        in0=BOUTH.ap()[:, h % 2, :],
                                        scalar=LOUTH.ap()[:, h:h + 1],
                                        in1=OPK.ap()[:, ti, :],
                                        op0=ALU.mult, op1=ALU.add),
                                    deps=[lo, f"gbout{tt}_{h}",
                                          f"opk{tt}_{h - 1}"],
                                    label=f"opk{tt}_{h}")
                    for ti in range(4):
                        for dch in range(NDI):
                            k = trseq[0]
                            trseq[0] += 1
                            pt = PTR0 if k % 2 == 0 else PTR1
                            pdep = []
                            if k >= 2:
                                pdep = [f"trev{k - 2}"]
                            tdep = [f"gel{tts[ti]}"] + pdep
                            if qt > 0:
                                tdep.append(f"womG{qt - 1}_7_{dch}")
                            tr = sc.add("t", lambda e, pt=pt, ti=ti, dch=dch:
                                e.transpose(
                                    pt.ap(),
                                    HIDTM.ap()[:, ti,
                                               dch * 128:(dch + 1) * 128],
                                    IDB[:]),
                                deps=tdep, label=f"htr{qt}_{ti}_{dch}")
                            sc.add("s", lambda e, pt=pt, ti=ti, dch=dch:
                                e.copy(
                                    out=HIT.ap()[:, dch,
                                                 ti * 128:(ti + 1) * 128],
                                    in_=pt.ap()),
                                deps=[tr], label=f"trev{k}")
                            sc.lbl[f"hitev{qt}_{ti}_{dch}"] = \
                                sc.lbl[f"trev{k}"]
                    for ti in range(4):
                        for dch in range(DT):
                            k = trseq[0]
                            trseq[0] += 1
                            pt = PTR0 if k % 2 == 0 else PTR1
                            pdep = [f"trev{k - 2}"] if k >= 2 else []
                            tr = sc.add("t", lambda e, pt=pt, ti=ti, dch=dch:
                                e.transpose(
                                    pt.ap(),
                                    OPK.ap()[:, ti, dch * 128:(dch + 1) * 128],
                                    IDB[:]),
                                deps=[f"opk{tts[ti]}_7"] + pdep,
                                label=f"otr{qt}_{ti}_{dch}")
                            sc.add("v", lambda e, pt=pt, qt=qt, ti=ti,
                                   dch=dch: e.tensor_add(
                                R.ap()[:, dch, qt * 512 + ti * 128:
                                       qt * 512 + (ti + 1) * 128],
                                pt.ap(),
                                R.ap()[:, dch, qt * 512 + ti * 128:
                                       qt * 512 + (ti + 1) * 128]),
                                deps=[tr], label=f"opka{qt}_{ti}_{dch}")
                            sc.lbl[f"trev{k}"] = sc.lbl[f"opka{qt}_{ti}_{dch}"]
                    for mch in range(DT):
                        wdep = []
                        if mch >= 2:
                            wdep = [f"womG{qt}_{mch - 2}_{NDI - 1}"]
                        elif qt > 0:
                            wdep = [f"womG{qt - 1}_{6 + mch}_{NDI - 1}"]
                        lwot = sc.add("g", lambda e, mch=mch: e.dma_start(
                            out=WOTS.ap()[:, mch % 2, :, :],
                            in_=pk_w_outT[mch]),
                            deps=wdep, label=f"lwot{qt}_{mch}")
                        for pch in range(NDI):
                            dep = [lwot] + [f"hitev{qt}_{ti_}_{pch}"
                                            for ti_ in range(4)]
                            if pch == 0 and (qt > 0 or mch >= 2):
                                dep.append(f"rupG{qt}_{mch - 2}" if mch >= 2
                                           else f"rupG{qt - 1}_{6 + mch}")
                            po = POUT0 if mch % 2 == 0 else POUT1
                            sc.add("t", lambda e, po=po, mch=mch, pch=pch:
                                e.matmul(
                                    po.ap(), WOTS.ap()[:, mch % 2, pch, :],
                                    HIT.ap()[:, pch, :],
                                    start=(pch == 0), stop=(pch == NDI - 1)),
                                deps=dep, label=f"womG{qt}_{mch}_{pch}")
                        po = POUT0 if mch % 2 == 0 else POUT1
                        sc.add("v", lambda e, po=po, qt=qt, mch=mch:
                            e.tensor_add(
                                R.ap()[:, mch, qt * 512:(qt + 1) * 512],
                                po.ap(),
                                R.ap()[:, mch, qt * 512:(qt + 1) * 512]),
                            deps=[f"womG{qt}_{mch}_{NDI - 1}"] +
                                 [f"opka{qt}_{ti_}_{mch}" for ti_ in range(4)],
                            label=f"rupG{qt}_{mch}")
                    sc.add("v", lambda e: e.tensor_copy(
                        out=JNK.ap()[0:1, 2:3], in_=JNK.ap()[0:1, 18:19]),
                        deps=[f"rupG{qt}_{m}" for m in range(DT)] +
                             [f"doto{tts[3]}_{h}" for h in range(PKH)] +
                             [f"htr{qt}_{ti_}_{NDI - 1}" for ti_ in range(4)] +
                             [f"otr{qt}_{ti_}_{DT - 1}" for ti_ in range(4)],
                        label=f"qjoin{qt}")
                if dbg == "R2":
                    dbg_dump(sc, lambda: R.ap().rearrange("p a b -> p (a b)"),
                             [f"qjoin{q_}" for q_ in range(4)])
                evG3 = sc.emit(blk, ext_deps=prev)
                _esG3.close()
                _esG23.close()
            prev = [e for v in evG3.values() for e in v]

        if "H" in phases:
            _esH0 = ExitStack()
            XLN3 = _esH0.enter_context(nc.sbuf_tensor("XLN3", [128, DT, T], BF16))
            SQH3 = _esH0.enter_context(nc.sbuf_tensor("SQH3", [128, DT, 512], BF16))
            ROWS3 = _esH0.enter_context(nc.sbuf_tensor("ROWS3", [1, 3, T], F32))
            LNA3 = _esH0.enter_context(nc.sbuf_tensor("LNA3", [128, 2, 512], F32))
            LNB3 = _esH0.enter_context(nc.sbuf_tensor("LNB3", [128, 2, 512], F32))
            BETA = _esH0.enter_context(nc.sbuf_tensor("BETA", [1, 16], F32))
            BETAE = _esH0.enter_context(nc.sbuf_tensor("BETAE", [1, 16], F32))
            PHS0 = _esH0.enter_context(nc.psum_tensor("PHS0", [1, 512], F32))
            PHS1 = _esH0.enter_context(nc.psum_tensor("PHS1", [1, 512], F32))
            PHQ0 = _esH0.enter_context(nc.psum_tensor("PHQ0", [1, 512], F32))
            PHQ1 = _esH0.enter_context(nc.psum_tensor("PHQ1", [1, 512], F32))
            PHB0 = _esH0.enter_context(nc.psum_tensor("PHB0", [128, 512], F32))
            PHB1 = _esH0.enter_context(nc.psum_tensor("PHB1", [128, 512], F32))
            if True:
                sc = Sched(ST)
                emit_ln(sc, R, XLN3, SQH3, ROWS3, LNA3, LNB3, PHS0, PHS1,
                        PHQ0, PHQ1, PHB0, PHB1, ONESCF, ONESCB, ONESRF, "ln2")
                for tt in range(TT):
                    sc.add("q", lambda e, tt=tt: e.dma_start(
                        out=xln2_dram[tt].rearrange("p (a b) -> p a b", a=DT),
                        in_=XLN3.ap()[:, :, tt * 128:(tt + 1) * 128]),
                        deps=[f"ln2_ap{(tt // 4) * DT + k}" for k in range(DT)],
                        label=f"x3s{tt}")
                lb = sc.add("q", lambda e: e.dma_start(out=BETA[:], in_=la_beta),
                            label="lbeta")
                eb = sc.add("s", lambda e: e.activation(
                    out=BETAE[:], in_=BETA[:], func=AF.Exp), deps=[lb])
                m8 = sc.add("v", lambda e: e.tensor_scalar(
                    out=BETAE[:], in0=BETAE[:], scalar1=8.0, scalar2=None,
                    op0=ALU.mult), deps=[eb])
                rc = sc.add("v", lambda e: e.reciprocal(
                    out=BETA[:], in_=BETAE[:]), deps=[m8, eb])
                sc.add("q", lambda e: e.dma_start(out=scb_dram, in_=BETA[:]),
                       deps=[rc], label="scbs")
                evH0 = sc.emit(blk, ext_deps=prev)
                _esH0.close()
            prev = [e for v in evH0.values() for e in v]

            _esH1 = ExitStack()
            XLH = _esH1.enter_context(nc.sbuf_tensor("XLH", [128, 2, DT, 128], BF16))
            WPRJ = _esH1.enter_context(nc.sbuf_tensor("WPRJ", [128, DT, D], BF16))
            QTM3 = _esH1.enter_context(nc.sbuf_tensor("QTM3", [128, 8, D], BF16))
            KTM3 = _esH1.enter_context(nc.sbuf_tensor("KTM3", [128, 8, D], BF16))
            VTM3 = _esH1.enter_context(nc.sbuf_tensor("VTM3", [128, 8, D], BF16))
            OTM3 = _esH1.enter_context(nc.sbuf_tensor("OTM3", [128, 8, D], BF16))
            OTT3 = _esH1.enter_context(nc.sbuf_tensor("OTT3", [128, DT, 1024], BF16))
            WOS3 = _esH1.enter_context(nc.sbuf_tensor("WOS3", [128, 2, DT, 128], BF16))
            SCB = _esH1.enter_context(nc.sbuf_tensor("SCB", [128, 16], F32))
            TMPH = _esH1.enter_context(nc.sbuf_tensor("TMPH", [128, D], BF16))
            SCS = _esH1.enter_context(nc.sbuf_tensor("SCS", [128, 2, 10, 16], F32))
            PXPB = _esH1.enter_context(nc.sbuf_tensor("PXPB", [128, 2, 10, 16], F32))
            MXH = _esH1.enter_context(nc.sbuf_tensor("MXH", [128, 16], F32))
            DENH = _esH1.enter_context(nc.sbuf_tensor("DENH", [128, 16], F32))
            OACCF = _esH1.enter_context(nc.sbuf_tensor("OACCF", [128, D], F32))
            PH0 = _esH1.enter_context(nc.psum_tensor("PH0", [128, 2, 512], F32))
            PH1 = _esH1.enter_context(nc.psum_tensor("PH1", [128, 2, 512], F32))
            PHT0 = _esH1.enter_context(nc.psum_tensor("PHT0", [128, 128], BF16))
            PHT1 = _esH1.enter_context(nc.psum_tensor("PHT1", [128, 128], BF16))
            PWO0 = _esH1.enter_context(nc.psum_tensor("PWO0", [128, 512], F32))
            PWO1 = _esH1.enter_context(nc.psum_tensor("PWO1", [128, 512], F32))
            pairs = [(s, sp) for s in range(4) for sp in range(s + 1)]
            from concourse.ap import AP as _APH
            for nh in range(2):
                sc = Sched(ST)
                lscb = sc.add("q", lambda e: e.dma_start(
                    out=SCB.ap(),
                    in_=_APH(scb_dram.tensor, 0, [[0, 128], [1, 16]])),
                    label="lscb")
                ph_hist = []
                for pj, bnm in enumerate(("bq", "bk", "bv")):
                    wsrc = (la_wqT, la_wkT, la_wvT)[pj]
                    dst = (QTM3, KTM3, VTM3)[pj]
                    lwp = []
                    for wh in range(2):
                        wdep = [f"qkm{pj - 1}_7_{DT - 1}"] if pj > 0 else []
                        lwp.append(sc.add("g", lambda e, wsrc=wsrc, wh=wh:
                            e.dma_start(
                                out=WPRJ.ap()[:, wh * 4:(wh + 1) * 4, :],
                                in_=wsrc.rearrange("(a p) d -> p a d", p=128)
                                [:, wh * 4:(wh + 1) * 4, :]),
                            deps=wdep, label=f"lwp{pj}_{wh}"))
                    for ti in range(8):
                        idx = pj * 8 + ti
                        s_, u_ = divmod(ti, 2)
                        gtile = s_ * 4 + nh * 2 + u_
                        lxh = sc.add("q", lambda e, ti=ti, gtile=gtile:
                            e.dma_start(
                                out=XLH.ap()[:, ti % 2, :, :]
                                .rearrange("p a b -> p (a b)"),
                                in_=xln2_dram[gtile]),
                            deps=([f"qkm{pj}_{ti - 2}_{DT - 1}"] if ti >= 2
                                  else ([f"qkm{pj - 1}_{6 + ti}_{DT - 1}"]
                                        if pj > 0 else [])),
                            label=f"xlh{pj}_{ti}")
                        ps = PH0 if idx % 2 == 0 else PH1
                        bdep = [lwp[0], lwp[1], lxh]
                        if len(ph_hist) >= 2:
                            bdep += ph_hist[-2]
                        for nch in range(2):
                            sc.add("t", lambda e, ps=ps, bnm=bnm, nch=nch:
                                e.matmul(
                                    ps.ap()[:, nch, :],
                                    ONESR.ap()[0:1, 0:128],
                                    brow(bnm, nch * 512, 512),
                                    start=True, stop=False),
                                deps=bdep, label=f"qkb{pj}_{ti}_{nch}")
                        for pch in range(DT):
                            for nch in range(2):
                                sc.add("t", lambda e, ps=ps, ti=ti, pch=pch,
                                       nch=nch: e.matmul(
                                    ps.ap()[:, nch, :],
                                    XLH.ap()[:, ti % 2, pch, :],
                                    WPRJ.ap()[:, pch,
                                              nch * 512:(nch + 1) * 512],
                                    start=False, stop=(pch == DT - 1)),
                                    deps=[f"qkb{pj}_{ti}_{nch}"] if pch == 0
                                    else [],
                                    label=(f"qkm{pj}_{ti}_{pch}" if nch == 1
                                           else None))
                        evs = [sc.add("v", lambda e, ps=ps, dst=dst,
                                       ti=ti: e.tensor_copy(
                            out=dst.ap()[:, ti, :],
                            in_=ps.ap().rearrange("p a b -> p (a b)")),
                            deps=[f"qkm{pj}_{ti}_{DT - 1}"],
                            label=f"qke{pj}_{ti}")]
                        ph_hist.append(evs)
                for u in range(2):
                    for pi, (s_, sp) in enumerate(pairs):
                        tiq = s_ * 2 + u
                        tik = sp * 2 + u
                        prevr = (f"scr{u}_{pi - 1}" if pi > 0 else
                                 (f"scr{u - 1}_9" if u > 0 else None))
                        ml = sc.add("v", lambda e, tiq=tiq, tik=tik:
                            e.tensor_mul(TMPH.ap(), QTM3.ap()[:, tiq, :],
                                         KTM3.ap()[:, tik, :]),
                            deps=[f"qke0_{tiq}", f"qke1_{tik}", prevr],
                            label=f"scm{u}_{pi}")
                        rr = sc.add("v", lambda e, u=u, pi=pi: e.tensor_reduce(
                            SCS.ap()[:, u, pi, :],
                            TMPH.ap().rearrange("p (h w) -> p h w", h=16),
                            AX.X, ALU.add), deps=[ml], label=f"scr0{u}_{pi}")
                        sc.add("v", lambda e, u=u, pi=pi: e.tensor_mul(
                            SCS.ap()[:, u, pi, :], SCS.ap()[:, u, pi, :],
                            SCB.ap()), deps=[rr, lscb],
                            label=f"scr{u}_{pi}")
                    for s_ in range(4):
                        base = s_ * (s_ + 1) // 2
                        mdep = [f"scr{u}_{base + sp}" for sp in range(s_ + 1)]
                        if s_ > 0 or u > 0:
                            mdep.append(f"sxe{u}_{base - 1}" if s_ > 0
                                        else f"sxe{u - 1}_9")
                        mx = sc.add("v", lambda e, u=u, base=base:
                            e.tensor_copy(out=MXH.ap(),
                                          in_=SCS.ap()[:, u, base, :]),
                            deps=mdep, label=f"sxm{u}_{s_}_0")
                        for sp in range(1, s_ + 1):
                            mx = sc.add("v", lambda e, u=u, base=base, sp=sp:
                                e.tensor_tensor(
                                    out=MXH.ap(), in0=MXH.ap(),
                                    in1=SCS.ap()[:, u, base + sp, :],
                                    op=ALU.max),
                                deps=[f"sxm{u}_{s_}_{sp - 1}"],
                                label=f"sxm{u}_{s_}_{sp}")
                        for sp in range(s_ + 1):
                            sb = sc.add("v", lambda e, u=u, base=base, sp=sp:
                                e.tensor_sub(
                                    PXPB.ap()[:, u, base + sp, :],
                                    SCS.ap()[:, u, base + sp, :], MXH.ap()),
                                deps=[mx, (f"sxp{u}_{s_ - 1}_{sp}"
                                           if s_ > 0 and sp < s_ else None),
                                      (f"sxp{u - 1}_3_{sp}"
                                       if s_ == 0 and u > 0 else None)],
                                label=f"sxs{u}_{s_}_{sp}")
                            sc.add("s", lambda e, u=u, base=base, sp=sp:
                                e.activation(
                                    out=PXPB.ap()[:, u, base + sp, :],
                                    in_=PXPB.ap()[:, u, base + sp, :],
                                    func=AF.Exp),
                                deps=[sb], label=f"sxe{u}_{base + sp}")
                        dn = sc.add("v", lambda e, u=u, base=base:
                            e.tensor_copy(out=DENH.ap(),
                                          in_=PXPB.ap()[:, u, base, :]),
                            deps=[f"sxe{u}_{base}",
                                  (f"sxp{u}_{s_ - 1}_{s_ - 1}" if s_ > 0 else
                                   (f"sxp{u - 1}_3_3" if u > 0 else None))],
                            label=f"sxd{u}_{s_}_0")
                        for sp in range(1, s_ + 1):
                            dn = sc.add("v", lambda e, u=u, base=base, sp=sp:
                                e.tensor_add(DENH.ap(), DENH.ap(),
                                             PXPB.ap()[:, u, base + sp, :]),
                                deps=[f"sxd{u}_{s_}_{sp - 1}",
                                      f"sxe{u}_{base + sp}"],
                                label=f"sxd{u}_{s_}_{sp}")
                        rc2 = sc.add("v", lambda e: e.reciprocal(
                            out=DENH.ap(), in_=DENH.ap()), deps=[dn],
                            label=f"sxr{u}_{s_}")
                        for sp in range(s_ + 1):
                            sc.add("v", lambda e, u=u, base=base, sp=sp:
                                e.tensor_mul(
                                    PXPB.ap()[:, u, base + sp, :],
                                    PXPB.ap()[:, u, base + sp, :], DENH.ap()),
                                deps=[rc2], label=f"sxp{u}_{s_}_{sp}")
                for u in range(2):
                    for s_ in range(4):
                        base = s_ * (s_ + 1) // 2
                        ti = s_ * 2 + u
                        for h in range(16):
                            for sp in range(s_ + 1):
                                tiv = sp * 2 + u
                                dep = [f"sxp{u}_{s_}_{sp}",
                                       f"qke2_{tiv}"]
                                if sp == 0:
                                    dep.append(f"otm{u}_{s_ - 1}" if s_ > 0
                                               else (f"otm{u - 1}_3"
                                                     if u > 0 else None))
                                    sc.add("v", lambda e, h=h, tiv=tiv, u=u,
                                           base=base: e.tensor_scalar(
                                        out=OACCF.ap()[:, h * 64:(h + 1) * 64],
                                        in0=VTM3.ap()[:, tiv,
                                                      h * 64:(h + 1) * 64],
                                        scalar1=PXPB.ap()[:, u, base, h:h + 1],
                                        scalar2=None, op0=ALU.mult),
                                        deps=dep, label=f"oas{u}_{s_}_{h}_0")
                                else:
                                    sc.add("v", lambda e, h=h, tiv=tiv, u=u,
                                           base=base, sp=sp:
                                        e.scalar_tensor_tensor(
                                            out=OACCF.ap()[:, h * 64:
                                                           (h + 1) * 64],
                                            in0=VTM3.ap()[:, tiv,
                                                          h * 64:(h + 1) * 64],
                                            scalar=PXPB.ap()[:, u, base + sp,
                                                             h:h + 1],
                                            in1=OACCF.ap()[:, h * 64:
                                                           (h + 1) * 64],
                                            op0=ALU.mult, op1=ALU.add),
                                        deps=dep + [f"oas{u}_{s_}_{h}_{sp-1}"],
                                        label=f"oas{u}_{s_}_{h}_{sp}")
                        sc.add("v", lambda e, ti=ti: e.tensor_copy(
                            out=OTM3.ap()[:, ti, :], in_=OACCF.ap()),
                            deps=[f"oas{u}_{s_}_{hh}_{s_}" for hh in range(16)],
                            label=f"otm{u}_{s_}")
                tcnt = [0]
                for ti in range(8):
                    for dch in range(DT):
                        k = tcnt[0]
                        tcnt[0] += 1
                        pt = PHT0 if k % 2 == 0 else PHT1
                        pdep = [f"otte{(k - 2) // DT}_{(k - 2) % DT}"] \
                            if k >= 2 else []
                        u, s_ = ti % 2, ti // 2
                        tr = sc.add("t", lambda e, pt=pt, ti=ti, dch=dch:
                            e.transpose(pt.ap(),
                                        OTM3.ap()[:, ti,
                                                  dch * 128:(dch + 1) * 128],
                                        IDB[:]),
                            deps=[f"otm{u}_{s_}"] + pdep,
                            label=f"ott{ti}_{dch}")
                        sc.add("s", lambda e, pt=pt, ti=ti, dch=dch: e.copy(
                            out=OTT3.ap()[:, dch, ti * 128:(ti + 1) * 128],
                            in_=pt.ap()), deps=[tr], label=f"otte{ti}_{dch}")
                for mch in range(DT):
                    wdep = [f"wo3m{mch - 2}_1_{DT - 1}"] if mch >= 2 else []
                    lwo3 = sc.add("g", lambda e, mch=mch: e.dma_start(
                        out=WOS3.ap()[:, mch % 2, :, :], in_=la_woT[mch]),
                        deps=wdep, label=f"lwo3{mch}")
                    for tch in range(2):
                        po = PWO0 if (mch * 2 + tch) % 2 == 0 else PWO1
                        bdep = [lwo3]
                        pidx = mch * 2 + tch
                        if pidx >= 2:
                            pm, ptc = divmod(pidx - 2, 2)
                            bdep += [f"rupH{pm}_{ptc}_0", f"rupH{pm}_{ptc}_1"]
                        sc.add("t", lambda e, po=po, mch=mch: e.matmul(
                            po.ap(), brow("bo", mch * 128, 128),
                            ONESR.ap()[0:1, :], start=True, stop=False),
                            deps=bdep, label=f"wo3b{mch}_{tch}")
                        for pch in range(DT):
                            sc.add("t", lambda e, po=po, mch=mch, tch=tch,
                                   pch=pch: e.matmul(
                                po.ap(), WOS3.ap()[:, mch % 2, pch, :],
                                OTT3.ap()[:, pch, tch * 512:(tch + 1) * 512],
                                start=False, stop=(pch == DT - 1)),
                                deps=([f"wo3b{mch}_{tch}"] if pch == 0
                                      else []) +
                                     [f"otte{ti_}_{pch}" for ti_ in
                                      (range(8) if pch == 0 or True else [])],
                                label=f"wo3m{mch}_{tch}_{pch}")
                        for hf in range(2):
                            sc.add("v", lambda e, po=po, mch=mch, tch=tch,
                                   hf=hf, nh=nh: e.tensor_add(
                                R.ap()[:, mch,
                                       (2 * tch + hf) * 512 + nh * 256:
                                       (2 * tch + hf) * 512 + nh * 256 + 256],
                                po.ap()[:, hf * 256:(hf + 1) * 256],
                                R.ap()[:, mch,
                                       (2 * tch + hf) * 512 + nh * 256:
                                       (2 * tch + hf) * 512 + nh * 256 + 256]),
                                deps=[f"wo3m{mch}_{tch}_{DT - 1}"],
                                label=f"rupH{mch}_{tch}_{hf}")
                evH1 = sc.emit(blk, ext_deps=prev)
                prev = [e for v in evH1.values() for e in v]
            _esH1.close()
            if dbg == "R3":
                pass

        if "I" in phases:
            _esI = ExitStack()
            QIT = _esI.enter_context(nc.sbuf_tensor("QIT", [128, MH, T], BF16))
            _esI0 = ExitStack()
            XLN4 = _esI0.enter_context(nc.sbuf_tensor("XLN4", [128, DT, T], BF16))
            _esI0a = ExitStack()
            SQH4 = _esI0a.enter_context(nc.sbuf_tensor("SQH4", [128, DT, 512], BF16))
            ROWS4 = _esI0a.enter_context(nc.sbuf_tensor("ROWS4", [1, 3, T], F32))
            LNA4 = _esI0a.enter_context(nc.sbuf_tensor("LNA4", [128, 2, 512], F32))
            LNB4 = _esI0a.enter_context(nc.sbuf_tensor("LNB4", [128, 2, 512], F32))
            PIS0 = _esI0a.enter_context(nc.psum_tensor("PIS0", [1, 512], F32))
            PIS1 = _esI0a.enter_context(nc.psum_tensor("PIS1", [1, 512], F32))
            PIQ0 = _esI0a.enter_context(nc.psum_tensor("PIQ0", [1, 512], F32))
            PIQ1 = _esI0a.enter_context(nc.psum_tensor("PIQ1", [1, 512], F32))
            PIB0 = _esI0a.enter_context(nc.psum_tensor("PIB0", [128, 512], F32))
            PIB1 = _esI0a.enter_context(nc.psum_tensor("PIB1", [128, 512], F32))
            if True:
                sc = Sched(ST)
                emit_ln(sc, R, XLN4, SQH4, ROWS4, LNA4, LNB4, PIS0, PIS1,
                        PIQ0, PIQ1, PIB0, PIB1, ONESCF, ONESCB, ONESRF, "ln3")
                evI0a = sc.emit(blk, ext_deps=prev)
                _esI0a.close()
            prev = [e for v in evI0a.values() for e in v]

            _esI0b = ExitStack()
            WMH = _esI0b.enter_context(nc.sbuf_tensor("WMH", [128, DT, D], BF16))
            WRT = _esI0b.enter_context(nc.sbuf_tensor("WRT", [128, DT, MNE], BF16))
            RL = _esI0b.enter_context(nc.sbuf_tensor("RL", [128, MNE], F32))
            M8R = _esI0b.enter_context(nc.sbuf_tensor("M8R", [128, 8], F32))
            SGG = _esI0b.enter_context(nc.sbuf_tensor("SGG", [128, 2], F32))
            EQM = _esI0b.enter_context(nc.sbuf_tensor("EQM", [128, MNE], F32))
            W16 = _esI0b.enter_context(nc.sbuf_tensor("W16", [128, MNE], F32))
            WTSB = _esI0b.enter_context(nc.sbuf_tensor("WTSB", [16, TT, 128], F32))
            PQI = _esI0b.enter_context(nc.psum_tensor("PQI", [128, 4, 512], F32))
            PRT = _esI0b.enter_context(nc.psum_tensor("PRT", [128, MNE], F32))
            PRT2 = _esI0b.enter_context(nc.psum_tensor("PRT2", [128, 128], F32))
            if True:
                sc = Sched(ST)
                lwmh = []
                for wh in range(2):
                    lwmh.append(sc.add("g", lambda e, wh=wh: e.dma_start(
                        out=WMH.ap()[:, wh * 4:(wh + 1) * 4, :],
                        in_=mo_wmhT[:, wh * 4:(wh + 1) * 4, :]),
                        label=f"lwmh{wh}"))
                lwrt = sc.add("g", lambda e: e.dma_start(
                    out=WRT.ap(),
                    in_=mo_wrT.rearrange("(a p) e -> p a e", p=128)),
                    label="lwrt")
                for hdch in range(MH):
                    for pch in range(DT):
                        for tch in range(4):
                            dep = [lwmh[pch // 4]]
                            if pch == 0 and hdch > 0:
                                dep.append(f"qie{hdch - 1}")
                            sc.add("t", lambda e, hdch=hdch, pch=pch, tch=tch:
                                e.matmul(
                                    PQI.ap()[:, tch, :],
                                    WMH.ap()[:, pch,
                                             hdch * 128:(hdch + 1) * 128],
                                    XLN4.ap()[:, pch,
                                              tch * 512:(tch + 1) * 512],
                                    start=(pch == 0), stop=(pch == DT - 1)),
                                deps=dep, label=f"qim{hdch}_{pch}_{tch}")
                    sc.add("v", lambda e, hdch=hdch: e.tensor_copy(
                        out=QIT.ap()[:, hdch, :],
                        in_=PQI.ap().rearrange("p a b -> p (a b)")),
                        deps=[f"qim{hdch}_{DT - 1}_{tc}" for tc in range(4)],
                        label=f"qie{hdch}")
                for tt in range(TT):
                    for pch in range(DT):
                        dep = [lwrt]
                        if pch == 0 and tt > 0:
                            dep.append(f"rte{tt - 1}")
                        sc.add("t", lambda e, tt=tt, pch=pch: e.matmul(
                            PRT.ap(), XLN4.ap()[:, pch, tt * 128:(tt + 1) * 128],
                            WRT.ap()[:, pch, :],
                            start=(pch == 0), stop=(pch == DT - 1)),
                            deps=dep, label=f"rtm{tt}_{pch}")
                    rte = sc.add("v", lambda e: e.tensor_copy(
                        out=RL.ap(), in_=PRT.ap()),
                        deps=[f"rtm{tt}_{DT - 1}",
                              (f"rw1_{tt - 1}" if tt > 0 else None)],
                        label=f"rte{tt}")
                    rm8 = sc.add("v", lambda e: e.max(
                        out=M8R.ap(), in_=RL.ap()),
                        deps=[rte, (f"rsg{tt - 1}" if tt > 0 else None),
                              (f"req1_{tt - 1}" if tt > 0 else None)],
                        label=f"rm8{tt}")
                    rsg = sc.add("s", lambda e: e.activation(
                        out=SGG.ap(), in_=M8R.ap()[:, 0:2], func=AF.Sigmoid),
                        deps=[rm8, (f"rw1_{tt - 1}" if tt > 0 else None)],
                        label=f"rsg{tt}")
                    rq0 = sc.add("v", lambda e: e.tensor_scalar(
                        out=EQM.ap(), in0=RL.ap(), scalar1=M8R.ap()[:, 0:1],
                        scalar2=None, op0=ALU.is_equal),
                        deps=[rm8, (f"rw0_{tt - 1}" if tt > 0 else None)],
                        label=f"req0_{tt}")
                    rw0 = sc.add("v", lambda e: e.tensor_scalar(
                        out=W16.ap(), in0=EQM.ap(), scalar1=SGG.ap()[:, 0:1],
                        scalar2=None, op0=ALU.mult),
                        deps=[rq0, rsg, (f"rtt{tt - 1}" if tt > 0 else None)],
                        label=f"rw0_{tt}")
                    rq1 = sc.add("v", lambda e: e.tensor_scalar(
                        out=EQM.ap(), in0=RL.ap(), scalar1=M8R.ap()[:, 1:2],
                        scalar2=None, op0=ALU.is_equal),
                        deps=[rw0], label=f"req1_{tt}")
                    rw1 = sc.add("v", lambda e: e.scalar_tensor_tensor(
                        out=W16.ap(), in0=EQM.ap(), scalar=SGG.ap()[:, 1:2],
                        in1=W16.ap(), op0=ALU.mult, op1=ALU.add),
                        deps=[rq1], label=f"rw1_{tt}")
                    rtt = sc.add("t", lambda e: e.transpose(
                        PRT2.ap()[0:16, :], W16.ap(), IDF[:]),
                        deps=[rw1, (f"rwe{tt - 1}" if tt > 0 else None)],
                        label=f"rtt{tt}")
                    sc.add("v", lambda e, tt=tt: e.tensor_copy(
                        out=WTSB.ap()[:, tt, :], in_=PRT2.ap()[0:16, :]),
                        deps=[rtt], label=f"rwe{tt}")
                sc.add("q", lambda e: e.dma_start(
                    out=wt_dram,
                    in_=WTSB.ap().rearrange("p a b -> p (a b)")),
                    deps=[f"rwe{tt_}" for tt_ in range(TT)], label="wts")
                evI0b = sc.emit(blk, ext_deps=prev)
                _esI0b.close()
                _esI0.close()
            prev = [e for v in evI0b.values() for e in v]

            _esIY = ExitStack()
            YT = _esIY.enter_context(nc.sbuf_tensor("YT", [128, MH, T], BF16))
            _esI1 = ExitStack()
            KFS = _esI1.enter_context(nc.sbuf_tensor("KFS", [128, 1, MES], BF16))
            VFS = _esI1.enter_context(nc.sbuf_tensor("VFS", [128, 2, 32, 128], BF16))
            ACTT = _esI1.enter_context(nc.sbuf_tensor("ACTT", [128, 2, T], BF16))
            ACTW = _esI1.enter_context(nc.sbuf_tensor("ACTW", [128, 2, T], BF16))
            WBV = _esI1.enter_context(nc.sbuf_tensor("WBV", [128, 2, T], BF16))
            PSC = _esI1.enter_context(nc.psum_tensor("PSC", [128, 4, 512], F32))
            PYT = _esI1.enter_context(nc.psum_tensor("PYT", [128, 4, 512], F32))
            if True:
                sc = Sched(ST)
                from concourse.ap import AP as _API
                for h in range(MH):
                    lkf = sc.add("g", lambda e, h=h: e.dma_start(
                        out=KFS.ap()[:, 0, :], in_=mo_kfT[h]),
                        deps=([f"scm{h - 1}_31_3"] if h >= 1 else []),
                        label=f"lkf{h}")
                    lvf = sc.add("g", lambda e, h=h: e.dma_start(
                        out=VFS.ap()[:, h % 2, :, :], in_=mo_vfT[h]),
                        deps=([f"ymm{h - 2}_31_3"] if h >= 2 else []),
                        label=f"lvf{h}")
                    for esch in range(32):
                        ee = esch // 2
                        if esch % 2 == 0:
                            wdep = [f"scl{h}_{esch - 3}"] if esch >= 3 else \
                                ([f"scl{h - 1}_31"] if h > 0 else [])
                            sc.add("g", lambda e, ee=ee: e.dma_start(
                                out=WBV.ap()[:, ee % 2, :],
                                in_=_API(wt_dram.tensor, ee * T,
                                         [[0, 128], [1, T]])),
                                deps=wdep, label=f"lwb{h}_{ee}")
                        for tch in range(4):
                            dep = [lkf]
                            if esch >= 1:
                                dep.append(f"gle{h}_{esch - 1}")
                            elif h > 0:
                                dep.append(f"gle{h - 1}_31")
                            sc.add("t", lambda e, h=h, esch=esch, tch=tch:
                                e.matmul(
                                    PSC.ap()[:, tch, :],
                                    KFS.ap()[:, 0,
                                             esch * 128:(esch + 1) * 128],
                                    QIT.ap()[:, h, tch * 512:(tch + 1) * 512],
                                    start=True, stop=True),
                                deps=dep, label=f"scm{h}_{esch}_{tch}")
                        sc.add("s", lambda e, esch=esch:
                            e.activation(
                                out=ACTT.ap()[:, esch % 2, :],
                                in_=PSC.ap().rearrange("p a b -> p (a b)"),
                                func=AF.Gelu),
                            deps=[f"scm{h}_{esch}_{tc}" for tc in range(4)] +
                                 [(f"scl{h}_{esch - 2}" if esch >= 2 else
                                   (f"scl{h - 1}_{30 + esch}" if h > 0
                                    else None))],
                            label=f"gle{h}_{esch}")
                        sc.add("v", lambda e, esch=esch, ee=ee: e.tensor_mul(
                            ACTW.ap()[:, esch % 2, :],
                            ACTT.ap()[:, esch % 2, :],
                            WBV.ap()[:, ee % 2, :]),
                            deps=[f"gle{h}_{esch}", f"lwb{h}_{ee}"] +
                                 ([f"ymm{h}_{esch - 2}_3"] if esch >= 2 else
                                  ([f"ymm{h - 1}_{30 + esch}_3"] if h > 0
                                   else [])),
                            label=f"scl{h}_{esch}")
                        for tch in range(4):
                            dep = [f"scl{h}_{esch}", lvf]
                            if esch == 0:
                                dep.append(f"yev{h - 1}" if h > 0 else None)
                            sc.add("t", lambda e, h=h, esch=esch, tch=tch:
                                e.matmul(
                                    PYT.ap()[:, tch, :],
                                    VFS.ap()[:, h % 2, esch, :],
                                    ACTW.ap()[:, esch % 2,
                                              tch * 512:(tch + 1) * 512],
                                    start=(esch == 0), stop=(esch == 31)),
                                deps=dep, label=f"ymm{h}_{esch}_{tch}")
                    sc.add("v", lambda e, h=h: e.tensor_copy(
                        out=YT.ap()[:, h, :],
                        in_=PYT.ap().rearrange("p a b -> p (a b)")),
                        deps=[f"ymm{h}_31_{tc}" for tc in range(4)],
                        label=f"yev{h}")
                evI1 = sc.emit(blk, ext_deps=prev)
                _esI1.close()
            prev = [e for v in evI1.values() for e in v]

            _esI2 = ExitStack()
            WMG = _esI2.enter_context(nc.sbuf_tensor("WMG", [128, 2, DT, 128], BF16))
            PMG0 = _esI2.enter_context(nc.psum_tensor("PMG0", [128, 512], F32))
            PMG1 = _esI2.enter_context(nc.psum_tensor("PMG1", [128, 512], F32))
            if True:
                sc = Sched(ST)
                for mch in range(DT):
                    wdep = [f"mgm{mch - 2}_3_{DT - 1}"] if mch >= 2 else []
                    lwmg = sc.add("g", lambda e, mch=mch: e.dma_start(
                        out=WMG.ap()[:, mch % 2, :, :], in_=mo_wmgT[mch]),
                        deps=wdep, label=f"lwmg{mch}")
                    for tch in range(4):
                        po = PMG0 if (mch * 4 + tch) % 2 == 0 else PMG1
                        pidx = mch * 4 + tch
                        for pch in range(DT):
                            dep = [lwmg]
                            if pch == 0 and pidx >= 2:
                                pm, ptc = divmod(pidx - 2, 4)
                                dep.append(f"rupI{pm}_{ptc}")
                            sc.add("t", lambda e, po=po, mch=mch, pch=pch,
                                   tch=tch: e.matmul(
                                po.ap(), WMG.ap()[:, mch % 2, pch, :],
                                YT.ap()[:, pch, tch * 512:(tch + 1) * 512],
                                start=(pch == 0), stop=(pch == DT - 1)),
                                deps=dep, label=f"mgm{mch}_{tch}_{pch}")
                        sc.add("v", lambda e, po=po, mch=mch, tch=tch:
                            e.tensor_add(
                                R.ap()[:, mch, tch * 512:(tch + 1) * 512],
                                po.ap(),
                                R.ap()[:, mch, tch * 512:(tch + 1) * 512]),
                            deps=[f"mgm{mch}_{tch}_{DT - 1}"],
                            label=f"rupI{mch}_{tch}")
                evI2 = sc.emit(blk, ext_deps=prev)
                _esI2.close()
                _esIY.close()
                _esI.close()
            prev = [e for v in evI2.values() for e in v]

        _es8 = ExitStack()
        OTM = _es8.enter_context(nc.sbuf_tensor("OTM", [128, 2, D], F32))
        PZ0 = _es8.enter_context(nc.psum_tensor("PZ0", [128, 128], F32))
        PZ1 = _es8.enter_context(nc.psum_tensor("PZ1", [128, 128], F32))
        if True:
            sc = Sched(ST)
            for tt in range(TT):
                evs = []
                for k in range(DT):
                    i = tt * DT + k
                    ps = PZ0 if i % 2 == 0 else PZ1
                    tr = sc.add("t", lambda e, ps=ps, k=k, tt=tt: e.transpose(
                        ps.ap(), R.ap()[:, k, tt * 128:(tt + 1) * 128],
                        IDF[:]),
                        deps=[(f"ov{i - 2}" if i >= 2 else None)],
                        label=f"ot{i}")
                    evs.append(sc.add("v", lambda e, ps=ps, k=k, tt=tt:
                        e.tensor_copy(
                            out=OTM.ap()[:, tt % 2, k * 128:(k + 1) * 128],
                            in_=ps.ap()),
                        deps=[tr] + ([f"odma{tt - 2}"]
                                     if (k == 0 and tt >= 2) else []),
                        label=f"ov{i}"))
                sc.add("q", lambda e, tt=tt: e.dma_start(
                    out=out_t[tt * 128:(tt + 1) * 128, :],
                    in_=OTM.ap()[:, tt % 2, :]), deps=evs +
                    ([f"odma{tt - 2}"] if tt >= 2 else []),
                    label=f"odma{tt}")
            sc.add("v", lambda e: e.tensor_copy(
                out=JNK.ap()[0:1, 0:1], in_=JNK.ap()[0:1, 16:17]),
                deps=[f"odma{tt_}" for tt_ in range(TT)],
                label="final_join")
            evZ = sc.emit(blk, ext_deps=prev)
            _es8.close()

    _es1.close()
    return nc


def bass_diag_dst(t_dram):
    from concourse.ap import AP as _AP
    return _AP(t_dram.tensor, 0,
               [[(TPAD + 1) * CIN, Q], [CIN, Q], [1, CIN]])


def emit_ln(sc, R, XLN, SQH, ROWS, LNA, LNB, PSS0, PSS1, PSQ0, PSQ1,
            PSB0, PSB1, ONESCF, ONESCB, ONESRF, tag):
    for tc in range(4):
        pss = PSS0 if tc % 2 == 0 else PSS1
        psq = PSQ0 if tc % 2 == 0 else PSQ1
        for k in range(DT):
            dep = [f"{tag}_mq{tc - 1}_{DT - 1}"] if tc > 0 else []
            sc.add("v", lambda e, tc=tc, k=k: e.tensor_mul(
                SQH.ap()[:, k, :],
                R.ap()[:, k, tc * 512:(tc + 1) * 512],
                R.ap()[:, k, tc * 512:(tc + 1) * 512]),
                deps=dep, label=f"{tag}_sq{tc}_{k}")
        for k in range(DT):
            dep = [f"{tag}_rows{tc - 2}"] if tc >= 2 else []
            sc.add("t", lambda e, tc=tc, k=k, pss=pss: e.matmul(
                pss.ap(), ONESCF.ap(),
                R.ap()[:, k, tc * 512:(tc + 1) * 512],
                start=(k == 0), stop=(k == DT - 1)),
                deps=dep, label=f"{tag}_ms{tc}_{k}")
        for k in range(DT):
            dep = [f"{tag}_sq{tc}_{k}"]
            if tc >= 2:
                dep.append(f"{tag}_rowq{tc - 2}")
            sc.add("t", lambda e, tc=tc, k=k, psq=psq: e.matmul(
                psq.ap(), ONESCB.ap(), SQH.ap()[:, k, :],
                start=(k == 0), stop=(k == DT - 1)),
                deps=dep, label=f"{tag}_mq{tc}_{k}")
        sc.add("v", lambda e, tc=tc, pss=pss: e.tensor_copy(
            out=ROWS.ap()[0:1, 0, tc * 512:(tc + 1) * 512], in_=pss.ap()),
            deps=[f"{tag}_ms{tc}_{DT - 1}"], label=f"{tag}_rows{tc}")
        sc.add("v", lambda e, tc=tc, psq=psq: e.tensor_copy(
            out=ROWS.ap()[0:1, 1, tc * 512:(tc + 1) * 512], in_=psq.ap()),
            deps=[f"{tag}_mq{tc}_{DT - 1}"], label=f"{tag}_rowq{tc}")
    rowdeps = [f"{tag}_rows{tc}" for tc in range(4)] + \
              [f"{tag}_rowq{tc}" for tc in range(4)]
    l1 = sc.add("v", lambda e: e.tensor_scalar(
        out=ROWS.ap()[0:1, 0, :], in0=ROWS.ap()[0:1, 0, :],
        scalar1=1.0 / D, scalar2=None, op0=ALU.mult), deps=rowdeps,
        label=f"{tag}_mean")
    l2 = sc.add("v", lambda e: e.tensor_scalar(
        out=ROWS.ap()[0:1, 1, :], in0=ROWS.ap()[0:1, 1, :],
        scalar1=1.0 / D, scalar2=None, op0=ALU.mult), deps=rowdeps,
        label=f"{tag}_msq")
    l3 = sc.add("s", lambda e: e.activation(
        out=ROWS.ap()[0:1, 2, :], in_=ROWS.ap()[0:1, 0, :], func=AF.Square),
        deps=[l1], label=f"{tag}_m2")
    l4 = sc.add("v", lambda e: e.tensor_sub(
        ROWS.ap()[0:1, 1, :], ROWS.ap()[0:1, 1, :], ROWS.ap()[0:1, 2, :]),
        deps=[l2, l3], label=f"{tag}_var")
    l4b = sc.add("v", lambda e: e.tensor_scalar(
        out=ROWS.ap()[0:1, 1, :], in0=ROWS.ap()[0:1, 1, :],
        scalar1=LN_EPS, scalar2=None, op0=ALU.add), deps=[l4],
        label=f"{tag}_vare")
    l5 = sc.add("s", lambda e: e.activation(
        out=ROWS.ap()[0:1, 1, :], in_=ROWS.ap()[0:1, 1, :], func=AF.Sqrt),
        deps=[l4b], label=f"{tag}_sd")
    l6 = sc.add("v", lambda e: e.reciprocal(
        out=ROWS.ap()[0:1, 2, :], in_=ROWS.ap()[0:1, 1, :]),
        deps=[l5, l3], label=f"{tag}_rstd")
    l7 = sc.add("v", lambda e: e.scalar_tensor_tensor(
        out=ROWS.ap()[0:1, 0, :], in0=ROWS.ap()[0:1, 0, :], scalar=-1.0,
        in1=ROWS.ap()[0:1, 2, :], op0=ALU.mult, op1=ALU.mult),
        deps=[l6], label=f"{tag}_nmr")
    for tc in range(4):
        pa = PSB0 if tc % 2 == 0 else PSB1
        dep_a = [l6] + ([f"{tag}_apm{(tc - 2) * DT + DT - 1}"]
                        if tc >= 2 else [])
        ba = sc.add("t", lambda e, tc=tc, pa=pa: e.matmul(
            pa.ap(), ONESRF.ap()[0:1, 0:128],
            ROWS.ap()[0:1, 2, tc * 512:(tc + 1) * 512],
            start=True, stop=True), deps=dep_a, label=f"{tag}_ba{tc}")
        av = sc.add("v", lambda e, tc=tc, pa=pa: e.tensor_copy(
            out=LNA.ap()[:, tc % 2, :], in_=pa.ap()),
            deps=[ba], label=f"{tag}_bav{tc}")
        bb = sc.add("t", lambda e, tc=tc, pa=pa: e.matmul(
            pa.ap(), ONESRF.ap()[0:1, 0:128],
            ROWS.ap()[0:1, 0, tc * 512:(tc + 1) * 512],
            start=True, stop=True), deps=[l7, av], label=f"{tag}_bb{tc}")
        bv_ = sc.add("v", lambda e, tc=tc, pa=pa: e.tensor_copy(
            out=LNB.ap()[:, tc % 2, :], in_=pa.ap()),
            deps=[bb], label=f"{tag}_bbv{tc}")
        for k in range(DT):
            i = tc * DT + k
            sc.add("v", lambda e, tc=tc, k=k: e.tensor_mul(
                XLN.ap()[:, k, tc * 512:(tc + 1) * 512],
                R.ap()[:, k, tc * 512:(tc + 1) * 512],
                LNA.ap()[:, tc % 2, :]),
                deps=[av, f"{tag}_mq3_{DT - 1}"], label=f"{tag}_apm{i}")
            sc.add("v", lambda e, tc=tc, k=k: e.tensor_add(
                XLN.ap()[:, k, tc * 512:(tc + 1) * 512],
                XLN.ap()[:, k, tc * 512:(tc + 1) * 512],
                LNB.ap()[:, tc % 2, :]),
                deps=[f"{tag}_apm{i}", bv_], label=f"{tag}_ap{i}")
    return [f"{tag}_ap{i}" for i in range(4 * DT)]


_PROG_CACHE = {}


def _host_prep(inputs):
    f32 = np.float32
    ai = {k: np.asarray(v) for k, v in inputs.items()}
    x = np.ascontiguousarray(ai["L0_attn_x"], dtype=f32)
    common = {}
    common["w_in"] = np.ascontiguousarray(ai["L0_attn_w_in"], f32)
    common["b_in"] = np.ascontiguousarray(ai["L0_attn_b_in"], f32).reshape(1, CIN)
    common["w_gate"] = np.ascontiguousarray(ai["L0_attn_w_gate"], f32)
    common["b_gate"] = np.ascontiguousarray(ai["L0_attn_b_gate"], f32).reshape(1, CIN)
    common["w_out"] = np.ascontiguousarray(ai["L0_attn_w_out"], f32)
    common["b_out"] = np.ascontiguousarray(ai["L0_attn_b_out"], f32).reshape(1, D)
    common["pk_keysT"] = np.ascontiguousarray(
        np.asarray(ai["L0_ffn_keys"], f32).transpose(2, 0, 3, 1))
    common["pk_wq"] = np.ascontiguousarray(ai["L0_ffn_Wq"], f32)
    common["pk_w_inT"] = np.ascontiguousarray(
        np.asarray(ai["L0_ffn_W_in"], f32).T
        .reshape(DT, 128, DIN0).transpose(1, 0, 2))
    common["pk_w_outT"] = np.ascontiguousarray(
        np.asarray(ai["L0_ffn_W_out"], f32).T
        .reshape(DIN0 // 128, 128, DT, 128).transpose(2, 1, 0, 3))
    common["pk_in_a"] = np.ascontiguousarray(ai["L0_ffn_in_a"], f32)
    common["pk_in_b"] = np.ascontiguousarray(ai["L0_ffn_in_b"], f32)
    common["pk_out_a"] = np.ascontiguousarray(ai["L0_ffn_out_a"], f32)
    common["pk_out_b"] = np.ascontiguousarray(ai["L0_ffn_out_b"], f32)
    common["la_wqT"] = np.ascontiguousarray(np.asarray(ai["L1_attn_Wq"], f32).T)
    common["la_wkT"] = np.ascontiguousarray(np.asarray(ai["L1_attn_Wk"], f32).T)
    common["la_wvT"] = np.ascontiguousarray(np.asarray(ai["L1_attn_Wv"], f32).T)
    common["la_woT"] = np.ascontiguousarray(
        np.asarray(ai["L1_attn_Wo"], f32).T
        .reshape(DT, 128, DT, 128).transpose(2, 1, 0, 3))
    for bn in ("bq", "bk", "bv", "bo"):
        common["la_" + bn] = np.ascontiguousarray(
            ai["L1_attn_" + bn], f32).reshape(1, D)
    common["la_beta"] = np.ascontiguousarray(
        ai["L1_attn_beta"], f32).reshape(1, 16)
    common["mo_wmhT"] = np.ascontiguousarray(
        np.asarray(ai["L1_ffn_w_mh"], f32).T
        .reshape(DT, 128, D).transpose(1, 0, 2))
    common["mo_wrT"] = np.ascontiguousarray(
        np.asarray(ai["L1_ffn_w_router"], f32).T)
    common["mo_kfT"] = np.ascontiguousarray(
        np.asarray(ai["L1_ffn_k_ffwd"], f32)
        .reshape(MH, MES, MD).transpose(0, 2, 1))
    common["mo_vfT"] = np.ascontiguousarray(
        np.asarray(ai["L1_ffn_v_ffwd"], f32)
        .reshape(MH, MNE, 2, 128, 128).transpose(0, 3, 1, 2, 4)
        .reshape(MH, 128, 32, 128))
    common["mo_wmgT"] = np.ascontiguousarray(
        np.asarray(ai["L1_ffn_w_mg"], f32).T
        .reshape(DT, 128, DT, 128).transpose(2, 1, 0, 3))
    common["c_identf"] = np.eye(128, dtype=f32)
    common["c_onesf_r"] = np.ones((1, 512), f32)
    common["c_onesf_c"] = np.ones((128, 1), f32)
    bdo = np.zeros((128, 8), f32)
    for g in range(8):
        bdo[g * 16:(g + 1) * 16, g] = 1.0
    common["c_bdones"] = bdo
    A = np.ascontiguousarray(ai["L0_attn_A"], f32)
    Bs = np.ascontiguousarray(ai["L0_attn_B_ssm"], f32)
    Cs = np.ascontiguousarray(ai["L0_attn_C_ssm"], f32)
    in_maps = []
    for core in range(NCORE):
        m = dict(common)
        m["x_slice"] = np.ascontiguousarray(
            x[:, core * NLOC:(core + 1) * NLOC, :].reshape(T, D))
        m["core_id"] = np.array([[core]], dtype=np.uint32)
        cs0 = core * CSH
        m["A_shard"] = np.ascontiguousarray(A[cs0:cs0 + CSH])
        m["B_shard"] = np.ascontiguousarray(Bs[cs0:cs0 + CSH])
        m["C_shard"] = np.ascontiguousarray(Cs[cs0:cs0 + CSH])
        in_maps.append(m)
    return in_maps


def _run(dbg=None):
    key = dbg
    if key not in _PROG_CACHE:
        nc = build_program(dbg=dbg)
        mybir.codegen_inst_isa_subclasses(nc)
        _PROG_CACHE[key] = nc
    return _PROG_CACHE[key]


def kernel(**inputs):
    in_maps = _host_prep(inputs)
    nc = _run(None)
    res = run_bass_kernel_spmd(nc, in_maps, core_ids=list(range(NCORE)))
    outs = [res.results[c]["out"] for c in range(NCORE)]
    full = np.zeros((B, SEQ, D), np.float32)
    for c in range(NCORE):
        full[:, c * NLOC:(c + 1) * NLOC, :] = outs[c].reshape(B, NLOC, D)
    return full


# version 05
# speedup vs baseline: 1.0680x; 1.0680x over previous
import numpy as np
from contextlib import ExitStack
import concourse.bass as bass
import concourse.mybir as mybir
from concourse.bass import IndirectOffsetOnAxis
from concourse.bass_utils import run_bass_kernel_spmd

F32 = mybir.dt.float32
BF16 = mybir.dt.bfloat16
U32 = mybir.dt.uint32
AF = mybir.ActivationFunctionType
ALU = mybir.AluOpType
AX = mybir.AxisListType

NCORE = 8
B = 4
SEQ = 4096
D = 1024
NLOC = SEQ // NCORE
T = B * NLOC
TT = T // 128
DT = D // 128
CIN = 2048
CT = CIN // 128
CSH = CIN // NCORE
NG = 32
Q = 64
NCH = NLOC // Q
NBI = B * NCH
NST = 16
TPAD = 144
LN_EPS = 1e-5
NSL = 8
CSL = CIN // NSL

PKH, PKK, NEXP, DIN0 = 8, 128, 16384, 2048
MH, MNE, MSE, MD = 8, 16, 256, 128
MES = MNE * MSE
RG8 = [[0, 1, 2, 3, 4, 5, 6, 7]]
A2A = CSH * NST * NBI


class St:

    def __init__(self, sem, step=1):
        self.e = None
        self.sem = sem
        self.step = step
        self.n = 0
        self.dma = False

    def put(self, instr, k=None):
        k = self.step if k is None else k
        instr.then_inc(self.sem, k)
        self.n += k

    def evs(self):
        return [(self.sem, self.n)]


class DmaSt:

    def __init__(self, sems):
        self.e = None
        self.pool = sems
        self.P = len(sems)
        self.idx = 0
        self.uses = [0] * self.P
        self.dma = True

    def sem_of(self, i):
        return self.pool[i % self.P], (i // self.P + 1) * 16

    def evs(self):
        return [(self.pool[j], self.uses[j] * 16) for j in range(self.P)
                if self.uses[j] > 0]


class Sched:

    KEYS = ("q", "g", "c", "t", "v", "s", "q2")

    def __init__(self, ST):
        self.ST = ST
        self.ops = {k: [] for k in self.KEYS}
        self.lbl = {}

    _AUTO = [0]
    DMA_KEYS = ("q", "g", "q2")

    def add(self, key, fn, deps=(), label=None):
        idx = len(self.ops[key])
        if label is None:
            Sched._AUTO[0] += 1
            label = f"_auto{Sched._AUTO[0]}"
        assert label not in self.lbl, f"dup label {label}"
        self.lbl[label] = (key, idx)
        flat = []
        for d in deps:
            if isinstance(d, list):
                flat.extend(d)
            else:
                flat.append(d)
        snap = {k: len(self.ops[k]) for k in self.DMA_KEYS}
        self.ops[key].append((fn, tuple(flat), label, snap))
        return label

    def emit(self, blk, ext_deps=()):
        openers = {"q": blk.sync, "g": blk.gpsimd, "c": blk.gpsimd,
                   "t": blk.tensor, "v": blk.vector, "s": blk.scalar,
                   "q2": blk.scalar}
        ST = self.ST
        bases = {k: (ST[k].idx if k in self.DMA_KEYS else ST[k].n)
                 for k in self.KEYS}

        def val_of(d, snap):
            if not isinstance(d, str):
                return d
            k, i = self.lbl[d]
            if k in self.DMA_KEYS:
                return ST[k].sem_of(bases[k] + i)
            return (ST[k].sem, bases[k] + i + 1)

        for k in self.KEYS:
            if not self.ops[k]:
                continue
            ops = self.ops[k]

            def body(eng, k=k, ops=ops):
                st = ST[k]
                st.e = eng
                waited = {}
                for ev in ext_deps:
                    if ev is not None and ev[1] > 0:
                        eng.wait_ge(ev[0], ev[1])
                        waited[id(ev[0])] = ev[1]
                for fn, deps, label, snap in ops:
                    for d in deps:
                        if d is None:
                            continue
                        sem, v = val_of(d, snap)
                        if waited.get(id(sem), -1) < v:
                            eng.wait_ge(sem, v)
                            waited[id(sem)] = v
                    if st.dma:
                        sem = st.pool[st.idx % st.P]
                        prior = st.uses[st.idx % st.P]
                        if prior > 0 and waited.get(id(sem), -1) < prior * 16:
                            eng.wait_ge(sem, prior * 16)
                            waited[id(sem)] = prior * 16
                        r = fn(eng)
                        r.then_inc(sem, 16)
                        st.uses[st.idx % st.P] += 1
                        st.idx += 1
                    else:
                        r = fn(eng)
                        if r is not None:
                            st.put(r)
                        else:
                            eng.sem_inc(st.sem, st.step)
                            st.n += st.step
            openers[k](body)
        evs = {}
        for k in self.KEYS:
            evs[k] = ST[k].evs()
        return evs


def build_program(dbg=None, nblocks=4, phases="ABCDEFGHI", csub=4):
    nc = bass.Bass("TRN2", target_bir_lowering=False, enable_partition_id=False)

    def din(name, shape, dt=F32):
        return nc.dram_tensor(name, list(shape), dt, kind="ExternalInput").ap()

    def dint(name, shape, dt=F32, shared=False):
        kw = dict(kind="Internal")
        if shared:
            kw["addr_space"] = "Shared"
        return nc.dram_tensor(name, list(shape), dt, **kw).ap()

    x_in = din("x_slice", (T, D))
    cid_in = din("core_id", (1, 1), U32)
    w_in = din("w_in", (D, CIN))
    b_in = din("b_in", (1, CIN))
    w_gate = din("w_gate", (D, CIN))
    b_gate = din("b_gate", (1, CIN))
    w_out = din("w_out", (CIN, D))
    b_out = din("b_out", (1, D))
    a_sh = din("A_shard", (CSH, NST, NST))
    bssm_sh = din("B_shard", (CSH, NST))
    cssm_sh = din("C_shard", (CSH, NST))

    identf = din("c_identf", (128, 128))

    pk_keysT = din("pk_keysT", (2, PKH, 512, PKK))
    pk_wq = din("pk_wq", (2 * PKH * 512, D))
    pk_w_inT = din("pk_w_inT", (128, DT, DIN0))
    pk_w_outT = din("pk_w_outT", (DT, 128, DIN0 // 128, 128))
    pk_in_a = din("pk_in_a", (NEXP, D))
    pk_in_b = din("pk_in_b", (NEXP, DIN0))
    pk_out_a = din("pk_out_a", (NEXP, DIN0))
    pk_out_b = din("pk_out_b", (NEXP, D))

    la_wqT = din("la_wqT", (D, D))
    la_wkT = din("la_wkT", (D, D))
    la_wvT = din("la_wvT", (D, D))
    la_woT = din("la_woT", (DT, 128, DT, 128))
    la_bq = din("la_bq", (1, D))
    la_bk = din("la_bk", (1, D))
    la_bv = din("la_bv", (1, D))
    la_bo = din("la_bo", (1, D))
    la_beta = din("la_beta", (1, 16))

    mo_wmhT = din("mo_wmhT", (128, DT, D))
    mo_wrT = din("mo_wrT", (D, MNE))
    mo_kfT = din("mo_kfT", (MH, 128, MES))
    mo_vfT = din("mo_vfT", (MH, 128, 32, 128))
    mo_wmgT = din("mo_wmgT", (DT, 128, DT, 128))

    onesf_r = din("c_onesf_r", (1, 512))
    onesf_c = din("c_onesf_c", (128, 1))
    bdones = din("c_bdones", (128, 8))

    out_t = nc.dram_tensor("out", [T, D], F32, kind="ExternalOutput").ap()
    dbg_t = None
    if dbg is not None:
        dbg_t = nc.dram_tensor("dbg", [128, DT * T], F32,
                               kind="ExternalOutput").ap()

    MATC = Q + 2 * Q * NST
    ct_dram = dint("ct_dram", (NST, CSH))
    bt_dram = dint("bt_dram", (NST, CSH))
    mat_sh = dint("mat_sh", (1, MATC * CSH), BF16)
    mat_all = dint("mat_all", (NCORE, MATC * CSH), BF16, shared=True)
    k_dram = dint("k_dram", (Q, CIN), BF16)
    kx_dram = dint("kx_dram", (Q, Q, CIN), BF16)
    v_dram = dint("v_dram", (Q, NST, CIN), BF16)
    u_dram = dint("u_dram", (Q, NST, CIN), BF16)
    t_dram = dint("t_dram", (Q, TPAD, CIN), BF16)
    xpf_dram = dint("xpf_dram", (Q, B, NCH, CIN), BF16)
    gate_dram = dint("gate_dram", (CT, 128, T), BF16)
    y_dram = dint("y_dram", (CT, 128, T), BF16)
    yl_dram = dint("yl_dram", (128, 512, Q), BF16)
    ga_in = dint("ga_in", (NCORE, A2A), BF16)
    ga_all = dint("ga_all", (NCORE, NCORE * A2A), BF16, shared=True)
    ga_out = dint("ga_out", (NCORE, A2A), BF16)
    ha_in = dint("ha_in", (NCORE, A2A), BF16)
    ha_all = dint("ha_all", (NCORE, NCORE * A2A), BF16, shared=True)
    ha_out = dint("ha_out", (NCORE, A2A), BF16)
    xln2_dram = dint("xln2_dram", (TT, 128, D), BF16)
    scb_dram = dint("scb_dram", (1, 16))
    scb2_dram = dint("scb2_dram", (1, D))
    wt_dram = dint("wt_dram", (MNE, T))

    _es1 = ExitStack()
    R = _es1.enter_context(nc.sbuf_tensor("R", [128, DT, T], F32))
    IDF = _es1.enter_context(nc.sbuf_tensor("IDF", [128, 128], F32))
    IDB = _es1.enter_context(nc.sbuf_tensor("IDB", [128, 128], BF16))
    ONESR = _es1.enter_context(nc.sbuf_tensor("ONESR", [1, 512], BF16))
    ONESRF = _es1.enter_context(nc.sbuf_tensor("ONESRF", [1, 512], F32))
    ONESCF = _es1.enter_context(nc.sbuf_tensor("ONESCF", [128, 1], F32))
    ONESCB = _es1.enter_context(nc.sbuf_tensor("ONESCB", [128, 1], BF16))
    POOLQ = [_es1.enter_context(nc.semaphore(f"pq{i}")) for i in range(20)]
    POOLG = [_es1.enter_context(nc.semaphore(f"pg{i}")) for i in range(8)]
    POOLQ2 = [_es1.enter_context(nc.semaphore(f"pq2_{i}")) for i in range(12)]
    CIDT = _es1.enter_context(nc.sbuf_tensor("CIDT", [1, 1], U32))
    JNK = _es1.enter_context(nc.sbuf_tensor("JNK", [1, 32], F32))
    BDONES = _es1.enter_context(nc.sbuf_tensor("BDONES", [128, 8], BF16))
    BIASR = _es1.enter_context(nc.sbuf_tensor("BIASR", [1, 5, 2048], BF16))
    _esBD = ExitStack()
    BD64 = _esBD.enter_context(nc.sbuf_tensor("BD64", [128, NG * 128], BF16))
    BD512 = _esBD.enter_context(nc.sbuf_tensor("BD512", [128, NG * 128], BF16))
    sem_q = _es1.enter_context(nc.semaphore("sq"))
    sem_g = _es1.enter_context(nc.semaphore("sg"))
    sem_c = _es1.enter_context(nc.semaphore("scc"))
    sem_t = _es1.enter_context(nc.semaphore("stt"))
    sem_v = _es1.enter_context(nc.semaphore("svv"))
    sem_s = _es1.enter_context(nc.semaphore("sss"))
    sem_q2 = _es1.enter_context(nc.semaphore("sq2"))
    blk = _es1.enter_context(nc.Block())
    if True:
        ST = {"q": DmaSt(POOLQ), "g": DmaSt(POOLG), "c": St(sem_c, 1),
              "t": St(sem_t, 1), "v": St(sem_v, 1), "s": St(sem_s, 1),
              "q2": DmaSt(POOLQ2)}

        BSLOT = {"b_in": (0, 0), "b_gate": (1, 0), "b_out": (2, 0),
                 "bq": (2, 1024), "bk": (3, 0), "bv": (3, 1024), "bo": (4, 0)}

        def brow(nm, lo, w):
            sl, off = BSLOT[nm]
            return BIASR.ap()[0:1, sl, off + lo:off + lo + w]

        def dbg_dump(sc, src_ap_fn, deps, nelem=DT * T):
            sc.add("q", lambda e: e.dma_start(
                out=dbg_t[:, 0:nelem], in_=src_ap_fn()), deps=deps,
                label="dbgdump")

        _es2 = ExitStack()
        XTOK = _es2.enter_context(nc.sbuf_tensor("XTOK", [128, 8 * D], F32))
        BDAF = _es2.enter_context(nc.sbuf_tensor("BDAF", [128, NG * 128], F32))
        BCOLF = _es2.enter_context(nc.sbuf_tensor("BCOLF", [128, NG], F32))
        BDONESF = _es2.enter_context(nc.sbuf_tensor("BDONESF", [128, 8], F32))
        BSCR = _es2.enter_context(nc.sbuf_tensor("BSCR", [1, 2048], F32))
        PSTR0 = _es2.enter_context(nc.psum_tensor("PSTR0", [128, 128], F32))
        PSTR1 = _es2.enter_context(nc.psum_tensor("PSTR1", [128, 128], F32))
        BDA = _es2.enter_context(nc.sbuf_tensor("BDA", [128, NG * 128], BF16))
        CCOLF = _es2.enter_context(nc.sbuf_tensor("CCOLF", [128, NG], F32))
        CCOLB = _es2.enter_context(nc.sbuf_tensor("CCOLB", [128, NG], BF16))
        BCOLB = _es2.enter_context(nc.sbuf_tensor("BCOLB", [128, NG], BF16))
        BDN = _es2.enter_context(nc.sbuf_tensor("BDN", [128, 128], BF16))
        BDT = _es2.enter_context(nc.sbuf_tensor("BDT", [128, 128], BF16))
        USTK = _es2.enter_context(nc.sbuf_tensor("USTK", [128, Q], BF16))
        VSTK = _es2.enter_context(nc.sbuf_tensor("VSTK", [128, Q], BF16))
        KTMP = _es2.enter_context(nc.sbuf_tensor("KTMP", [128, Q], BF16))
        KT8 = _es2.enter_context(nc.sbuf_tensor("KT8", [8, Q], BF16))
        KSB = _es2.enter_context(nc.sbuf_tensor("KSB", [Q, CSH], BF16))
        VSB = _es2.enter_context(nc.sbuf_tensor("VSB", [Q, NST * CSH], BF16))
        USB = _es2.enter_context(nc.sbuf_tensor("USB", [Q, NST * CSH], BF16))
        PSMA = _es2.enter_context(nc.psum_tensor("PSMA", [128, 128], F32))
        PSMB = _es2.enter_context(nc.psum_tensor("PSMB", [128, 128], F32))
        PSMT = _es2.enter_context(nc.psum_tensor("PSMT", [128, 128], BF16))
        if True:
            sc = Sched(ST)
            l_idf = sc.add("q", lambda e: e.dma_start(out=IDF[:], in_=identf))
            l_xh = []
            l_xh.append(sc.add("q", lambda e: e.dma_start(
                out=XTOK.ap().rearrange("p (a b) -> p a b", a=8),
                in_=x_in.rearrange("(a p) b -> p a b", p=128)
                [:, 0:8, :]), deps=[], label="l_x0"))
            l_idb = sc.add("v", lambda e: e.tensor_copy(out=IDB[:], in_=IDF[:]),
                           deps=[l_idf])
            sc.add("q", lambda e: e.dma_start(out=ONESRF[:], in_=onesf_r),
                   label="l_orf")
            sc.add("v", lambda e: e.tensor_copy(out=ONESR[:], in_=ONESRF[:]),
                   deps=["l_orf"], label="l_or")
            sc.add("q", lambda e: e.dma_start(out=ONESCF[:], in_=onesf_c),
                   label="l_ocf")
            l_bdo0 = sc.add("q", lambda e: e.dma_start(out=BDONESF[:],
                                                       in_=bdones))
            l_bdo = sc.add("v", lambda e: e.tensor_copy(out=BDONES[:],
                                                        in_=BDONESF[:]),
                           deps=[l_bdo0])
            sc.add("v", lambda e: e.tensor_copy(out=ONESCB[:], in_=ONESCF[:]),
                   deps=["l_ocf"], label="l_ocb")
            sc.add("q", lambda e: e.dma_start(out=CIDT[:], in_=cid_in),
                   label="l_cid")
            sc.add("v", lambda e: e.memset(JNK[:], 0.0), label="l_jnk")
            bias_l = []
            prevb = None
            for nm, src, w in [("b_in", b_in, 2048), ("b_gate", b_gate, 2048),
                               ("b_out", b_out, 1024), ("bq", la_bq, 1024),
                               ("bk", la_bk, 1024), ("bv", la_bv, 1024),
                               ("bo", la_bo, 1024)]:
                lb = sc.add("q", lambda e, src=src, w=w: e.dma_start(
                    out=BSCR.ap()[0:1, 0:w], in_=src),
                    deps=[prevb] if prevb else [], label=f"lbf_{nm}")
                prevb = sc.add("v", lambda e, nm=nm, w=w: e.tensor_copy(
                    out=brow(nm, 0, w), in_=BSCR.ap()[0:1, 0:w]),
                    deps=[lb], label=f"l_{nm}")
                bias_l.append(prevb)
            l_xh.append(sc.add("q", lambda e: e.dma_start(
                out=XTOK.ap().rearrange("p (a b) -> p a b", a=8),
                in_=x_in.rearrange("(a p) b -> p a b", p=128)
                [:, 8:16, :]), deps=["trv63"], label="l_x1"))

            tr_l = []
            for i in range(TT * DT):
                tt, k = divmod(i, DT)
                ps = PSTR0 if i % 2 == 0 else PSTR1
                deps = [l_xh[i // 64], l_idf] + ([f"trv{i - 2}"] if i >= 2
                                                 else [])
                sc.add("t", lambda e, ps=ps, i=i: e.transpose(
                    ps.ap(), XTOK.ap()[:, (i % 64) * 128:(i % 64 + 1) * 128],
                    IDF[:]), deps=deps, label=f"trt{i}")
                sc.add("v", lambda e, ps=ps, tt=tt, k=k: e.tensor_copy(
                    out=R.ap()[:, k, tt * 128:(tt + 1) * 128], in_=ps.ap()),
                    deps=[f"trt{i}"], label=f"trv{i}")
                tr_l.append(f"trv{i}")

            m_z = sc.add("v", lambda e: e.memset(BDAF[:], 0.0))
            bd_loads0 = []
            for gsub in range(8):
                for c4 in range(4):
                    bd_loads0.append(sc.add("q", lambda e, gsub=gsub, c4=c4:
                        e.dma_start(
                            out=BDAF.ap().rearrange("p (g f) -> p g f", g=NG)
                            [gsub * 16:gsub * 16 + 16, c4::4,
                             gsub * 16:gsub * 16 + 16],
                            in_=a_sh[4 * gsub + c4::32, :, :]
                            .rearrange("gb n m -> n gb m")), deps=[m_z]))
            bdc = sc.add("v", lambda e: e.tensor_copy(out=BDA[:], in_=BDAF[:]),
                         deps=bd_loads0 + [m_z], label="l_bdc")
            colq = []
            from concourse.ap import AP as _APc
            ctr = sc.add("q", lambda e: e.dma_start(
                out=_APc(ct_dram.tensor, 0, [[CSH, NST], [1, CSH]]),
                in_=_APc(cssm_sh.tensor, 0, [[1, NST], [NST, CSH]]),
                allow_slow_non_contiguous=True))
            btr = sc.add("q", lambda e: e.dma_start(
                out=_APc(bt_dram.tensor, 0, [[CSH, NST], [1, CSH]]),
                in_=_APc(bssm_sh.tensor, 0, [[1, NST], [NST, CSH]]),
                allow_slow_non_contiguous=True))
            for gsub in range(8):
                for c4 in range(4):
                    colq.append(sc.add("q", lambda e, gsub=gsub, c4=c4:
                        e.dma_start(
                            out=_APc(CCOLF, gsub * 16 * NG + c4,
                                     [[NG, 16], [4, 8]]),
                            in_=_APc(ct_dram.tensor, 4 * gsub + c4,
                                     [[CSH, NST], [32, 8]]),
                            allow_slow_non_contiguous=True),
                        deps=[ctr]))
                    colq.append(sc.add("q", lambda e, gsub=gsub, c4=c4:
                        e.dma_start(
                            out=_APc(BCOLF, gsub * 16 * NG + c4,
                                     [[NG, 16], [4, 8]]),
                            in_=_APc(bt_dram.tensor, 4 * gsub + c4,
                                     [[CSH, NST], [32, 8]]),
                            allow_slow_non_contiguous=True),
                        deps=[btr]))
            ccb = sc.add("v", lambda e: e.tensor_copy(out=CCOLB[:],
                                                      in_=CCOLF[:]),
                         deps=colq, label="l_ccb")
            bcb = sc.add("v", lambda e: e.tensor_copy(out=BCOLB[:],
                                                      in_=BCOLF[:]),
                         deps=colq, label="l_bcb")
            bd_loads = [bdc, ccb, bcb] + colq

            mat_end = []
            for g_ in range(NG):
                P = f"m{g_}"
                pre = ([f"m{g_ - 1}end"] if g_ > 0
                       else bd_loads + [l_idb, l_bdo, "l_ocf"])
                BDAg = BDA.ap()[:, g_ * 128:(g_ + 1) * 128]

                tp0 = sc.add("t", lambda e, BDAg=BDAg: e.transpose(
                    PSMT.ap(), BDAg, IDB[:]), deps=pre, label=P + "tp0")
                bdt0 = sc.add("v", lambda e: e.tensor_copy(
                    out=BDT[:], in_=PSMT.ap()), deps=[tp0], label=P + "bdt0")
                bdn0 = sc.add("v", lambda e, BDAg=BDAg: e.tensor_copy(
                    out=BDN[:], in_=BDAg), deps=pre, label=P + "bdn0")
                v0 = sc.add("v", lambda e, g_=g_: e.tensor_copy(
                    out=VSTK.ap()[:, 0:1], in_=BCOLB.ap()[:, g_:g_ + 1]),
                    deps=pre, label=P + "v0")
                u1m = sc.add("t", lambda e, g_=g_, BDAg=BDAg: e.matmul(
                    PSMA.ap()[:, 0:1], BDAg, CCOLB.ap()[:, g_:g_ + 1],
                    start=True, stop=True), deps=pre + [tp0], label=P + "u1m")
                u0 = sc.add("v", lambda e: e.tensor_copy(
                    out=USTK.ap()[:, 0:1], in_=PSMA.ap()[:, 0:1]),
                    deps=[u1m], label=P + "u0")
                lastU, lastV, lastN, lastT = u0, v0, bdn0, bdt0
                lastSQN = lastSQT = None
                for k in range(6):
                    m = 1 << k
                    us = sc.add("t", lambda e, m=m: e.matmul(
                        PSMA.ap()[:, 0:m], BDN[:], USTK.ap()[:, 0:m],
                        start=True, stop=True),
                        deps=[lastN, lastU, u0], label=P + f"us{k}")
                    vs = sc.add("t", lambda e, m=m: e.matmul(
                        PSMB.ap()[:, 0:m], BDT[:], VSTK.ap()[:, 0:m],
                        start=True, stop=True),
                        deps=[lastT, lastV], label=P + f"vs{k}")
                    ue = sc.add("v", lambda e, m=m: e.tensor_copy(
                        out=USTK.ap()[:, m:2 * m], in_=PSMA.ap()[:, 0:m]),
                        deps=[us], label=P + f"ue{k}")
                    ve = sc.add("v", lambda e, m=m: e.tensor_copy(
                        out=VSTK.ap()[:, m:2 * m], in_=PSMB.ap()[:, 0:m]),
                        deps=[vs], label=P + f"ve{k}")
                    lastU, lastV = ue, ve
                    sqn = sc.add("t", lambda e: e.matmul(
                        PSMA.ap(), BDT[:], BDN[:], start=True, stop=True),
                        deps=[us, vs, ue, lastN, lastT], label=P + f"sqn{k}")
                    sqt = sc.add("t", lambda e: e.matmul(
                        PSMB.ap(), BDN[:], BDT[:], start=True, stop=True),
                        deps=[us, vs, ve, sqn], label=P + f"sqt{k}")
                    nn_ = sc.add("v", lambda e: e.tensor_copy(
                        out=BDN[:], in_=PSMA.ap()), deps=[sqn, sqt],
                        label=P + f"nn{k}")
                    tt_ = sc.add("v", lambda e: e.tensor_copy(
                        out=BDT[:], in_=PSMB.ap()), deps=[sqn, sqt],
                        label=P + f"tt{k}")
                    if k == 5:
                        sv64 = sc.add("v", lambda e, g_=g_: e.tensor_copy(
                            out=BD64.ap()[:, g_ * 128:(g_ + 1) * 128],
                            in_=PSMB.ap()), deps=[sqn, sqt], label=P + "sv64")
                    lastN, lastT = nn_, tt_
                for k2 in range(2):
                    sqn = sc.add("t", lambda e: e.matmul(
                        PSMA.ap(), BDT[:], BDN[:], start=True, stop=True),
                        deps=[lastN, lastT] + ([sv64] if k2 == 0 else []),
                        label=P + f"p2n{k2}")
                    sqt = sc.add("t", lambda e: e.matmul(
                        PSMB.ap(), BDN[:], BDT[:], start=True, stop=True),
                        deps=[lastN, lastT, sqn] +
                             ([sv64] if k2 == 0 else []),
                        label=P + f"p2t{k2}")
                    nn_ = sc.add("v", lambda e: e.tensor_copy(
                        out=BDN[:], in_=PSMA.ap()), deps=[sqn, sqt],
                        label=P + f"p2nn{k2}")
                    tt_ = sc.add("v", lambda e: e.tensor_copy(
                        out=BDT[:], in_=PSMB.ap()), deps=[sqn, sqt],
                        label=P + f"p2tt{k2}")
                    lastN, lastT = nn_, tt_
                sq512 = sc.add("t", lambda e: e.matmul(
                    PSMB.ap(), BDN[:], BDT[:], start=True, stop=True),
                    deps=[lastN, lastT], label=P + "sq512")
                sv512 = sc.add("v", lambda e, g_=g_: e.tensor_copy(
                    out=BD512.ap()[:, g_ * 128:(g_ + 1) * 128],
                    in_=PSMB.ap()), deps=[sq512], label=P + "sv512")

                ktm = sc.add("v", lambda e, g_=g_: e.tensor_scalar(
                    out=KTMP[:], in0=VSTK[:], scalar1=CCOLF.ap()[:, g_:g_ + 1],
                    scalar2=None, op0=ALU.mult), deps=[lastV], label=P + "ktm")
                kmm = sc.add("t", lambda e: e.matmul(
                    PSMA.ap()[0:8, 0:Q], BDONES[:], KTMP[:],
                    start=True, stop=True), deps=[ktm, sq512], label=P + "kmm")
                kev = sc.add("v", lambda e: e.tensor_copy(
                    out=KT8[:], in_=PSMA.ap()[0:8, 0:Q]), deps=[kmm],
                    label=P + "kev")
                ktr = sc.add("t", lambda e: e.transpose(
                    PSMT.ap()[0:Q, 0:8], KT8[:], IDB.ap()[0:8, 0:8]),
                    deps=[kev], label=P + "ktr")
                gb, c4 = divmod(g_, 4)
                off = 32 * gb + c4
                kse = sc.add("v", lambda e, off=off: e.tensor_copy(
                    out=KSB.ap()[:, off:off + 29:4],
                    in_=PSMT.ap()[0:Q, 0:8]), deps=[ktr], label=P + "kse")
                utr = sc.add("t", lambda e: e.transpose(
                    PSMT.ap()[0:Q, :], USTK[:], IDB[:]),
                    deps=[lastU, ktr, kse], label=P + "utr")
                use = sc.add("v", lambda e, off=off: e.tensor_copy(
                    out=USB.ap().rearrange("p (n c) -> p n c", n=NST)
                    [:, :, off:off + 29:4],
                    in_=PSMT.ap()[0:Q, :].rearrange("p (g n) -> p g n", g=8)
                    .transpose([0, 2, 1])), deps=[utr], label=P + "use")
                vtr = sc.add("t", lambda e: e.transpose(
                    PSMT.ap()[0:Q, :], VSTK[:], IDB[:]),
                    deps=[ktm, use], label=P + "vtr")
                vse = sc.add("v", lambda e, off=off: e.tensor_copy(
                    out=VSB.ap().rearrange("p (n c) -> p n c", n=NST)
                    [:, :, off:off + 29:4],
                    in_=PSMT.ap()[0:Q, :].rearrange("p (g n) -> p g n", g=8)
                    .transpose([0, 2, 1])), deps=[vtr], label=P + "vse")
                sc.add("v", lambda e: e.tensor_copy(
                    out=KT8.ap()[0:1, 0:1], in_=KT8.ap()[0:1, 0:1]),
                    deps=[kse, use, vse, sv64, sv512], label=f"m{g_}end")
                mat_end.append(f"m{g_}end")

            mk = sc.add("q", lambda e: e.dma_start(
                out=mat_sh[0:1, 0:Q * CSH]
                .rearrange("o (p c) -> (o p) c", p=Q), in_=KSB.ap()),
                deps=mat_end)
            mv = sc.add("q", lambda e: e.dma_start(
                out=mat_sh[0:1, Q * CSH:(Q + Q * NST) * CSH]
                .rearrange("o (p c) -> (o p) c", p=Q), in_=VSB.ap()),
                deps=mat_end)
            mu = sc.add("q", lambda e: e.dma_start(
                out=mat_sh[0:1, (Q + Q * NST) * CSH:]
                .rearrange("o (p c) -> (o p) c", p=Q), in_=USB.ap()),
                deps=mat_end)
            ag = sc.add("c", lambda e: e.collective_compute(
                "AllGather", ALU.bypass, replica_groups=RG8,
                ins=[mat_sh], outs=[mat_all]), deps=[mk, mv, mu])
            KO, VO = Q * CSH, (Q + Q * NST) * CSH
            mre = []
            for r in range(NCORE):
                mre.append(sc.add("q", lambda e, r=r: e.dma_start(
                    out=k_dram[:, r * CSH:(r + 1) * CSH],
                    in_=mat_all[r:r + 1, 0:KO]
                    .rearrange("o (j c) -> (o j) c", j=Q)), deps=[ag]))
                mre.append(sc.add("q", lambda e, r=r: e.dma_start(
                    out=v_dram.rearrange("j n c -> (j n) c")
                    [:, r * CSH:(r + 1) * CSH],
                    in_=mat_all[r:r + 1, KO:VO]
                    .rearrange("o (j c) -> (o j) c", j=Q * NST)), deps=[ag]))
                mre.append(sc.add("q", lambda e, r=r: e.dma_start(
                    out=u_dram.rearrange("j n c -> (j n) c")
                    [:, r * CSH:(r + 1) * CSH],
                    in_=mat_all[r:r + 1, VO:]
                    .rearrange("o (j c) -> (o j) c", j=Q * NST)), deps=[ag]))
            if dbg == "R0":
                dbg_dump(sc, lambda: R.ap().rearrange("p a b -> p (a b)"), tr_l)
            evA = sc.emit(blk)
            _es2.close()
        prev = [e for v in evA.values() for e in v]
        ev_mats = evA["q"]
        ev_R0 = evA["v"]

        _es3 = ExitStack()
        XLN = _es3.enter_context(nc.sbuf_tensor("XLN", [128, DT, T], BF16))
        SQH = _es3.enter_context(nc.sbuf_tensor("SQH", [128, DT, 512], BF16))
        ROWS = _es3.enter_context(nc.sbuf_tensor("ROWS", [1, 3, T], F32))
        LNA = _es3.enter_context(nc.sbuf_tensor("LNA", [128, 2, 512], F32))
        LNB = _es3.enter_context(nc.sbuf_tensor("LNB", [128, 2, 512], F32))
        WSB = _es3.enter_context(nc.sbuf_tensor("WSB", [128, DT, 1024], BF16))
        XTM = _es3.enter_context(nc.sbuf_tensor("XTM", [128, 2, CIN], BF16))
        GTM = _es3.enter_context(nc.sbuf_tensor("GTM", [128, 2, 512], BF16))
        PSS0 = _es3.enter_context(nc.psum_tensor("PSS0", [1, 512], F32))
        PSS1 = _es3.enter_context(nc.psum_tensor("PSS1", [1, 512], F32))
        PSQ0 = _es3.enter_context(nc.psum_tensor("PSQ0", [1, 512], F32))
        PSQ1 = _es3.enter_context(nc.psum_tensor("PSQ1", [1, 512], F32))
        PSB0 = _es3.enter_context(nc.psum_tensor("PSB0", [128, 512], F32))
        PSB1 = _es3.enter_context(nc.psum_tensor("PSB1", [128, 512], F32))
        if True:
            sc = Sched(ST)
            emit_ln(sc, R, XLN, SQH, ROWS, LNA, LNB, PSS0, PSS1, PSQ0, PSQ1,
                    PSB0, PSB1, ONESCF, ONESCB, ONESRF, "ln0")
            for half in range(2):
                lw = sc.add("g", lambda e, half=half: e.dma_start(
                    out=WSB.ap(),
                    in_=w_in.rearrange("(a p) c -> p a c", p=128)
                    [:, :, half * 1024:(half + 1) * 1024]),
                    deps=([f"xpev{half - 1}_{TT - 1}_{2 * half - 1}"]
                          if half > 0 else []),
                    label=f"lwin{half}")
                for tt in range(TT):
                    strips = []
                    for s2 in range(2):
                        pb = PSB0 if (2 * tt + s2) % 2 == 0 else PSB1
                        s4 = half * 2 + s2
                        dep = [lw, f"ln0_ap{(tt // 4) * DT + DT - 1}"]
                        if tt == 0 and half == 0:
                            dep += ["ln0_bav2", "ln0_bbv2", "ln0_bav3",
                                    "ln0_bbv3"]
                        if tt >= 1:
                            dep += [f"xpev{half}_{tt - 1}_{s2}"]
                        if tt >= 2:
                            dep += [f"xpdma{half}_{tt - 2}_0",
                                    f"xpdma{half}_{tt - 2}_1"]
                        sc.add("t", lambda e, s4=s4, pb=pb: e.matmul(
                            pb.ap(), ONESR.ap()[0:1, 0:128],
                            brow("b_in", s4 * 512, 512), start=True,
                            stop=False), deps=dep, label=f"xpb{half}_{tt}_{s2}")
                        for k in range(DT):
                            sc.add("t", lambda e, s2=s2, k=k, tt=tt, pb=pb:
                                e.matmul(
                                    pb.ap(),
                                    XLN.ap()[:, k, tt * 128:(tt + 1) * 128],
                                    WSB.ap()[:, k, s2 * 512:(s2 + 1) * 512],
                                    start=False, stop=(k == DT - 1)),
                                deps=[f"xpb{half}_{tt}_{s2}"],
                                label=f"xpm{half}_{tt}_{s2}_{k}")
                        strips.append(sc.add("s",
                            lambda e, s4=s4, tt=tt, pb=pb: e.copy(
                                out=XTM.ap()[:, tt % 2,
                                             s4 * 512:(s4 + 1) * 512],
                                in_=pb.ap()),
                            deps=[f"xpm{half}_{tt}_{s2}_{DT - 1}"],
                            label=f"xpev{half}_{tt}_{s2}"))
                    b, i2 = divmod(tt, 4)
                    for h in range(2):
                        sc.add("q", lambda e, tt=tt, b=b, i2=i2, h=h,
                               half=half: e.dma_start(
                            out=xpf_dram[:, b, i2 * 2 + h,
                                         half * 1024:(half + 1) * 1024],
                            in_=XTM.ap()[h * 64:(h + 1) * 64, tt % 2,
                                         half * 1024:(half + 1) * 1024]),
                            deps=strips, label=f"xpdma{half}_{tt}_{h}")
            xp_last = f"xpdma1_{TT - 1}_1"
            for half in range(2):
                lg = sc.add("g", lambda e, half=half: e.dma_start(
                    out=WSB.ap(),
                    in_=w_gate.rearrange("(a p) c -> p a c", p=128)
                    [:, :, half * 1024:(half + 1) * 1024]),
                    deps=[xp_last] if half == 0 else
                         [f"gtdma{8 * half - 1}_3"],
                    label=f"lwg{half}")
                for ct8 in range(8):
                    ct = half * 8 + ct8
                    for tc in range(4):
                        i = ct * 4 + tc
                        pb = PSB0 if i % 2 == 0 else PSB1
                        dep0 = [lg, xp_last]
                        if i >= 2:
                            pi = i - 2
                            dep0 += [f"gtev{pi // 4}_{pi % 4}",
                                     f"gtdma{pi // 4}_{pi % 4}"]
                        sc.add("t", lambda e, ct=ct, pb=pb: e.matmul(
                            pb.ap(), brow("b_gate", ct * 128, 128),
                            ONESR.ap()[0:1, 0:512], start=True, stop=False),
                            deps=dep0 + [f"ln0_ap{tc * DT + DT - 1}"],
                            label=f"gtb{ct}_{tc}")
                        for k in range(DT):
                            sc.add("t", lambda e, ct8=ct8, tc=tc, k=k, pb=pb:
                                e.matmul(
                                    pb.ap(),
                                    WSB.ap()[:, k, ct8 * 128:(ct8 + 1) * 128],
                                    XLN.ap()[:, k, tc * 512:(tc + 1) * 512],
                                    start=False, stop=(k == DT - 1)),
                                deps=[f"gtb{ct}_{tc}"],
                                label=f"gtm{ct}_{tc}_{k}")
                        sc.add("s", lambda e, ct=ct, tc=tc, pb=pb:
                            e.activation(
                                out=GTM.ap()[:, (ct * 4 + tc) % 2, :],
                                in_=pb.ap(), func=AF.Sigmoid),
                            deps=[f"gtm{ct}_{tc}_{DT - 1}"],
                            label=f"gtev{ct}_{tc}")
                        sc.add("q", lambda e, ct=ct, tc=tc: e.dma_start(
                            out=gate_dram[ct, :, tc * 512:(tc + 1) * 512],
                            in_=GTM.ap()[:, (ct * 4 + tc) % 2, :]),
                            deps=[f"gtev{ct}_{tc}"], label=f"gtdma{ct}_{tc}")
            if dbg == "XLN0":
                dbg_dump(sc, lambda: R.ap().rearrange("p a b -> p (a b)"),
                         [f"ln0_ap{i}" for i in range(32)])
            evB = sc.emit(blk, ext_deps=prev)
            _es3.close()
        prev = [e for v in evB.values() for e in v]
        ev_xp = evB["q2"]

        if "C" in phases:
            _es4 = ExitStack()
            ZSB = _es4.enter_context(nc.sbuf_tensor("ZSB", [128, 2048], BF16))
            TWIN = _es4.enter_context(nc.sbuf_tensor("TWIN", [128, 80 * CSL], BF16))
            XPW = _es4.enter_context(nc.sbuf_tensor("XPW", [128, NBI * CSL], BF16))
            YLT = _es4.enter_context(nc.sbuf_tensor("YLT", [128, 8, Q], BF16))
            GA = _es4.enter_context(nc.sbuf_tensor("GA", [128, 512, NST], BF16))
            GAT = _es4.enter_context(nc.sbuf_tensor("GAT", [128, Q * 128], BF16))
            PCS = [_es4.enter_context(nc.psum_tensor(f"PC{i}", [128, 80], F32))
                   for i in range(4)]
            PGT = _es4.enter_context(nc.psum_tensor("PGT", [128, 128], BF16))
            PGT1 = _es4.enter_context(nc.psum_tensor("PGT1", [128, 128], BF16))
            if True:
                sc = Sched(ST)
                zv = sc.add("v", lambda e: e.memset(ZSB[:], 0.0))
                zga = sc.add("v", lambda e: e.memset(
                    GA.ap().rearrange("p g n -> p (g n)"), 0.0), label="zga")
                zgat = sc.add("v", lambda e: e.memset(GAT[:], 0.0), label="zgat")
                zd = []
                TDTOT = Q * TPAD * CIN
                ZCH = 128 * 2048
                nz = (TDTOT + ZCH - 1) // ZCH
                from concourse.ap import AP as _AP
                for zi in range(nz):
                    lo = zi * ZCH
                    cnt = min(ZCH, TDTOT - lo)
                    zd.append(sc.add("q", lambda e, lo=lo, cnt=cnt: e.dma_start(
                        out=_AP(t_dram.tensor, lo, [[1, cnt]]),
                        in_=ZSB.ap()[:, 0:cnt // 128]), deps=[zv]))
                kx0 = sc.add("q", lambda e: e.dma_start(
                    out=kx_dram[0:1, :, :].rearrange("o j c -> (o j) c"),
                    in_=k_dram), deps=[ev_mats])
                kxl = kx0
                for m in range(6):
                    w = 1 << m
                    kxl = sc.add("q", lambda e, w=w: e.dma_start(
                        out=kx_dram[w:2 * w, :, :].rearrange("s j c -> (s j) c"),
                        in_=kx_dram[0:w, :, :].rearrange("s j c -> (s j) c")),
                        deps=[kxl])
                tb_diag = sc.add("q", lambda e: e.dma_start(
                    out=bass_diag_dst(t_dram), in_=kx_dram),
                    deps=[kxl, ev_mats] + zd)
                from concourse.ap import AP as _APg2
                tb_g = sc.add("q", lambda e: e.dma_start(
                    out=_APg2(t_dram.tensor, 128 * CIN,
                              [[CIN, NST], [TPAD * CIN, Q], [1, CIN]]),
                    in_=_APg2(v_dram.tensor, (Q - 1) * NST * CIN,
                              [[CIN, NST], [-NST * CIN, Q], [1, CIN]])),
                    deps=zd + [ev_mats])
                conv_slices = range(NSL) if csub >= 2 else range(0)
                conv_mm = csub >= 3
                for cs in conv_slices:
                    ldeps = [tb_diag, tb_g, ev_xp] \
                        + ([f"c1ev{cs - 1}_63", f"c1gv{cs - 1}_63"]
                           if (cs > 0 and conv_mm) else [])
                    lt_parts = []
                    for tg8 in range(8):
                        lt_parts.append(sc.add("q", lambda e, cs=cs, tg8=tg8:
                            e.dma_start(
                                out=TWIN.ap()
                                .rearrange("p (tg c) -> p tg c", tg=80)
                                [0:Q, tg8 * 8:(tg8 + 1) * 8, :],
                                in_=t_dram[:, tg8 * 8:(tg8 + 1) * 8,
                                           cs * CSL:(cs + 1) * CSL]),
                            deps=ldeps, label=f"ctl{cs}_{tg8}"))
                    lt2_parts = []
                    for tg2 in range(2):
                        lt2_parts.append(sc.add("q", lambda e, cs=cs, tg2=tg2:
                            e.dma_start(
                                out=TWIN.ap()
                                .rearrange("p (tg c) -> p tg c", tg=80)
                                [0:Q, Q + tg2 * 8:Q + (tg2 + 1) * 8, :],
                                in_=t_dram[:, 128 + tg2 * 8:128 + (tg2 + 1) * 8,
                                           cs * CSL:(cs + 1) * CSL]),
                            deps=ldeps, label=f"ctl2{cs}_{tg2}"))
                    lx_parts = []
                    for bq_ in range(4):
                        lx_parts.append(sc.add("q", lambda e, cs=cs, bq_=bq_:
                            e.dma_start(
                                out=XPW.ap()
                                .rearrange("p (bi c) -> p bi c", bi=NBI)
                                [0:Q, bq_ * 8:(bq_ + 1) * 8, :],
                                in_=xpf_dram.rearrange("s b i c -> s (b i) c")
                                [:, bq_ * 8:(bq_ + 1) * 8,
                                 cs * CSL:(cs + 1) * CSL]),
                            deps=ldeps, label=f"cxl{cs}_{bq_}"))
                    lt = lt_parts
                    lt2 = lt2_parts
                    lx = lx_parts
                    for g4 in (range(64) if conv_mm else range(0)):
                        ps = PCS[g4 % 4]
                        g512 = cs * 64 + g4
                        mms = []
                        for j in range(4):
                            cl = g4 * 4 + j
                            dep = [lt, lt2, lx]
                            if g4 >= 4:
                                dep += [f"c1ev{cs}_{g4 - 4}", f"c1gv{cs}_{g4 - 4}"]
                            mms.append(sc.add("t", lambda e, cl=cl, j=j, ps=ps:
                                e.matmul(
                                    ps.ap()[32 * j:32 * (j + 1), :],
                                    XPW.ap().rearrange("p (bi c) -> p bi c",
                                                       bi=NBI)[0:Q, :, cl],
                                    TWIN.ap().rearrange("p (tg c) -> p tg c",
                                                        tg=80)[0:Q, :, cl],
                                    start=True, stop=True,
                                    tile_position=(0, 32 * j)),
                                deps=dep, label=f"c1m{cs}_{g4}_{j}"))
                        sc.add("s", lambda e, g4=g4, ps=ps: e.copy(
                            out=YLT.ap()[:, g4 % 8, :], in_=ps.ap()[:, 0:Q]),
                            deps=mms + ([f"yld{cs}_{g4 // 8 - 1}"]
                                        if g4 >= 8 else
                                        ([f"yld{cs - 1}_7"] if cs > 0 else [])),
                            label=f"c1ev{cs}_{g4}")
                        sc.add("s", lambda e, g512=g512, ps=ps: e.copy(
                            out=GA.ap()[:, g512, :], in_=ps.ap()[:, Q:80]),
                            deps=mms + ["zga"], label=f"c1gv{cs}_{g4}")
                        if g4 % 8 == 7:
                            sc.add("q", lambda e, cs=cs, g4=g4: e.dma_start(
                                out=yl_dram[:, cs * 64 + g4 - 7:cs * 64 + g4 + 1, :],
                                in_=YLT.ap()),
                                deps=[f"c1ev{cs}_{gg}" for gg in
                                      range(g4 - 7, g4 + 1)],
                                label=f"yld{cs}_{g4 // 8}")
                if csub < 3:
                    ga_tr = None
                ga_tr = []
                ga_range = range(Q) if csub >= 4 else range(0)
                for gb_ in ga_range:
                    pg = PGT if gb_ % 2 == 0 else PGT1
                    tr = sc.add("t", lambda e, gb_=gb_, pg=pg: e.transpose(
                        pg.ap(),
                        GA.ap().rearrange("p g n -> p (g n)")
                        [:, gb_ * 128:(gb_ + 1) * 128], IDB[:]),
                        deps=[f"c1gv{(gb_ * 8) // 64}_{(gb_ * 8) % 64 + 7}",
                              (f"gatv{gb_ - 2}" if gb_ >= 2 else None)],
                        label=f"gatt{gb_}")
                    ga_tr.append(sc.add("v", lambda e, gb_=gb_, pg=pg: e.tensor_copy(
                        out=GAT.ap()[:, gb_ * 128:(gb_ + 1) * 128], in_=pg.ap()),
                        deps=[tr, "zgat"], label=f"gatv{gb_}"))
                ga_send = []
                from concourse.ap import AP as _APg
                for gs_ in (range(8) if csub >= 4 else range(0)):
                    ga_send.append(sc.add("q", lambda e, gs_=gs_: e.dma_start(
                        out=_APg(ga_in.tensor, gs_ * 1024,
                                 [[8192, NST], [A2A, NCORE], [128, 8], [1, 128]]),
                        in_=GAT.ap()[gs_ * 16:(gs_ + 1) * 16, :]
                        .rearrange("p (q gbl f) -> p q gbl f", q=NCORE, gbl=8)),
                        deps=ga_tr, label=f"gas{gs_}"))
                a2a1 = None
                if csub >= 4:
                    a2a1 = sc.add("c", lambda e: e.collective_compute(
                        "AllGather", ALU.bypass, replica_groups=RG8,
                        ins=[ga_in.rearrange("r f -> () (r f)")],
                        outs=[ga_all]), deps=ga_send)
                from concourse.ap import AP as _AP2

                def slice_my(e, src_all, dst):
                    qv = e.value_load(in_tensor=CIDT.ap()[0:1, 0:1],
                                      min_val=0, max_val=NCORE - 1)
                    return e.dma_start(
                        out=dst,
                        in_=_AP2(src_all.tensor, qv * A2A,
                                 [[NCORE * A2A, NCORE], [1, A2A]]))
                a2a1s = sc.add("q", lambda e: slice_my(e, ga_all, ga_out),
                               deps=[a2a1], label="ga_slice")
                evC = sc.emit(blk, ext_deps=prev)
                _es4.close()
            prev = [e for v in evC.values() for e in v]

        ev_a2a1 = evC["c"] if ("C" in phases and csub >= 4) else None

        if "D" in phases:
            _es5 = ExitStack()
            GSH = _es5.enter_context(nc.sbuf_tensor("GSH", [128, NCORE * 1024], BF16))
            SHISTL = _es5.enter_context(nc.sbuf_tensor("SHISTL", [128, 4, 9, NBI], BF16))
            HSUPHL = _es5.enter_context(nc.sbuf_tensor("HSUPHL", [128, 4, 9, B], BF16))
            TCCL = _es5.enter_context(nc.sbuf_tensor("TCCL", [128, 4, NBI], BF16))
            HINS = _es5.enter_context(nc.sbuf_tensor("HINS", [128, NCORE * 1024], BF16))
            PDS = [_es5.enter_context(nc.psum_tensor(f"PD{i}", [128, NBI], F32))
                   for i in range(8)]
            if True:
                sc = Sched(ST)
                zhins = sc.add("v", lambda e: e.memset(HINS[:], 0.0),
                               label="zhins")
                gsh_l = []
                from concourse.ap import AP as _APh
                for gs_ in range(8):
                    gsh_l.append(sc.add("q", lambda e, gs_=gs_: e.dma_start(
                        out=GSH.ap()[gs_ * 16:(gs_ + 1) * 16, :]
                        .rearrange("p (r f) -> p r f", r=NCORE),
                        in_=_APh(ga_out.tensor, gs_ * 1024,
                                 [[8192, NST], [A2A, NCORE], [1, 1024]])),
                        deps=[], label=f"gshl{gs_}"))
                ha_sends = []

                def d_group_ctx(g_):
                    ln_ = g_ % 4
                    return dict(
                        g_=g_, Pn=f"d{g_}", gb=g_ // 4, c4=g_ % 4,
                        SHIST=SHISTL.ap()[:, ln_], HSUPH=HSUPHL.ap()[:, ln_],
                        TCC=TCCL.ap()[:, ln_],
                        PD0=PDS[ln_ * 2], PD1=PDS[ln_ * 2 + 1],
                        A64g=BD64.ap()[:, g_ * 128:(g_ + 1) * 128],
                        A512g=BD512.ap()[:, g_ * 128:(g_ + 1) * 128])

                def gcols(g_, i):
                    gb_, c4_ = divmod(g_, 4)
                    return GSH.ap().rearrange(
                        "p (r gb c4 b i) -> p r gb c4 b i",
                        r=NCORE, gb=8, c4=4, b=B)[:, :, gb_, c4_, :, i] \
                        .transpose([0, 2, 1])

                for w in range(NG // 4):
                    ctxs = [d_group_ctx(4 * w + ln) for ln in range(4)]
                    for C_ in ctxs:
                        g_, Pn = C_["g_"], C_["Pn"]
                        pre = gsh_l + ([f"d{g_ - 4}done"] if g_ >= 4 else [])
                        sc.add("v", lambda e, S=C_["SHIST"]: e.memset(
                            S[:, 0, :], 0.0), deps=pre, label=Pn + "z0")
                        sc.add("v", lambda e, Hp=C_["HSUPH"]: e.memset(
                            Hp[:, 0, :], 0.0), deps=pre, label=Pn + "zh")
                    for i in range(1, 9):
                        for C_ in ctxs:
                            Pn = C_["Pn"]
                            ps = C_["PD0"] if i % 2 == 0 else C_["PD1"]
                            sc.add("t", lambda e, i=i, ps=ps, A=C_["A64g"],
                                   S=C_["SHIST"]: e.matmul(
                                ps.ap(), A, S[:, i - 1, :],
                                start=True, stop=True),
                                deps=[Pn + (f"av{i - 1}" if i > 1 else "z0")],
                                label=Pn + f"am{i}")
                        for C_ in ctxs:
                            Pn = C_["Pn"]
                            ps = C_["PD0"] if i % 2 == 0 else C_["PD1"]
                            sc.add("v", lambda e, i=i, ps=ps, g_=C_["g_"],
                                   S=C_["SHIST"]: e.tensor_add(
                                S[:, i, :], ps.ap(), gcols(g_, i - 1)),
                                deps=[Pn + f"am{i}"], label=Pn + f"av{i}")
                    for r in range(1, 9):
                        for C_ in ctxs:
                            Pn = C_["Pn"]
                            ps = C_["PD0"] if r % 2 == 0 else C_["PD1"]
                            sc.add("t", lambda e, r=r, ps=ps, A=C_["A512g"],
                                   Hp=C_["HSUPH"]: e.matmul(
                                ps.ap()[:, 0:B], A, Hp[:, r - 1, :],
                                start=True, stop=True),
                                deps=[Pn + (f"bv{r - 1}" if r > 1 else "zh"),
                                      Pn + "av8"],
                                label=Pn + f"bm{r}")
                        for C_ in ctxs:
                            Pn = C_["Pn"]
                            ps = C_["PD0"] if r % 2 == 0 else C_["PD1"]
                            sc.add("v", lambda e, r=r, ps=ps, Hp=C_["HSUPH"],
                                   S=C_["SHIST"]: e.tensor_add(
                                Hp[:, r, :], ps.ap()[:, 0:B],
                                S[:, 8, r - 1::8]),
                                deps=[Pn + f"bm{r}"], label=Pn + f"bv{r}")
                    for C_ in ctxs:
                        Pn = C_["Pn"]
                        sc.add("v", lambda e, T=C_["TCC"], Hp=C_["HSUPH"]:
                            e.tensor_copy(
                                out=T.rearrange("p (b r) -> p b r", b=B),
                                in_=Hp[:, 0:8, :].transpose([0, 2, 1])),
                            deps=[Pn + "bv8"], label=Pn + "tc0")
                    for i in range(8):
                        for C_ in ctxs:
                            Pn = C_["Pn"]
                            sc.add("v", lambda e, i=i, gb=C_["gb"],
                                   c4=C_["c4"], T=C_["TCC"], S=C_["SHIST"]:
                                e.tensor_add(
                                    HINS.ap().rearrange(
                                        "p (r gb c4 b i) -> p r gb c4 b i",
                                        r=NCORE, gb=8, c4=4, b=B)
                                    [:, :, gb, c4, :, i].transpose([0, 2, 1]),
                                    T.rearrange("p (b r) -> p b r", b=B),
                                    S[:, i, :]
                                    .rearrange("p (b r) -> p b r", b=B)),
                                deps=[Pn + (f"cv{i - 1}" if i > 0 else "tc0"),
                                      "zhins"],
                                label=Pn + f"he{i}")
                        if i < 7:
                            for C_ in ctxs:
                                Pn = C_["Pn"]
                                ps = C_["PD0"] if i % 2 == 0 else C_["PD1"]
                                sc.add("t", lambda e, ps=ps, A=C_["A64g"],
                                       T=C_["TCC"]: e.matmul(
                                    ps.ap(), A, T, start=True, stop=True),
                                    deps=[Pn + f"he{i}"], label=Pn + f"cm{i}")
                            for C_ in ctxs:
                                Pn = C_["Pn"]
                                ps = C_["PD0"] if i % 2 == 0 else C_["PD1"]
                                sc.add("v", lambda e, ps=ps, T=C_["TCC"]:
                                    e.tensor_copy(out=T, in_=ps.ap()),
                                    deps=[Pn + f"cm{i}"], label=Pn + f"cv{i}")
                    for C_ in ctxs:
                        Pn, g_ = C_["Pn"], C_["g_"]
                        sc.add("v", lambda e, T=C_["TCC"]: e.tensor_copy(
                            out=T[0:1, 0:1], in_=T[0:1, 0:1]),
                            deps=[Pn + f"he{i_}" for i_ in range(8)],
                            label=f"d{g_}done")
                hdone = [f"d{g_}done" for g_ in range(NG)]
                for gs_ in range(8):
                    ha_sends.append(sc.add("q", lambda e, gs_=gs_: e.dma_start(
                        out=_APh(ha_in.tensor, gs_ * 1024,
                                 [[8192, NST], [A2A, NCORE], [1, 1024]]),
                        in_=HINS.ap()[gs_ * 16:(gs_ + 1) * 16, :]
                        .rearrange("p (r f) -> p r f", r=NCORE)),
                        deps=hdone, label=f"has{gs_}"))
                a2a2 = sc.add("c", lambda e: e.collective_compute(
                    "AllGather", ALU.bypass, replica_groups=RG8,
                    ins=[ha_in.rearrange("r f -> () (r f)")], outs=[ha_all]),
                    deps=ha_sends)
                from concourse.ap import AP as _AP3

                def slice_my2(e):
                    qv = e.value_load(in_tensor=CIDT.ap()[0:1, 0:1],
                                      min_val=0, max_val=NCORE - 1)
                    return e.dma_start(
                        out=ha_out,
                        in_=_AP3(ha_all.tensor, qv * A2A,
                                 [[NCORE * A2A, NCORE], [1, A2A]]))
                a2a2s = sc.add("q", lambda e: slice_my2(e),
                               deps=[a2a2],
                               label="ha_slice")
                evD = sc.emit(blk, ext_deps=prev)
                _es5.close()
            _esBD.close()
            prev = [e for v in evD.values() for e in v]

        ev_a2a2 = evD["c"] if "D" in phases else None

        if "E" in phases:
            _es6 = ExitStack()
            HF = _es6.enter_context(nc.sbuf_tensor("HF", [128, 8192], BF16))
            UWIN = _es6.enter_context(nc.sbuf_tensor("UWIN", [128, Q * CSL], BF16))
            YLS0 = _es6.enter_context(nc.sbuf_tensor("YLS0", [128, 64, Q], BF16))
            YLS1 = _es6.enter_context(nc.sbuf_tensor("YLS1", [128, 64, Q], BF16))
            CARR = _es6.enter_context(nc.sbuf_tensor("CARR", [128, 4, Q], F32))
            PES = [_es6.enter_context(nc.psum_tensor(f"PE{i}", [128, Q], F32))
                   for i in range(4)]
            if True:
                from concourse.ap import AP as _AP
                sc = Sched(ST)
                for cs in range(NSL):
                    hfl = sc.add("q", lambda e, cs=cs: e.dma_start(
                        out=HF.ap()[64:64 + NST, :],
                        in_=ha_out[cs:cs + 1, :]
                        .rearrange("o (n f) -> (o n) f", n=NST)),
                        deps=[(f"c2ev{cs - 1}_63" if cs > 0 else None)],
                        label=f"hfl{cs}")
                    YLS = YLS0 if cs % 2 == 0 else YLS1
                    ul = sc.add("q", lambda e, cs=cs: e.dma_start(
                        out=UWIN.ap().rearrange("p (t c) -> p t c", t=Q)
                        [64:64 + NST, :, :],
                        in_=u_dram[:, :, cs * CSL:(cs + 1) * CSL]
                        .transpose([1, 0, 2])),
                        deps=[(f"c2ev{cs - 1}_63" if cs > 0 else None)],
                        label=f"ul{cs}")
                    yll = sc.add("q", lambda e, cs=cs, YLS=YLS: e.dma_start(
                        out=YLS.ap(), in_=yl_dram[:, cs * 64:(cs + 1) * 64, :]),
                        deps=[(f"yrf{2 * (cs - 2) + 1}" if cs >= 2 else None)],
                        label=f"yll{cs}")
                    for g4 in range(64):
                        ps = PES[g4 % 4]
                        mms = []
                        for j in range(4):
                            c = cs * CSL + g4 * 4 + j
                            rr = c % CSH
                            gbv, r2 = divmod(rr, 32)
                            gsv, c4v = divmod(r2, 4)
                            hoff = ((gsv * 8 + gbv) * 4 + c4v) * 32
                            dep = [hfl, ul, yll]
                            if g4 >= 4:
                                dep += [f"c2sc{cs}_{g4 - 4}"]
                            mms.append(sc.add("t", lambda e, j=j, ps=ps,
                                              hoff=hoff, cl=g4 * 4 + j: e.matmul(
                                ps.ap()[32 * j:32 * (j + 1), :],
                                HF.ap().rearrange("p (f b i) -> p f b i",
                                                  b=B, i=NCH)
                                [64:64 + NST, hoff // 32, :, :],
                                UWIN.ap().rearrange("p (t c) -> p t c", t=Q)
                                [64:64 + NST, :, cl],
                                start=True, stop=True,
                                tile_position=(64, 32 * j)),
                                deps=dep, label=f"c2m{cs}_{g4}_{j}"))
                        sc.add("s", lambda e, g4=g4, ps=ps: e.copy(
                            out=CARR.ap()[:, g4 % 4, :], in_=ps.ap()),
                            deps=mms + ([f"c2ev{cs}_{g4 - 4}"] if g4 >= 4 else
                                        ([f"c2ev{cs - 1}_{g4 + 60}"]
                                         if cs > 0 else [])),
                            label=f"c2sc{cs}_{g4}")
                        sc.add("v", lambda e, g4=g4, YLS=YLS: e.tensor_add(
                            YLS.ap()[:, g4, :], CARR.ap()[:, g4 % 4, :],
                            YLS.ap()[:, g4, :]),
                            deps=[f"c2sc{cs}_{g4}"], label=f"c2ev{cs}_{g4}")
                    for h2 in range(2):
                        ct = cs * 2 + h2
                        FR = 64 * Q
                        for c4 in range(4):
                            sc.add("q", lambda e, ct=ct, h2=h2, YLS=YLS, FR=FR,
                                   c4=c4: e.dma_start(
                                out=_AP(y_dram.tensor, ct * 128 * T + c4 * T,
                                        [[Q, 32], [4 * T, 32], [1, Q]]),
                                in_=_AP(YLS, c4 * 32 * FR + h2 * 32 * Q,
                                        [[FR, 32], [Q, 32], [1, Q]])),
                                deps=[f"c2ev{cs}_{32 * h2 + gg}"
                                      for gg in range(32)],
                                label=f"yrf{ct}_{c4}")
                        sc.add("v", lambda e, ct=ct: e.tensor_copy(
                            out=JNK.ap()[0:1, ct:ct + 1],
                            in_=JNK.ap()[0:1, ct + 16:ct + 17]),
                            deps=[f"yrf{ct}_{cc}" for cc in range(4)],
                            label=f"yrf{ct}")
                evE = sc.emit(blk, ext_deps=prev)
                _es6.close()
            prev = [e for v in evE.values() for e in v]

        ev_yrf = evE["q2"] if "E" in phases else None

        if "F" in phases:
            _es7 = ExitStack()
            YGM = _es7.enter_context(nc.sbuf_tensor("YGM", [128, CT, T], BF16))
            GSL = _es7.enter_context(nc.sbuf_tensor("GSL", [128, T], BF16))
            WOUT = _es7.enter_context(nc.sbuf_tensor("WOUT", [128, CT, D], BF16))
            PF0 = _es7.enter_context(nc.psum_tensor("PF0", [128, 512], F32))
            PF1 = _es7.enter_context(nc.psum_tensor("PF1", [128, 512], F32))
            PF2 = _es7.enter_context(nc.psum_tensor("PF2", [128, 512], F32))
            PF3 = _es7.enter_context(nc.psum_tensor("PF3", [128, 512], F32))
            PF4 = _es7.enter_context(nc.psum_tensor("PF4", [128, 512], F32))
            PF5 = _es7.enter_context(nc.psum_tensor("PF5", [128, 512], F32))
            PF6 = _es7.enter_context(nc.psum_tensor("PF6", [128, 512], F32))
            PF7 = _es7.enter_context(nc.psum_tensor("PF7", [128, 512], F32))
            if True:
                PFS = [PF0, PF1, PF2, PF3, PF4, PF5, PF6, PF7]
                sc = Sched(ST)
                lwo = sc.add("g", lambda e: e.dma_start(
                    out=WOUT.ap(),
                    in_=w_out.rearrange("(a p) d -> p a d", p=128)),
                    label="lwo")
                for ct in range(CT):
                    ly = sc.add("q", lambda e, ct=ct: e.dma_start(
                        out=YGM.ap()[:, ct, :], in_=y_dram[ct, :, :]),
                        deps=[], label=f"ly{ct}")
                    lgt = sc.add("q", lambda e, ct=ct: e.dma_start(
                        out=GSL.ap(), in_=gate_dram[ct, :, :]),
                        deps=[f"ygm{ct - 1}" if ct > 0 else None],
                        label=f"lg{ct}")
                    sc.add("v", lambda e, ct=ct: e.tensor_mul(
                        YGM.ap()[:, ct, :], YGM.ap()[:, ct, :], GSL.ap()),
                        deps=[ly, lgt], label=f"ygm{ct}")
                for tc in range(4):
                    for dt_ in range(DT):
                        ps = PFS[dt_]
                        sc.add("t", lambda e, dt_=dt_, ps=ps: e.matmul(
                            ps.ap(), brow("b_out", dt_ * 128, 128),
                            ONESR.ap()[0:1, :], start=True, stop=False),
                            deps=[lwo, (f"rup{tc - 1}_{dt_}" if tc > 0 else None)] +
                                 ([f"ygm{ct}" for ct in range(CT)] if dt_ == 0
                                  else []),
                            label=f"wob{tc}_{dt_}")
                        for ct in range(CT):
                            sc.add("t", lambda e, ct=ct, dt_=dt_, tc=tc, ps=ps:
                                e.matmul(
                                    ps.ap(),
                                    WOUT.ap()[:, ct, dt_ * 128:(dt_ + 1) * 128],
                                    YGM.ap()[:, ct, tc * 512:(tc + 1) * 512],
                                    start=False, stop=(ct == CT - 1)),
                                deps=[f"wob{tc}_{dt_}"],
                                label=f"wom{tc}_{dt_}_{ct}")
                        sc.add("v", lambda e, tc=tc, dt_=dt_, ps=ps: e.tensor_add(
                            R.ap()[:, dt_, tc * 512:(tc + 1) * 512], ps.ap(),
                            R.ap()[:, dt_, tc * 512:(tc + 1) * 512]),
                            deps=[f"wom{tc}_{dt_}_{CT - 1}"],
                            label=f"rup{tc}_{dt_}")
                r1_done = [f"rup{tc}_{dt_}" for tc in range(4)
                           for dt_ in range(DT)]
                if dbg == "R1":
                    dbg_dump(sc, lambda: R.ap().rearrange("p a b -> p (a b)"),
                             r1_done)
                evF = sc.emit(blk, ext_deps=prev)
                _es7.close()
            prev = [e for v in evF.values() for e in v]

        if "G" in phases:
            _esG0 = ExitStack()
            XLN2 = _esG0.enter_context(nc.sbuf_tensor("XLN2", [128, DT, T], BF16))
            SQH2 = _esG0.enter_context(nc.sbuf_tensor("SQH2", [128, DT, 512], BF16))
            ROWS2 = _esG0.enter_context(nc.sbuf_tensor("ROWS2", [1, 3, T], F32))
            LNA2 = _esG0.enter_context(nc.sbuf_tensor("LNA2", [128, 2, 512], F32))
            LNB2 = _esG0.enter_context(nc.sbuf_tensor("LNB2", [128, 2, 512], F32))
            PGS0 = _esG0.enter_context(nc.psum_tensor("PGS0", [1, 512], F32))
            PGS1 = _esG0.enter_context(nc.psum_tensor("PGS1", [1, 512], F32))
            PGQ0 = _esG0.enter_context(nc.psum_tensor("PGQ0", [1, 512], F32))
            PGQ1 = _esG0.enter_context(nc.psum_tensor("PGQ1", [1, 512], F32))
            PGB0 = _esG0.enter_context(nc.psum_tensor("PGB0", [128, 512], F32))
            PGB1 = _esG0.enter_context(nc.psum_tensor("PGB1", [128, 512], F32))
            if True:
                sc = Sched(ST)
                emit_ln(sc, R, XLN2, SQH2, ROWS2, LNA2, LNB2, PGS0, PGS1,
                        PGQ0, PGQ1, PGB0, PGB1, ONESCF, ONESCB, ONESRF, "ln1")
                for tt in range(TT):
                    sc.add("q", lambda e, tt=tt: e.dma_start(
                        out=xln2_dram[tt].rearrange("p (a b) -> p a b", a=DT),
                        in_=XLN2.ap()[:, :, tt * 128:(tt + 1) * 128]),
                        deps=[f"ln1_ap{(tt // 4) * DT + k}" for k in range(DT)],
                        label=f"x2s{tt}")
                evG0 = sc.emit(blk, ext_deps=prev)
                _esG0.close()
            prev = [e for v in evG0.values() for e in v]

            _esG23 = ExitStack()
            IDXS = _esG23.enter_context(nc.sbuf_tensor("IDXS", [128, TT, PKH], U32))
            GATES = _esG23.enter_context(nc.sbuf_tensor("GATES", [128, TT, PKH], F32))
            LINS = _esG23.enter_context(nc.sbuf_tensor("LINS", [128, TT, PKH], F32))
            _esG12 = ExitStack()
            WKT = _esG12.enter_context(
                nc.sbuf_tensor("WKT", [128, DT, 16, PKK], BF16))

            _esG1 = ExitStack()
            KSTS = _esG1.enter_context(nc.sbuf_tensor("KSTS", [128, 2, 4, 128], BF16))
            WQS = _esG1.enter_context(nc.sbuf_tensor("WQS", [128, 2, 4, D], BF16))
            WKN = _esG1.enter_context(nc.sbuf_tensor("WKN", [128, 2, D], BF16))
            PWK = _esG1.enter_context(nc.psum_tensor("PWK", [128, 2, 512], F32))
            PTK0 = _esG1.enter_context(nc.psum_tensor("PTK0", [128, 128], BF16))
            PTK1 = _esG1.enter_context(nc.psum_tensor("PTK1", [128, 128], BF16))
            if True:
                sc = Sched(ST)
                for ph in range(16):
                    pp, hh = divmod(ph, PKH)
                    pb = ph % 2
                    sdep = [f"wkm{ph - 2}_3_1"] if ph >= 2 else []
                    lk = sc.add("g", lambda e, pp=pp, hh=hh, pb=pb: e.dma_start(
                        out=KSTS.ap()[:, pb, :, :],
                        in_=pk_keysT[pp, hh]
                        .rearrange("(a p) k -> p a k", p=128)),
                        deps=sdep, label=f"wklk{ph}")
                    lw = []
                    for pch in range(4):
                        lw.append(sc.add("g", lambda e, ph=ph, pch=pch, pb=pb:
                            e.dma_start(
                                out=WQS.ap()[:, pb, pch, :],
                                in_=pk_wq[ph * 512 + pch * 128:
                                          ph * 512 + (pch + 1) * 128, :]),
                            deps=sdep, label=f"wklw{ph}_{pch}"))
                    for pch in range(4):
                        for nch in range(2):
                            dep = [lk, lw[pch]]
                            if pch == 0 and ph >= 1:
                                dep += [f"wknv{ph - 1}_0", f"wknv{ph - 1}_1"]
                            sc.add("t", lambda e, pch=pch, nch=nch, pb=pb:
                                e.matmul(
                                    PWK.ap()[:, nch, :],
                                    KSTS.ap()[:, pb, pch, :],
                                    WQS.ap()[:, pb, pch,
                                             nch * 512:(nch + 1) * 512],
                                    start=(pch == 0), stop=(pch == 3)),
                                deps=dep, label=f"wkm{ph}_{pch}_{nch}")
                    for nch in range(2):
                        sc.add("v", lambda e, pb=pb, nch=nch: e.tensor_copy(
                            out=WKN.ap()[:, pb, nch * 512:(nch + 1) * 512],
                            in_=PWK.ap()[:, nch, :]),
                            deps=[f"wkm{ph}_3_{nch}"] +
                                 ([f"wktt{ph - 2}_{4 * nch + 3}"]
                                  if ph >= 2 else []),
                            label=f"wknv{ph}_{nch}")
                    for dch in range(DT):
                        pt = PTK0 if dch % 2 == 0 else PTK1
                        pdep = []
                        if dch >= 2:
                            pdep = [f"wkte{ph}_{dch - 2}"]
                        elif ph >= 1:
                            pdep = [f"wkte{ph - 1}_{6 + dch}"]
                        tr = sc.add("t", lambda e, pt=pt, pb=pb, dch=dch:
                            e.transpose(
                                pt.ap(),
                                WKN.ap()[:, pb, dch * 128:(dch + 1) * 128],
                                IDB[:]),
                            deps=[f"wknv{ph}_{dch // 4}"] + pdep,
                            label=f"wktt{ph}_{dch}")
                        sc.add("s", lambda e, pt=pt, dch=dch, ph=ph: e.copy(
                            out=WKT.ap()[:, dch, ph, :], in_=pt.ap()),
                            deps=[tr], label=f"wkte{ph}_{dch}")
                evG1 = sc.emit(blk, ext_deps=prev)
                _esG1.close()
            prev = [e for v in evG1.values() for e in v]

            _esG2 = ExitStack()
            XL = _esG2.enter_context(nc.sbuf_tensor("XL", [128, 2, DT, 128], BF16))
            SIMB = _esG2.enter_context(nc.sbuf_tensor("SIMB", [128, 2048], F32))
            X2TT = _esG2.enter_context(nc.sbuf_tensor("X2TT", [128, 2, D], BF16))
            AINH = _esG2.enter_context(nc.sbuf_tensor("AINH", [128, 2, D], BF16))
            TMPD = _esG2.enter_context(nc.sbuf_tensor("TMPD", [128, D], BF16))
            M8ALL = _esG2.enter_context(nc.sbuf_tensor("M8ALL", [128, 16, 8], F32))
            MIALL = _esG2.enter_context(nc.sbuf_tensor("MIALL", [128, 16, 8], U32))
            MIFALL = _esG2.enter_context(nc.sbuf_tensor("MIFALL", [128, 16], F32))
            SCALL = _esG2.enter_context(nc.sbuf_tensor("SCALL", [128, 8], F32))
            IDXF8 = _esG2.enter_context(nc.sbuf_tensor("IDXF8", [128, 8], F32))
            DOTA = _esG2.enter_context(nc.sbuf_tensor("DOTA", [128, 1], F32))
            IDXF = _esG2.enter_context(nc.sbuf_tensor("IDXF", [128, 1], F32))
            PSIM = _esG2.enter_context(nc.psum_tensor("PSIM", [128, 4, 512], F32))
            PXT0 = _esG2.enter_context(nc.psum_tensor("PXT0", [128, 128], BF16))
            PXT1 = _esG2.enter_context(nc.psum_tensor("PXT1", [128, 128], BF16))
            if True:
                sc = Sched(ST)
                for tt in range(TT):
                    sl = tt % 2
                    lx = sc.add("q", lambda e, tt=tt, sl=sl: e.dma_start(
                        out=XL.ap()[:, sl, :, :].rearrange("p a b -> p (a b)"),
                        in_=xln2_dram[tt]),
                        deps=([f"simm{tt - 2}_7_3", f"x2tt{tt - 2}_7"]
                              if tt >= 2 else []),
                        label=f"xl{tt}")
                    for dch in range(DT):
                        for q4 in range(4):
                            dep = [lx]
                            if dch == 0 and tt >= 1:
                                dep.append(f"sime{tt - 1}")
                            sc.add("t", lambda e, dch=dch, q4=q4, sl=sl:
                                e.matmul(
                                    PSIM.ap()[:, q4, :],
                                    XL.ap()[:, sl, dch, :],
                                    WKT.ap()[:, dch, 4 * q4:4 * (q4 + 1), :]
                                    .rearrange("p a b -> p (a b)"),
                                    start=(dch == 0), stop=(dch == DT - 1)),
                                deps=dep, label=f"simm{tt}_{dch}_{q4}")
                    dep = [f"simm{tt}_{DT - 1}_{q4}" for q4 in range(4)]
                    if tt >= 1:
                        dep.append(f"mif{tt - 1}")
                    sc.add("v", lambda e: e.tensor_copy(
                        out=SIMB.ap(),
                        in_=PSIM.ap().rearrange("p a b -> p (a b)")),
                        deps=dep, label=f"sime{tt}")
                    for dch in range(DT):
                        px = PXT0 if dch % 2 == 0 else PXT1
                        pdep = []
                        if dch >= 2:
                            pdep = [f"x2t{tt}_{dch - 2}"]
                        elif tt >= 1:
                            pdep = [f"x2t{tt - 1}_{6 + dch}"]
                        tr = sc.add("t", lambda e, px=px, sl=sl, dch=dch:
                            e.transpose(px.ap(), XL.ap()[:, sl, dch, :],
                                        IDB[:]),
                            deps=[lx] + pdep, label=f"x2tt{tt}_{dch}")
                        vdep = [tr]
                        if tt >= 2:
                            vdep.append(f"dota{tt - 2}_7")
                        sc.add("v", lambda e, px=px, sl=sl, dch=dch:
                            e.tensor_copy(
                                out=X2TT.ap()[:, sl, dch * 128:(dch + 1) * 128],
                                in_=px.ap()),
                            deps=vdep, label=f"x2t{tt}_{dch}")
                    topdep0 = ([f"gate{tt - 1}", f"idxu{tt - 1}"]
                               if tt >= 1 else [])
                    for h16 in range(16):
                        mx = sc.add("v", lambda e, h16=h16: e.max(
                            out=M8ALL.ap()[:, h16, :],
                            in_=SIMB.ap()[:, h16 * 128:(h16 + 1) * 128]),
                            deps=[f"sime{tt}"] +
                                 (topdep0 if h16 == 0 else []),
                            label=f"mxa{tt}_{h16}")
                        sc.add("v", lambda e, h16=h16: e.max_index(
                            out=MIALL.ap()[:, h16, :],
                            in_max=M8ALL.ap()[:, h16, :],
                            in_values=SIMB.ap()[:, h16 * 128:(h16 + 1) * 128]),
                            deps=[mx], label=f"mia{tt}_{h16}")
                    cast = sc.add("v", lambda e: e.tensor_copy(
                        out=MIFALL.ap(), in_=MIALL.ap()[:, :, 0]),
                        deps=[f"mia{tt}_{h_}" for h_ in range(16)],
                        label=f"mif{tt}")
                    fi = sc.add("v", lambda e: e.scalar_tensor_tensor(
                        out=IDXF8.ap(), in0=MIFALL.ap()[:, 0:8], scalar=128.0,
                        in1=MIFALL.ap()[:, 8:16], op0=ALU.mult, op1=ALU.add),
                        deps=[cast], label=f"idxf{tt}")
                    sc.add("v", lambda e, tt=tt: e.tensor_copy(
                        out=IDXS.ap()[:, tt, :], in_=IDXF8.ap()),
                        deps=[fi] + ([f"gain{tt - 2}_7"] if tt >= 2 else []),
                        label=f"idxu{tt}")
                    scca = sc.add("v", lambda e: e.tensor_add(
                        SCALL.ap(), M8ALL.ap()[:, 0:8, 0],
                        M8ALL.ap()[:, 8:16, 0]),
                        deps=[f"mxa{tt}_{h_}" for h_ in range(16)] + [fi],
                        label=f"scc{tt}")
                    sc.add("v", lambda e, tt=tt: e.tensor_scalar(
                        out=GATES.ap()[:, tt, :], in0=SCALL.ap(),
                        scalar1=0.0, scalar2=None, op0=ALU.max),
                        deps=[scca], label=f"gate{tt}")
                    for h in range(PKH):
                        ga = sc.add("g", lambda e, tt=tt, h=h:
                            e.indirect_dma_start(
                                out=AINH.ap()[:, h % 2, :], out_offset=None,
                                in_=pk_in_a,
                                in_offset=IndirectOffsetOnAxis(
                                    ap=IDXS.ap()[:, tt, h:h + 1], axis=0)),
                            deps=[f"idxu{tt}",
                                  (f"dota{tt}_{h - 2}" if h >= 2 else
                                   (f"dota{tt - 1}_{6 + h}" if tt >= 1
                                    else None))],
                            label=f"gain{tt}_{h}")
                        dt_ = sc.add("v", lambda e, h=h, sl=sl:
                            e.scalar_tensor_tensor(
                                out=TMPD.ap(), in0=AINH.ap()[:, h % 2, :],
                                scalar=1.0, in1=X2TT.ap()[:, sl, :],
                                op0=ALU.mult, op1=ALU.mult,
                                accum_out=DOTA.ap()),
                            deps=[ga, f"gate{tt}",
                                  (f"lin{tt}_{h - 1}" if h > 0 else
                                   (f"lin{tt - 1}_7" if tt >= 1 else None))]
                                 + [f"x2t{tt}_{dc}" for dc in range(DT)],
                            label=f"dota{tt}_{h}")
                        sc.add("v", lambda e, tt=tt, h=h: e.tensor_mul(
                            LINS.ap()[:, tt, h:h + 1], DOTA.ap(),
                            GATES.ap()[:, tt, h:h + 1]),
                            deps=[dt_],
                            label=f"lin{tt}_{h}")
                evG2 = sc.emit(blk, ext_deps=prev)
                _esG2.close()
                _esG12.close()
            prev = [e for v in evG2.values() for e in v]

            _esG3 = ExitStack()
            WINT = _esG3.enter_context(nc.sbuf_tensor("WINT", [128, DT, DIN0], BF16))
            WOTS = _esG3.enter_context(
                nc.sbuf_tensor("WOTS", [128, 2, DIN0 // 128, 128], BF16))
            HIDTM = _esG3.enter_context(nc.sbuf_tensor("HIDTM", [128, 4, DIN0], BF16))
            HIT = _esG3.enter_context(nc.sbuf_tensor("HIT", [128, DIN0 // 128, 512], BF16))
            OPK = _esG3.enter_context(nc.sbuf_tensor("OPK", [128, 4, D], BF16))
            BINH = _esG3.enter_context(nc.sbuf_tensor("BINH", [128, 2, DIN0], BF16))
            AOUTH = _esG3.enter_context(nc.sbuf_tensor("AOUTH", [128, 2, DIN0], BF16))
            BOUTH = _esG3.enter_context(nc.sbuf_tensor("BOUTH", [128, 2, D], BF16))
            XLG = _esG3.enter_context(nc.sbuf_tensor("XLG", [128, 2, DT, 128], BF16))
            TMP3 = _esG3.enter_context(nc.sbuf_tensor("TMP3", [128, DIN0], BF16))
            DOTO = _esG3.enter_context(nc.sbuf_tensor("DOTO", [128, 1], F32))
            LOUTH = _esG3.enter_context(nc.sbuf_tensor("LOUTH", [128, PKH], F32))
            HIDP = _esG3.enter_context(nc.psum_tensor("HIDP", [128, 4, 512], F32))
            POUT0 = _esG3.enter_context(nc.psum_tensor("POUT0", [128, 512], F32))
            POUT1 = _esG3.enter_context(nc.psum_tensor("POUT1", [128, 512], F32))
            PTR0 = _esG3.enter_context(nc.psum_tensor("PTR0", [128, 128], BF16))
            PTR1 = _esG3.enter_context(nc.psum_tensor("PTR1", [128, 128], BF16))
            if True:
                sc = Sched(ST)
                lwin = []
                for wh in range(2):
                    lwin.append(sc.add("g", lambda e, wh=wh: e.dma_start(
                        out=WINT.ap()[:, wh * 4:(wh + 1) * 4, :],
                        in_=pk_w_inT[:, wh * 4:(wh + 1) * 4, :]),
                        label=f"lwin{wh}"))
                NDI = DIN0 // 128
                trseq = [0]
                for qt in range(4):
                    tts = list(range(4 * qt, 4 * qt + 4))
                    for ti, tt in enumerate(tts):
                        lx3 = sc.add("q", lambda e, tt=tt: e.dma_start(
                            out=XLG.ap()[:, tt % 2, :, :]
                            .rearrange("p a b -> p (a b)"),
                            in_=xln2_dram[tt]),
                            deps=([f"wim{tt - 2}_{DT - 1}_3"] if tt >= 2
                                  else []),
                            label=f"xlg{tt}")
                        for pch in range(DT):
                            for nch in range(4):
                                dep = [lx3, lwin[pch // 4]]
                                if pch == 0 and tt >= 1:
                                    dep.append(f"gel{tt - 1}")
                                sc.add("t", lambda e, pch=pch, nch=nch, tt=tt:
                                    e.matmul(
                                        HIDP.ap()[:, nch, :],
                                        XLG.ap()[:, tt % 2, pch, :],
                                        WINT.ap()[:, pch,
                                                  nch * 512:(nch + 1) * 512],
                                        start=(pch == 0), stop=(pch == DT - 1)),
                                    deps=dep, label=f"wim{tt}_{pch}_{nch}")
                        for h in range(PKH):
                            gd = [None]
                            if h >= 2:
                                gd = [f"sct{tt}_{h - 2}"]
                            elif tt >= 1:
                                gd = [f"sct{tt - 1}_{6 + h}"]
                            sc.add("g", lambda e, tt=tt, h=h:
                                e.indirect_dma_start(
                                    out=BINH.ap()[:, h % 2, :],
                                    out_offset=None,
                                    in_=pk_in_b,
                                    in_offset=IndirectOffsetOnAxis(
                                        ap=IDXS.ap()[:, tt, h:h + 1],
                                        axis=0)),
                                deps=gd, label=f"gbin{tt}_{h}")
                        for h in range(PKH):
                            dep = [f"wim{tt}_{DT - 1}_3", f"gbin{tt}_{h}"]
                            if h > 0:
                                dep.append(f"sct{tt}_{h - 1}")
                            sc.add("v", lambda e, tt=tt, h=h:
                                e.scalar_tensor_tensor(
                                    out=HIDP.ap()
                                    .rearrange("p a b -> p (a b)"),
                                    in0=BINH.ap()[:, h % 2, :],
                                    scalar=LINS.ap()[:, tt, h:h + 1],
                                    in1=HIDP.ap()
                                    .rearrange("p a b -> p (a b)"),
                                    op0=ALU.mult, op1=ALU.add),
                                deps=dep, label=f"sct{tt}_{h}")
                        gdep = [f"sct{tt}_7"]
                        if qt > 0:
                            gdep.append(f"qjoin{qt - 1}")
                        sc.add("s", lambda e, ti=ti: e.activation(
                            out=HIDTM.ap()[:, ti, :],
                            in_=HIDP.ap().rearrange("p a b -> p (a b)"),
                            func=AF.Gelu),
                            deps=gdep, label=f"gel{tt}")
                        for h in range(PKH):
                            gd = [None]
                            if h >= 2:
                                gd = [f"doto{tt}_{h - 2}"]
                            elif ti >= 1:
                                gd = [f"doto{tts[ti - 1]}_{6 + h}"]
                            elif qt > 0:
                                gd = [f"doto{4 * qt - 1}_{6 + h}"]
                            sc.add("g", lambda e, tt=tt, h=h:
                                e.indirect_dma_start(
                                    out=AOUTH.ap()[:, h % 2, :],
                                    out_offset=None,
                                    in_=pk_out_a,
                                    in_offset=IndirectOffsetOnAxis(
                                        ap=IDXS.ap()[:, tt, h:h + 1],
                                        axis=0)),
                                deps=gd, label=f"gaout{tt}_{h}")
                            gd2 = [None]
                            if h >= 2:
                                gd2 = [f"opk{tt}_{h - 2}"]
                            elif ti >= 1:
                                gd2 = [f"opk{tts[ti - 1]}_{6 + h}"]
                            elif qt > 0:
                                gd2 = [f"opk{4 * qt - 1}_{6 + h}"]
                            sc.add("g", lambda e, tt=tt, h=h:
                                e.indirect_dma_start(
                                    out=BOUTH.ap()[:, h % 2, :],
                                    out_offset=None,
                                    in_=pk_out_b,
                                    in_offset=IndirectOffsetOnAxis(
                                        ap=IDXS.ap()[:, tt, h:h + 1], axis=0)),
                                deps=gd2, label=f"gbout{tt}_{h}")
                            ddep = [f"gaout{tt}_{h}", f"gel{tt}"]
                            if h > 0:
                                ddep.append(f"lout{tt}_{h - 1}")
                            elif ti >= 1:
                                ddep.append(f"lout{tts[ti - 1]}_7")
                            elif qt > 0:
                                ddep.append(f"lout{4 * qt - 1}_7")
                            dt2 = sc.add("v", lambda e, ti=ti, h=h:
                                e.scalar_tensor_tensor(
                                    out=TMP3.ap(),
                                    in0=AOUTH.ap()[:, h % 2, :], scalar=1.0,
                                    in1=HIDTM.ap()[:, ti, :],
                                    op0=ALU.mult, op1=ALU.mult,
                                    accum_out=DOTO.ap()),
                                deps=ddep, label=f"doto{tt}_{h}")
                            lo = sc.add("v", lambda e, tt=tt, h=h: e.tensor_mul(
                                LOUTH.ap()[:, h:h + 1], DOTO.ap(),
                                GATES.ap()[:, tt, h:h + 1]),
                                deps=[dt2], label=f"lout{tt}_{h}")
                            if h == 0:
                                odep = [lo, f"gbout{tt}_{h}"]
                                if qt > 0 and ti == 0:
                                    odep.append(f"qjoin{qt - 1}")
                                sc.add("v", lambda e, ti=ti, h=h:
                                    e.tensor_scalar(
                                        out=OPK.ap()[:, ti, :],
                                        in0=BOUTH.ap()[:, h % 2, :],
                                        scalar1=LOUTH.ap()[:, h:h + 1],
                                        scalar2=None, op0=ALU.mult),
                                    deps=odep, label=f"opk{tt}_{h}")
                            else:
                                sc.add("v", lambda e, ti=ti, h=h:
                                    e.scalar_tensor_tensor(
                                        out=OPK.ap()[:, ti, :],
                                        in0=BOUTH.ap()[:, h % 2, :],
                                        scalar=LOUTH.ap()[:, h:h + 1],
                                        in1=OPK.ap()[:, ti, :],
                                        op0=ALU.mult, op1=ALU.add),
                                    deps=[lo, f"gbout{tt}_{h}",
                                          f"opk{tt}_{h - 1}"],
                                    label=f"opk{tt}_{h}")
                    for ti in range(4):
                        for dch in range(NDI):
                            k = trseq[0]
                            trseq[0] += 1
                            pt = PTR0 if k % 2 == 0 else PTR1
                            pdep = []
                            if k >= 2:
                                pdep = [f"trev{k - 2}"]
                            tdep = [f"gel{tts[ti]}"] + pdep
                            if qt > 0:
                                tdep.append(f"womG{qt - 1}_7_{dch}")
                            tr = sc.add("t", lambda e, pt=pt, ti=ti, dch=dch:
                                e.transpose(
                                    pt.ap(),
                                    HIDTM.ap()[:, ti,
                                               dch * 128:(dch + 1) * 128],
                                    IDB[:]),
                                deps=tdep, label=f"htr{qt}_{ti}_{dch}")
                            sc.add("s", lambda e, pt=pt, ti=ti, dch=dch:
                                e.copy(
                                    out=HIT.ap()[:, dch,
                                                 ti * 128:(ti + 1) * 128],
                                    in_=pt.ap()),
                                deps=[tr], label=f"trev{k}")
                            sc.lbl[f"hitev{qt}_{ti}_{dch}"] = \
                                sc.lbl[f"trev{k}"]
                    for ti in range(4):
                        for dch in range(DT):
                            k = trseq[0]
                            trseq[0] += 1
                            pt = PTR0 if k % 2 == 0 else PTR1
                            pdep = [f"trev{k - 2}"] if k >= 2 else []
                            tr = sc.add("t", lambda e, pt=pt, ti=ti, dch=dch:
                                e.transpose(
                                    pt.ap(),
                                    OPK.ap()[:, ti, dch * 128:(dch + 1) * 128],
                                    IDB[:]),
                                deps=[f"opk{tts[ti]}_7"] + pdep,
                                label=f"otr{qt}_{ti}_{dch}")
                            sc.add("v", lambda e, pt=pt, qt=qt, ti=ti,
                                   dch=dch: e.tensor_add(
                                R.ap()[:, dch, qt * 512 + ti * 128:
                                       qt * 512 + (ti + 1) * 128],
                                pt.ap(),
                                R.ap()[:, dch, qt * 512 + ti * 128:
                                       qt * 512 + (ti + 1) * 128]),
                                deps=[tr], label=f"opka{qt}_{ti}_{dch}")
                            sc.lbl[f"trev{k}"] = sc.lbl[f"opka{qt}_{ti}_{dch}"]
                    for mch in range(DT):
                        wdep = []
                        if mch >= 2:
                            wdep = [f"womG{qt}_{mch - 2}_{NDI - 1}"]
                        elif qt > 0:
                            wdep = [f"womG{qt - 1}_{6 + mch}_{NDI - 1}"]
                        lwot = sc.add("g", lambda e, mch=mch: e.dma_start(
                            out=WOTS.ap()[:, mch % 2, :, :],
                            in_=pk_w_outT[mch]),
                            deps=wdep, label=f"lwot{qt}_{mch}")
                        for pch in range(NDI):
                            dep = [lwot] + [f"hitev{qt}_{ti_}_{pch}"
                                            for ti_ in range(4)]
                            if pch == 0 and (qt > 0 or mch >= 2):
                                dep.append(f"rupG{qt}_{mch - 2}" if mch >= 2
                                           else f"rupG{qt - 1}_{6 + mch}")
                            po = POUT0 if mch % 2 == 0 else POUT1
                            sc.add("t", lambda e, po=po, mch=mch, pch=pch:
                                e.matmul(
                                    po.ap(), WOTS.ap()[:, mch % 2, pch, :],
                                    HIT.ap()[:, pch, :],
                                    start=(pch == 0), stop=(pch == NDI - 1)),
                                deps=dep, label=f"womG{qt}_{mch}_{pch}")
                        po = POUT0 if mch % 2 == 0 else POUT1
                        sc.add("v", lambda e, po=po, qt=qt, mch=mch:
                            e.tensor_add(
                                R.ap()[:, mch, qt * 512:(qt + 1) * 512],
                                po.ap(),
                                R.ap()[:, mch, qt * 512:(qt + 1) * 512]),
                            deps=[f"womG{qt}_{mch}_{NDI - 1}"] +
                                 [f"opka{qt}_{ti_}_{mch}" for ti_ in range(4)],
                            label=f"rupG{qt}_{mch}")
                    sc.add("v", lambda e: e.tensor_copy(
                        out=JNK.ap()[0:1, 2:3], in_=JNK.ap()[0:1, 18:19]),
                        deps=[f"rupG{qt}_{m}" for m in range(DT)] +
                             [f"doto{tts[3]}_{h}" for h in range(PKH)] +
                             [f"htr{qt}_{ti_}_{NDI - 1}" for ti_ in range(4)] +
                             [f"otr{qt}_{ti_}_{DT - 1}" for ti_ in range(4)],
                        label=f"qjoin{qt}")
                if dbg == "R2":
                    dbg_dump(sc, lambda: R.ap().rearrange("p a b -> p (a b)"),
                             [f"qjoin{q_}" for q_ in range(4)])
                evG3 = sc.emit(blk, ext_deps=prev)
                _esG3.close()
                _esG23.close()
            prev = [e for v in evG3.values() for e in v]

        if "H" in phases:
            _esH0 = ExitStack()
            XLN3 = _esH0.enter_context(nc.sbuf_tensor("XLN3", [128, DT, T], BF16))
            SQH3 = _esH0.enter_context(nc.sbuf_tensor("SQH3", [128, DT, 512], BF16))
            ROWS3 = _esH0.enter_context(nc.sbuf_tensor("ROWS3", [1, 3, T], F32))
            LNA3 = _esH0.enter_context(nc.sbuf_tensor("LNA3", [128, 2, 512], F32))
            LNB3 = _esH0.enter_context(nc.sbuf_tensor("LNB3", [128, 2, 512], F32))
            BETA = _esH0.enter_context(nc.sbuf_tensor("BETA", [1, 16], F32))
            BETAE = _esH0.enter_context(nc.sbuf_tensor("BETAE", [1, 16], F32))
            PHS0 = _esH0.enter_context(nc.psum_tensor("PHS0", [1, 512], F32))
            PHS1 = _esH0.enter_context(nc.psum_tensor("PHS1", [1, 512], F32))
            PHQ0 = _esH0.enter_context(nc.psum_tensor("PHQ0", [1, 512], F32))
            PHQ1 = _esH0.enter_context(nc.psum_tensor("PHQ1", [1, 512], F32))
            PHB0 = _esH0.enter_context(nc.psum_tensor("PHB0", [128, 512], F32))
            PHB1 = _esH0.enter_context(nc.psum_tensor("PHB1", [128, 512], F32))
            if True:
                sc = Sched(ST)
                emit_ln(sc, R, XLN3, SQH3, ROWS3, LNA3, LNB3, PHS0, PHS1,
                        PHQ0, PHQ1, PHB0, PHB1, ONESCF, ONESCB, ONESRF, "ln2")
                for tt in range(TT):
                    sc.add("q", lambda e, tt=tt: e.dma_start(
                        out=xln2_dram[tt].rearrange("p (a b) -> p a b", a=DT),
                        in_=XLN3.ap()[:, :, tt * 128:(tt + 1) * 128]),
                        deps=[f"ln2_ap{(tt // 4) * DT + k}" for k in range(DT)],
                        label=f"x3s{tt}")
                lb = sc.add("q", lambda e: e.dma_start(out=BETA[:], in_=la_beta),
                            label="lbeta")
                eb = sc.add("s", lambda e: e.activation(
                    out=BETAE[:], in_=BETA[:], func=AF.Exp), deps=[lb])
                m8 = sc.add("v", lambda e: e.tensor_scalar(
                    out=BETAE[:], in0=BETAE[:], scalar1=8.0, scalar2=None,
                    op0=ALU.mult), deps=[eb])
                rc = sc.add("v", lambda e: e.reciprocal(
                    out=BETA[:], in_=BETAE[:]), deps=[m8, eb])
                sc.add("q", lambda e: e.dma_start(out=scb_dram, in_=BETA[:]),
                       deps=[rc], label="scbs")
                evH0 = sc.emit(blk, ext_deps=prev)
                _esH0.close()
            prev = [e for v in evH0.values() for e in v]

            _esH1 = ExitStack()
            XLH = _esH1.enter_context(nc.sbuf_tensor("XLH", [128, 2, DT, 128], BF16))
            WPRJ = _esH1.enter_context(nc.sbuf_tensor("WPRJ", [128, DT, D], BF16))
            QTM3 = _esH1.enter_context(nc.sbuf_tensor("QTM3", [128, 8, D], BF16))
            KTM3 = _esH1.enter_context(nc.sbuf_tensor("KTM3", [128, 8, D], BF16))
            VTM3 = _esH1.enter_context(nc.sbuf_tensor("VTM3", [128, 8, D], BF16))
            OTM3 = _esH1.enter_context(nc.sbuf_tensor("OTM3", [128, 8, D], BF16))
            OTT3 = _esH1.enter_context(nc.sbuf_tensor("OTT3", [128, DT, 1024], BF16))
            WOS3 = _esH1.enter_context(nc.sbuf_tensor("WOS3", [128, 2, DT, 128], BF16))
            SCB = _esH1.enter_context(nc.sbuf_tensor("SCB", [128, 16], F32))
            TMPH = _esH1.enter_context(nc.sbuf_tensor("TMPH", [128, D], BF16))
            SCS = _esH1.enter_context(nc.sbuf_tensor("SCS", [128, 2, 10, 16], F32))
            PXPB = _esH1.enter_context(nc.sbuf_tensor("PXPB", [128, 2, 10, 16], F32))
            MXH = _esH1.enter_context(nc.sbuf_tensor("MXH", [128, 16], F32))
            DENH = _esH1.enter_context(nc.sbuf_tensor("DENH", [128, 16], F32))
            OACCF = _esH1.enter_context(nc.sbuf_tensor("OACCF", [128, D], F32))
            PH0 = _esH1.enter_context(nc.psum_tensor("PH0", [128, 2, 512], F32))
            PH1 = _esH1.enter_context(nc.psum_tensor("PH1", [128, 2, 512], F32))
            PHT0 = _esH1.enter_context(nc.psum_tensor("PHT0", [128, 128], BF16))
            PHT1 = _esH1.enter_context(nc.psum_tensor("PHT1", [128, 128], BF16))
            PWO0 = _esH1.enter_context(nc.psum_tensor("PWO0", [128, 512], F32))
            PWO1 = _esH1.enter_context(nc.psum_tensor("PWO1", [128, 512], F32))
            pairs = [(s, sp) for s in range(4) for sp in range(s + 1)]
            from concourse.ap import AP as _APH
            for nh in range(2):
                sc = Sched(ST)
                lscb = sc.add("q", lambda e: e.dma_start(
                    out=SCB.ap(),
                    in_=_APH(scb_dram.tensor, 0, [[0, 128], [1, 16]])),
                    label="lscb")
                ph_hist = []
                for pj, bnm in enumerate(("bq", "bk", "bv")):
                    wsrc = (la_wqT, la_wkT, la_wvT)[pj]
                    dst = (QTM3, KTM3, VTM3)[pj]
                    lwp = []
                    for wh in range(2):
                        wdep = [f"qkm{pj - 1}_7_{DT - 1}"] if pj > 0 else []
                        lwp.append(sc.add("g", lambda e, wsrc=wsrc, wh=wh:
                            e.dma_start(
                                out=WPRJ.ap()[:, wh * 4:(wh + 1) * 4, :],
                                in_=wsrc.rearrange("(a p) d -> p a d", p=128)
                                [:, wh * 4:(wh + 1) * 4, :]),
                            deps=wdep, label=f"lwp{pj}_{wh}"))
                    for ti in range(8):
                        idx = pj * 8 + ti
                        s_, u_ = divmod(ti, 2)
                        gtile = s_ * 4 + nh * 2 + u_
                        lxh = sc.add("q", lambda e, ti=ti, gtile=gtile:
                            e.dma_start(
                                out=XLH.ap()[:, ti % 2, :, :]
                                .rearrange("p a b -> p (a b)"),
                                in_=xln2_dram[gtile]),
                            deps=([f"qkm{pj}_{ti - 2}_{DT - 1}"] if ti >= 2
                                  else ([f"qkm{pj - 1}_{6 + ti}_{DT - 1}"]
                                        if pj > 0 else [])),
                            label=f"xlh{pj}_{ti}")
                        ps = PH0 if idx % 2 == 0 else PH1
                        bdep = [lwp[0], lwp[1], lxh]
                        if len(ph_hist) >= 2:
                            bdep += ph_hist[-2]
                        for nch in range(2):
                            sc.add("t", lambda e, ps=ps, bnm=bnm, nch=nch:
                                e.matmul(
                                    ps.ap()[:, nch, :],
                                    ONESR.ap()[0:1, 0:128],
                                    brow(bnm, nch * 512, 512),
                                    start=True, stop=False),
                                deps=bdep, label=f"qkb{pj}_{ti}_{nch}")
                        for pch in range(DT):
                            for nch in range(2):
                                sc.add("t", lambda e, ps=ps, ti=ti, pch=pch,
                                       nch=nch: e.matmul(
                                    ps.ap()[:, nch, :],
                                    XLH.ap()[:, ti % 2, pch, :],
                                    WPRJ.ap()[:, pch,
                                              nch * 512:(nch + 1) * 512],
                                    start=False, stop=(pch == DT - 1)),
                                    deps=[f"qkb{pj}_{ti}_{nch}"] if pch == 0
                                    else [],
                                    label=(f"qkm{pj}_{ti}_{pch}" if nch == 1
                                           else None))
                        evs = [sc.add("v", lambda e, ps=ps, dst=dst,
                                       ti=ti: e.tensor_copy(
                            out=dst.ap()[:, ti, :],
                            in_=ps.ap().rearrange("p a b -> p (a b)")),
                            deps=[f"qkm{pj}_{ti}_{DT - 1}"],
                            label=f"qke{pj}_{ti}")]
                        ph_hist.append(evs)
                for u in range(2):
                    for pi, (s_, sp) in enumerate(pairs):
                        tiq = s_ * 2 + u
                        tik = sp * 2 + u
                        prevr = (f"scr{u}_{pi - 1}" if pi > 0 else
                                 (f"scr{u - 1}_9" if u > 0 else None))
                        ml = sc.add("v", lambda e, tiq=tiq, tik=tik:
                            e.tensor_mul(TMPH.ap(), QTM3.ap()[:, tiq, :],
                                         KTM3.ap()[:, tik, :]),
                            deps=[f"qke0_{tiq}", f"qke1_{tik}", prevr],
                            label=f"scm{u}_{pi}")
                        rr = sc.add("v", lambda e, u=u, pi=pi: e.tensor_reduce(
                            SCS.ap()[:, u, pi, :],
                            TMPH.ap().rearrange("p (h w) -> p h w", h=16),
                            AX.X, ALU.add), deps=[ml], label=f"scr0{u}_{pi}")
                        sc.add("v", lambda e, u=u, pi=pi: e.tensor_mul(
                            SCS.ap()[:, u, pi, :], SCS.ap()[:, u, pi, :],
                            SCB.ap()), deps=[rr, lscb],
                            label=f"scr{u}_{pi}")
                    for s_ in range(4):
                        base = s_ * (s_ + 1) // 2
                        mdep = [f"scr{u}_{base + sp}" for sp in range(s_ + 1)]
                        if s_ > 0 or u > 0:
                            mdep.append(f"sxe{u}_{base - 1}" if s_ > 0
                                        else f"sxe{u - 1}_9")
                        mx = sc.add("v", lambda e, u=u, base=base:
                            e.tensor_copy(out=MXH.ap(),
                                          in_=SCS.ap()[:, u, base, :]),
                            deps=mdep, label=f"sxm{u}_{s_}_0")
                        for sp in range(1, s_ + 1):
                            mx = sc.add("v", lambda e, u=u, base=base, sp=sp:
                                e.tensor_tensor(
                                    out=MXH.ap(), in0=MXH.ap(),
                                    in1=SCS.ap()[:, u, base + sp, :],
                                    op=ALU.max),
                                deps=[f"sxm{u}_{s_}_{sp - 1}"],
                                label=f"sxm{u}_{s_}_{sp}")
                        for sp in range(s_ + 1):
                            sb = sc.add("v", lambda e, u=u, base=base, sp=sp:
                                e.tensor_sub(
                                    PXPB.ap()[:, u, base + sp, :],
                                    SCS.ap()[:, u, base + sp, :], MXH.ap()),
                                deps=[mx, (f"sxp{u}_{s_ - 1}_{sp}"
                                           if s_ > 0 and sp < s_ else None),
                                      (f"sxp{u - 1}_3_{sp}"
                                       if s_ == 0 and u > 0 else None)],
                                label=f"sxs{u}_{s_}_{sp}")
                            sc.add("s", lambda e, u=u, base=base, sp=sp:
                                e.activation(
                                    out=PXPB.ap()[:, u, base + sp, :],
                                    in_=PXPB.ap()[:, u, base + sp, :],
                                    func=AF.Exp),
                                deps=[sb], label=f"sxe{u}_{base + sp}")
                        dn = sc.add("v", lambda e, u=u, base=base:
                            e.tensor_copy(out=DENH.ap(),
                                          in_=PXPB.ap()[:, u, base, :]),
                            deps=[f"sxe{u}_{base}",
                                  (f"sxp{u}_{s_ - 1}_{s_ - 1}" if s_ > 0 else
                                   (f"sxp{u - 1}_3_3" if u > 0 else None))],
                            label=f"sxd{u}_{s_}_0")
                        for sp in range(1, s_ + 1):
                            dn = sc.add("v", lambda e, u=u, base=base, sp=sp:
                                e.tensor_add(DENH.ap(), DENH.ap(),
                                             PXPB.ap()[:, u, base + sp, :]),
                                deps=[f"sxd{u}_{s_}_{sp - 1}",
                                      f"sxe{u}_{base + sp}"],
                                label=f"sxd{u}_{s_}_{sp}")
                        rc2 = sc.add("v", lambda e: e.reciprocal(
                            out=DENH.ap(), in_=DENH.ap()), deps=[dn],
                            label=f"sxr{u}_{s_}")
                        for sp in range(s_ + 1):
                            sc.add("v", lambda e, u=u, base=base, sp=sp:
                                e.tensor_mul(
                                    PXPB.ap()[:, u, base + sp, :],
                                    PXPB.ap()[:, u, base + sp, :], DENH.ap()),
                                deps=[rc2], label=f"sxp{u}_{s_}_{sp}")
                for u in range(2):
                    for s_ in range(4):
                        base = s_ * (s_ + 1) // 2
                        ti = s_ * 2 + u
                        for h in range(16):
                            for sp in range(s_ + 1):
                                tiv = sp * 2 + u
                                dep = [f"sxp{u}_{s_}_{sp}",
                                       f"qke2_{tiv}"]
                                if sp == 0:
                                    dep.append(f"otm{u}_{s_ - 1}" if s_ > 0
                                               else (f"otm{u - 1}_3"
                                                     if u > 0 else None))
                                    sc.add("v", lambda e, h=h, tiv=tiv, u=u,
                                           base=base: e.tensor_scalar(
                                        out=OACCF.ap()[:, h * 64:(h + 1) * 64],
                                        in0=VTM3.ap()[:, tiv,
                                                      h * 64:(h + 1) * 64],
                                        scalar1=PXPB.ap()[:, u, base, h:h + 1],
                                        scalar2=None, op0=ALU.mult),
                                        deps=dep, label=f"oas{u}_{s_}_{h}_0")
                                else:
                                    sc.add("v", lambda e, h=h, tiv=tiv, u=u,
                                           base=base, sp=sp:
                                        e.scalar_tensor_tensor(
                                            out=OACCF.ap()[:, h * 64:
                                                           (h + 1) * 64],
                                            in0=VTM3.ap()[:, tiv,
                                                          h * 64:(h + 1) * 64],
                                            scalar=PXPB.ap()[:, u, base + sp,
                                                             h:h + 1],
                                            in1=OACCF.ap()[:, h * 64:
                                                           (h + 1) * 64],
                                            op0=ALU.mult, op1=ALU.add),
                                        deps=dep + [f"oas{u}_{s_}_{h}_{sp-1}"],
                                        label=f"oas{u}_{s_}_{h}_{sp}")
                        sc.add("v", lambda e, ti=ti: e.tensor_copy(
                            out=OTM3.ap()[:, ti, :], in_=OACCF.ap()),
                            deps=[f"oas{u}_{s_}_{hh}_{s_}" for hh in range(16)],
                            label=f"otm{u}_{s_}")
                tcnt = [0]
                for ti in range(8):
                    for dch in range(DT):
                        k = tcnt[0]
                        tcnt[0] += 1
                        pt = PHT0 if k % 2 == 0 else PHT1
                        pdep = [f"otte{(k - 2) // DT}_{(k - 2) % DT}"] \
                            if k >= 2 else []
                        u, s_ = ti % 2, ti // 2
                        tr = sc.add("t", lambda e, pt=pt, ti=ti, dch=dch:
                            e.transpose(pt.ap(),
                                        OTM3.ap()[:, ti,
                                                  dch * 128:(dch + 1) * 128],
                                        IDB[:]),
                            deps=[f"otm{u}_{s_}"] + pdep,
                            label=f"ott{ti}_{dch}")
                        sc.add("s", lambda e, pt=pt, ti=ti, dch=dch: e.copy(
                            out=OTT3.ap()[:, dch, ti * 128:(ti + 1) * 128],
                            in_=pt.ap()), deps=[tr], label=f"otte{ti}_{dch}")
                for mch in range(DT):
                    wdep = [f"wo3m{mch - 2}_1_{DT - 1}"] if mch >= 2 else []
                    lwo3 = sc.add("g", lambda e, mch=mch: e.dma_start(
                        out=WOS3.ap()[:, mch % 2, :, :], in_=la_woT[mch]),
                        deps=wdep, label=f"lwo3{mch}")
                    for tch in range(2):
                        po = PWO0 if (mch * 2 + tch) % 2 == 0 else PWO1
                        bdep = [lwo3]
                        pidx = mch * 2 + tch
                        if pidx >= 2:
                            pm, ptc = divmod(pidx - 2, 2)
                            bdep += [f"rupH{pm}_{ptc}_0", f"rupH{pm}_{ptc}_1"]
                        sc.add("t", lambda e, po=po, mch=mch: e.matmul(
                            po.ap(), brow("bo", mch * 128, 128),
                            ONESR.ap()[0:1, :], start=True, stop=False),
                            deps=bdep, label=f"wo3b{mch}_{tch}")
                        for pch in range(DT):
                            sc.add("t", lambda e, po=po, mch=mch, tch=tch,
                                   pch=pch: e.matmul(
                                po.ap(), WOS3.ap()[:, mch % 2, pch, :],
                                OTT3.ap()[:, pch, tch * 512:(tch + 1) * 512],
                                start=False, stop=(pch == DT - 1)),
                                deps=([f"wo3b{mch}_{tch}"] if pch == 0
                                      else []) +
                                     [f"otte{ti_}_{pch}" for ti_ in
                                      (range(8) if pch == 0 or True else [])],
                                label=f"wo3m{mch}_{tch}_{pch}")
                        for hf in range(2):
                            sc.add("v", lambda e, po=po, mch=mch, tch=tch,
                                   hf=hf, nh=nh: e.tensor_add(
                                R.ap()[:, mch,
                                       (2 * tch + hf) * 512 + nh * 256:
                                       (2 * tch + hf) * 512 + nh * 256 + 256],
                                po.ap()[:, hf * 256:(hf + 1) * 256],
                                R.ap()[:, mch,
                                       (2 * tch + hf) * 512 + nh * 256:
                                       (2 * tch + hf) * 512 + nh * 256 + 256]),
                                deps=[f"wo3m{mch}_{tch}_{DT - 1}"],
                                label=f"rupH{mch}_{tch}_{hf}")
                evH1 = sc.emit(blk, ext_deps=prev)
                prev = [e for v in evH1.values() for e in v]
            _esH1.close()
            if dbg == "R3":
                pass

        if "I" in phases:
            _esI = ExitStack()
            QIT = _esI.enter_context(nc.sbuf_tensor("QIT", [128, MH, T], BF16))
            _esI0 = ExitStack()
            XLN4 = _esI0.enter_context(nc.sbuf_tensor("XLN4", [128, DT, T], BF16))
            _esI0a = ExitStack()
            SQH4 = _esI0a.enter_context(nc.sbuf_tensor("SQH4", [128, DT, 512], BF16))
            ROWS4 = _esI0a.enter_context(nc.sbuf_tensor("ROWS4", [1, 3, T], F32))
            LNA4 = _esI0a.enter_context(nc.sbuf_tensor("LNA4", [128, 2, 512], F32))
            LNB4 = _esI0a.enter_context(nc.sbuf_tensor("LNB4", [128, 2, 512], F32))
            PIS0 = _esI0a.enter_context(nc.psum_tensor("PIS0", [1, 512], F32))
            PIS1 = _esI0a.enter_context(nc.psum_tensor("PIS1", [1, 512], F32))
            PIQ0 = _esI0a.enter_context(nc.psum_tensor("PIQ0", [1, 512], F32))
            PIQ1 = _esI0a.enter_context(nc.psum_tensor("PIQ1", [1, 512], F32))
            PIB0 = _esI0a.enter_context(nc.psum_tensor("PIB0", [128, 512], F32))
            PIB1 = _esI0a.enter_context(nc.psum_tensor("PIB1", [128, 512], F32))
            if True:
                sc = Sched(ST)
                emit_ln(sc, R, XLN4, SQH4, ROWS4, LNA4, LNB4, PIS0, PIS1,
                        PIQ0, PIQ1, PIB0, PIB1, ONESCF, ONESCB, ONESRF, "ln3")
                evI0a = sc.emit(blk, ext_deps=prev)
                _esI0a.close()
            prev = [e for v in evI0a.values() for e in v]

            _esI0b = ExitStack()
            WMH = _esI0b.enter_context(nc.sbuf_tensor("WMH", [128, DT, D], BF16))
            WRT = _esI0b.enter_context(nc.sbuf_tensor("WRT", [128, DT, MNE], BF16))
            RL = _esI0b.enter_context(nc.sbuf_tensor("RL", [128, MNE], F32))
            M8R = _esI0b.enter_context(nc.sbuf_tensor("M8R", [128, 8], F32))
            SGG = _esI0b.enter_context(nc.sbuf_tensor("SGG", [128, 2], F32))
            EQM = _esI0b.enter_context(nc.sbuf_tensor("EQM", [128, MNE], F32))
            W16 = _esI0b.enter_context(nc.sbuf_tensor("W16", [128, MNE], F32))
            WTSB = _esI0b.enter_context(nc.sbuf_tensor("WTSB", [16, TT, 128], F32))
            PQI = _esI0b.enter_context(nc.psum_tensor("PQI", [128, 4, 512], F32))
            PRT = _esI0b.enter_context(nc.psum_tensor("PRT", [128, MNE], F32))
            PRT2 = _esI0b.enter_context(nc.psum_tensor("PRT2", [128, 128], F32))
            if True:
                sc = Sched(ST)
                lwmh = []
                for wh in range(2):
                    lwmh.append(sc.add("g", lambda e, wh=wh: e.dma_start(
                        out=WMH.ap()[:, wh * 4:(wh + 1) * 4, :],
                        in_=mo_wmhT[:, wh * 4:(wh + 1) * 4, :]),
                        label=f"lwmh{wh}"))
                lwrt = sc.add("g", lambda e: e.dma_start(
                    out=WRT.ap(),
                    in_=mo_wrT.rearrange("(a p) e -> p a e", p=128)),
                    label="lwrt")
                for hdch in range(MH):
                    for pch in range(DT):
                        for tch in range(4):
                            dep = [lwmh[pch // 4]]
                            if pch == 0 and hdch > 0:
                                dep.append(f"qie{hdch - 1}")
                            sc.add("t", lambda e, hdch=hdch, pch=pch, tch=tch:
                                e.matmul(
                                    PQI.ap()[:, tch, :],
                                    WMH.ap()[:, pch,
                                             hdch * 128:(hdch + 1) * 128],
                                    XLN4.ap()[:, pch,
                                              tch * 512:(tch + 1) * 512],
                                    start=(pch == 0), stop=(pch == DT - 1)),
                                deps=dep, label=f"qim{hdch}_{pch}_{tch}")
                    sc.add("v", lambda e, hdch=hdch: e.tensor_copy(
                        out=QIT.ap()[:, hdch, :],
                        in_=PQI.ap().rearrange("p a b -> p (a b)")),
                        deps=[f"qim{hdch}_{DT - 1}_{tc}" for tc in range(4)],
                        label=f"qie{hdch}")
                for tt in range(TT):
                    for pch in range(DT):
                        dep = [lwrt]
                        if pch == 0 and tt > 0:
                            dep.append(f"rte{tt - 1}")
                        sc.add("t", lambda e, tt=tt, pch=pch: e.matmul(
                            PRT.ap(), XLN4.ap()[:, pch, tt * 128:(tt + 1) * 128],
                            WRT.ap()[:, pch, :],
                            start=(pch == 0), stop=(pch == DT - 1)),
                            deps=dep, label=f"rtm{tt}_{pch}")
                    rte = sc.add("v", lambda e: e.tensor_copy(
                        out=RL.ap(), in_=PRT.ap()),
                        deps=[f"rtm{tt}_{DT - 1}",
                              (f"rw1_{tt - 1}" if tt > 0 else None)],
                        label=f"rte{tt}")
                    rm8 = sc.add("v", lambda e: e.max(
                        out=M8R.ap(), in_=RL.ap()),
                        deps=[rte, (f"rsg{tt - 1}" if tt > 0 else None),
                              (f"req1_{tt - 1}" if tt > 0 else None)],
                        label=f"rm8{tt}")
                    rsg = sc.add("s", lambda e: e.activation(
                        out=SGG.ap(), in_=M8R.ap()[:, 0:2], func=AF.Sigmoid),
                        deps=[rm8, (f"rw1_{tt - 1}" if tt > 0 else None)],
                        label=f"rsg{tt}")
                    rq0 = sc.add("v", lambda e: e.tensor_scalar(
                        out=EQM.ap(), in0=RL.ap(), scalar1=M8R.ap()[:, 0:1],
                        scalar2=None, op0=ALU.is_equal),
                        deps=[rm8, (f"rw0_{tt - 1}" if tt > 0 else None)],
                        label=f"req0_{tt}")
                    rw0 = sc.add("v", lambda e: e.tensor_scalar(
                        out=W16.ap(), in0=EQM.ap(), scalar1=SGG.ap()[:, 0:1],
                        scalar2=None, op0=ALU.mult),
                        deps=[rq0, rsg, (f"rtt{tt - 1}" if tt > 0 else None)],
                        label=f"rw0_{tt}")
                    rq1 = sc.add("v", lambda e: e.tensor_scalar(
                        out=EQM.ap(), in0=RL.ap(), scalar1=M8R.ap()[:, 1:2],
                        scalar2=None, op0=ALU.is_equal),
                        deps=[rw0], label=f"req1_{tt}")
                    rw1 = sc.add("v", lambda e: e.scalar_tensor_tensor(
                        out=W16.ap(), in0=EQM.ap(), scalar=SGG.ap()[:, 1:2],
                        in1=W16.ap(), op0=ALU.mult, op1=ALU.add),
                        deps=[rq1], label=f"rw1_{tt}")
                    rtt = sc.add("t", lambda e: e.transpose(
                        PRT2.ap()[0:16, :], W16.ap(), IDF[:]),
                        deps=[rw1, (f"rwe{tt - 1}" if tt > 0 else None)],
                        label=f"rtt{tt}")
                    sc.add("v", lambda e, tt=tt: e.tensor_copy(
                        out=WTSB.ap()[:, tt, :], in_=PRT2.ap()[0:16, :]),
                        deps=[rtt], label=f"rwe{tt}")
                sc.add("q", lambda e: e.dma_start(
                    out=wt_dram,
                    in_=WTSB.ap().rearrange("p a b -> p (a b)")),
                    deps=[f"rwe{tt_}" for tt_ in range(TT)], label="wts")
                evI0b = sc.emit(blk, ext_deps=prev)
                _esI0b.close()
                _esI0.close()
            prev = [e for v in evI0b.values() for e in v]

            _esIY = ExitStack()
            YT = _esIY.enter_context(nc.sbuf_tensor("YT", [128, MH, T], BF16))
            _esI1 = ExitStack()
            KFS = _esI1.enter_context(nc.sbuf_tensor("KFS", [128, 1, MES], BF16))
            VFS = _esI1.enter_context(nc.sbuf_tensor("VFS", [128, 2, 32, 128], BF16))
            ACTT = _esI1.enter_context(nc.sbuf_tensor("ACTT", [128, 2, T], BF16))
            ACTW = _esI1.enter_context(nc.sbuf_tensor("ACTW", [128, 2, T], BF16))
            WBV = _esI1.enter_context(nc.sbuf_tensor("WBV", [128, 2, T], BF16))
            PSC = _esI1.enter_context(nc.psum_tensor("PSC", [128, 4, 512], F32))
            PYT = _esI1.enter_context(nc.psum_tensor("PYT", [128, 4, 512], F32))
            if True:
                sc = Sched(ST)
                from concourse.ap import AP as _API
                for h in range(MH):
                    lkf = sc.add("g", lambda e, h=h: e.dma_start(
                        out=KFS.ap()[:, 0, :], in_=mo_kfT[h]),
                        deps=([f"scm{h - 1}_31_3"] if h >= 1 else []),
                        label=f"lkf{h}")
                    lvf = sc.add("g", lambda e, h=h: e.dma_start(
                        out=VFS.ap()[:, h % 2, :, :], in_=mo_vfT[h]),
                        deps=([f"ymm{h - 2}_31_3"] if h >= 2 else []),
                        label=f"lvf{h}")
                    for esch in range(32):
                        ee = esch // 2
                        if esch % 2 == 0:
                            wdep = [f"scl{h}_{esch - 3}"] if esch >= 3 else \
                                ([f"scl{h - 1}_31"] if h > 0 else [])
                            sc.add("g", lambda e, ee=ee: e.dma_start(
                                out=WBV.ap()[:, ee % 2, :],
                                in_=_API(wt_dram.tensor, ee * T,
                                         [[0, 128], [1, T]])),
                                deps=wdep, label=f"lwb{h}_{ee}")
                        for tch in range(4):
                            dep = [lkf]
                            if esch >= 1:
                                dep.append(f"gle{h}_{esch - 1}_{tch}")
                            elif h > 0:
                                dep.append(f"gle{h - 1}_31_{tch}")
                            sc.add("t", lambda e, h=h, esch=esch, tch=tch:
                                e.matmul(
                                    PSC.ap()[:, tch, :],
                                    KFS.ap()[:, 0,
                                             esch * 128:(esch + 1) * 128],
                                    QIT.ap()[:, h, tch * 512:(tch + 1) * 512],
                                    start=True, stop=True),
                                deps=dep, label=f"scm{h}_{esch}_{tch}")
                        for tch in range(4):
                            sc.add("s", lambda e, esch=esch, tch=tch:
                                e.activation(
                                    out=ACTT.ap()[:, esch % 2,
                                                  tch * 512:(tch + 1) * 512],
                                    in_=PSC.ap()[:, tch, :], func=AF.Gelu),
                                deps=[f"scm{h}_{esch}_{tch}",
                                      (f"scl{h}_{esch - 2}" if esch >= 2 else
                                       (f"scl{h - 1}_{30 + esch}" if h > 0
                                        else None))],
                                label=f"gle{h}_{esch}_{tch}")
                        sc.add("v", lambda e, esch=esch, ee=ee: e.tensor_mul(
                            ACTW.ap()[:, esch % 2, :],
                            ACTT.ap()[:, esch % 2, :],
                            WBV.ap()[:, ee % 2, :]),
                            deps=[f"gle{h}_{esch}_{tc}" for tc in range(4)] +
                                 [f"lwb{h}_{ee}"] +
                                 ([f"ymm{h}_{esch - 2}_3"] if esch >= 2 else
                                  ([f"ymm{h - 1}_{30 + esch}_3"] if h > 0
                                   else [])),
                            label=f"scl{h}_{esch}")
                        for tch in range(4):
                            dep = [f"scl{h}_{esch}", lvf]
                            if esch == 0:
                                dep.append(f"yev{h - 1}" if h > 0 else None)
                            sc.add("t", lambda e, h=h, esch=esch, tch=tch:
                                e.matmul(
                                    PYT.ap()[:, tch, :],
                                    VFS.ap()[:, h % 2, esch, :],
                                    ACTW.ap()[:, esch % 2,
                                              tch * 512:(tch + 1) * 512],
                                    start=(esch == 0), stop=(esch == 31)),
                                deps=dep, label=f"ymm{h}_{esch}_{tch}")
                    sc.add("v", lambda e, h=h: e.tensor_copy(
                        out=YT.ap()[:, h, :],
                        in_=PYT.ap().rearrange("p a b -> p (a b)")),
                        deps=[f"ymm{h}_31_{tc}" for tc in range(4)],
                        label=f"yev{h}")
                evI1 = sc.emit(blk, ext_deps=prev)
                _esI1.close()
            prev = [e for v in evI1.values() for e in v]

            _esI2 = ExitStack()
            WMG = _esI2.enter_context(nc.sbuf_tensor("WMG", [128, 2, DT, 128], BF16))
            PMG0 = _esI2.enter_context(nc.psum_tensor("PMG0", [128, 512], F32))
            PMG1 = _esI2.enter_context(nc.psum_tensor("PMG1", [128, 512], F32))
            if True:
                sc = Sched(ST)
                for mch in range(DT):
                    wdep = [f"mgm{mch - 2}_3_{DT - 1}"] if mch >= 2 else []
                    lwmg = sc.add("g", lambda e, mch=mch: e.dma_start(
                        out=WMG.ap()[:, mch % 2, :, :], in_=mo_wmgT[mch]),
                        deps=wdep, label=f"lwmg{mch}")
                    for tch in range(4):
                        po = PMG0 if (mch * 4 + tch) % 2 == 0 else PMG1
                        pidx = mch * 4 + tch
                        for pch in range(DT):
                            dep = [lwmg]
                            if pch == 0 and pidx >= 2:
                                pm, ptc = divmod(pidx - 2, 4)
                                dep.append(f"rupI{pm}_{ptc}")
                            sc.add("t", lambda e, po=po, mch=mch, pch=pch,
                                   tch=tch: e.matmul(
                                po.ap(), WMG.ap()[:, mch % 2, pch, :],
                                YT.ap()[:, pch, tch * 512:(tch + 1) * 512],
                                start=(pch == 0), stop=(pch == DT - 1)),
                                deps=dep, label=f"mgm{mch}_{tch}_{pch}")
                        sc.add("v", lambda e, po=po, mch=mch, tch=tch:
                            e.tensor_add(
                                R.ap()[:, mch, tch * 512:(tch + 1) * 512],
                                po.ap(),
                                R.ap()[:, mch, tch * 512:(tch + 1) * 512]),
                            deps=[f"mgm{mch}_{tch}_{DT - 1}"],
                            label=f"rupI{mch}_{tch}")
                evI2 = sc.emit(blk, ext_deps=prev)
                _esI2.close()
                _esIY.close()
                _esI.close()
            prev = [e for v in evI2.values() for e in v]

        _es8 = ExitStack()
        OTM = _es8.enter_context(nc.sbuf_tensor("OTM", [128, 2, D], F32))
        PZ0 = _es8.enter_context(nc.psum_tensor("PZ0", [128, 128], F32))
        PZ1 = _es8.enter_context(nc.psum_tensor("PZ1", [128, 128], F32))
        if True:
            sc = Sched(ST)
            for tt in range(TT):
                evs = []
                for k in range(DT):
                    i = tt * DT + k
                    ps = PZ0 if i % 2 == 0 else PZ1
                    tr = sc.add("t", lambda e, ps=ps, k=k, tt=tt: e.transpose(
                        ps.ap(), R.ap()[:, k, tt * 128:(tt + 1) * 128],
                        IDF[:]),
                        deps=[(f"ov{i - 2}" if i >= 2 else None)],
                        label=f"ot{i}")
                    evs.append(sc.add("v", lambda e, ps=ps, k=k, tt=tt:
                        e.tensor_copy(
                            out=OTM.ap()[:, tt % 2, k * 128:(k + 1) * 128],
                            in_=ps.ap()),
                        deps=[tr] + ([f"odma{tt - 2}"]
                                     if (k == 0 and tt >= 2) else []),
                        label=f"ov{i}"))
                sc.add("q", lambda e, tt=tt: e.dma_start(
                    out=out_t[tt * 128:(tt + 1) * 128, :],
                    in_=OTM.ap()[:, tt % 2, :]), deps=evs +
                    ([f"odma{tt - 2}"] if tt >= 2 else []),
                    label=f"odma{tt}")
            sc.add("v", lambda e: e.tensor_copy(
                out=JNK.ap()[0:1, 0:1], in_=JNK.ap()[0:1, 16:17]),
                deps=[f"odma{tt_}" for tt_ in range(TT)],
                label="final_join")
            evZ = sc.emit(blk, ext_deps=prev)
            _es8.close()

    _es1.close()
    return nc


def bass_diag_dst(t_dram):
    from concourse.ap import AP as _AP
    return _AP(t_dram.tensor, 0,
               [[(TPAD + 1) * CIN, Q], [CIN, Q], [1, CIN]])


def emit_ln(sc, R, XLN, SQH, ROWS, LNA, LNB, PSS0, PSS1, PSQ0, PSQ1,
            PSB0, PSB1, ONESCF, ONESCB, ONESRF, tag):
    for tc in range(4):
        pss = PSS0 if tc % 2 == 0 else PSS1
        psq = PSQ0 if tc % 2 == 0 else PSQ1
        for k in range(DT):
            dep = [f"{tag}_mq{tc - 1}_{DT - 1}"] if tc > 0 else []
            sc.add("v", lambda e, tc=tc, k=k: e.tensor_mul(
                SQH.ap()[:, k, :],
                R.ap()[:, k, tc * 512:(tc + 1) * 512],
                R.ap()[:, k, tc * 512:(tc + 1) * 512]),
                deps=dep, label=f"{tag}_sq{tc}_{k}")
        for k in range(DT):
            dep = [f"{tag}_rows{tc - 2}"] if tc >= 2 else []
            sc.add("t", lambda e, tc=tc, k=k, pss=pss: e.matmul(
                pss.ap(), ONESCF.ap(),
                R.ap()[:, k, tc * 512:(tc + 1) * 512],
                start=(k == 0), stop=(k == DT - 1)),
                deps=dep, label=f"{tag}_ms{tc}_{k}")
        for k in range(DT):
            dep = [f"{tag}_sq{tc}_{k}"]
            if tc >= 2:
                dep.append(f"{tag}_rowq{tc - 2}")
            sc.add("t", lambda e, tc=tc, k=k, psq=psq: e.matmul(
                psq.ap(), ONESCB.ap(), SQH.ap()[:, k, :],
                start=(k == 0), stop=(k == DT - 1)),
                deps=dep, label=f"{tag}_mq{tc}_{k}")
        sc.add("v", lambda e, tc=tc, pss=pss: e.tensor_copy(
            out=ROWS.ap()[0:1, 0, tc * 512:(tc + 1) * 512], in_=pss.ap()),
            deps=[f"{tag}_ms{tc}_{DT - 1}"], label=f"{tag}_rows{tc}")
        sc.add("v", lambda e, tc=tc, psq=psq: e.tensor_copy(
            out=ROWS.ap()[0:1, 1, tc * 512:(tc + 1) * 512], in_=psq.ap()),
            deps=[f"{tag}_mq{tc}_{DT - 1}"], label=f"{tag}_rowq{tc}")
    rowdeps = [f"{tag}_rows{tc}" for tc in range(4)] + \
              [f"{tag}_rowq{tc}" for tc in range(4)]
    l1 = sc.add("v", lambda e: e.tensor_scalar(
        out=ROWS.ap()[0:1, 0, :], in0=ROWS.ap()[0:1, 0, :],
        scalar1=1.0 / D, scalar2=None, op0=ALU.mult), deps=rowdeps,
        label=f"{tag}_mean")
    l2 = sc.add("v", lambda e: e.tensor_scalar(
        out=ROWS.ap()[0:1, 1, :], in0=ROWS.ap()[0:1, 1, :],
        scalar1=1.0 / D, scalar2=None, op0=ALU.mult), deps=rowdeps,
        label=f"{tag}_msq")
    l3 = sc.add("s", lambda e: e.activation(
        out=ROWS.ap()[0:1, 2, :], in_=ROWS.ap()[0:1, 0, :], func=AF.Square),
        deps=[l1], label=f"{tag}_m2")
    l4 = sc.add("v", lambda e: e.tensor_sub(
        ROWS.ap()[0:1, 1, :], ROWS.ap()[0:1, 1, :], ROWS.ap()[0:1, 2, :]),
        deps=[l2, l3], label=f"{tag}_var")
    l4b = sc.add("v", lambda e: e.tensor_scalar(
        out=ROWS.ap()[0:1, 1, :], in0=ROWS.ap()[0:1, 1, :],
        scalar1=LN_EPS, scalar2=None, op0=ALU.add), deps=[l4],
        label=f"{tag}_vare")
    l5 = sc.add("s", lambda e: e.activation(
        out=ROWS.ap()[0:1, 1, :], in_=ROWS.ap()[0:1, 1, :], func=AF.Sqrt),
        deps=[l4b], label=f"{tag}_sd")
    l6 = sc.add("v", lambda e: e.reciprocal(
        out=ROWS.ap()[0:1, 2, :], in_=ROWS.ap()[0:1, 1, :]),
        deps=[l5, l3], label=f"{tag}_rstd")
    l7 = sc.add("v", lambda e: e.scalar_tensor_tensor(
        out=ROWS.ap()[0:1, 0, :], in0=ROWS.ap()[0:1, 0, :], scalar=-1.0,
        in1=ROWS.ap()[0:1, 2, :], op0=ALU.mult, op1=ALU.mult),
        deps=[l6], label=f"{tag}_nmr")
    for tc in range(4):
        pa = PSB0 if tc % 2 == 0 else PSB1
        dep_a = [l6] + ([f"{tag}_apm{(tc - 2) * DT + DT - 1}"]
                        if tc >= 2 else [])
        ba = sc.add("t", lambda e, tc=tc, pa=pa: e.matmul(
            pa.ap(), ONESRF.ap()[0:1, 0:128],
            ROWS.ap()[0:1, 2, tc * 512:(tc + 1) * 512],
            start=True, stop=True), deps=dep_a, label=f"{tag}_ba{tc}")
        av = sc.add("v", lambda e, tc=tc, pa=pa: e.tensor_copy(
            out=LNA.ap()[:, tc % 2, :], in_=pa.ap()),
            deps=[ba], label=f"{tag}_bav{tc}")
        bb = sc.add("t", lambda e, tc=tc, pa=pa: e.matmul(
            pa.ap(), ONESRF.ap()[0:1, 0:128],
            ROWS.ap()[0:1, 0, tc * 512:(tc + 1) * 512],
            start=True, stop=True), deps=[l7, av], label=f"{tag}_bb{tc}")
        bv_ = sc.add("v", lambda e, tc=tc, pa=pa: e.tensor_copy(
            out=LNB.ap()[:, tc % 2, :], in_=pa.ap()),
            deps=[bb], label=f"{tag}_bbv{tc}")
        for k in range(DT):
            i = tc * DT + k
            sc.add("v", lambda e, tc=tc, k=k: e.tensor_mul(
                XLN.ap()[:, k, tc * 512:(tc + 1) * 512],
                R.ap()[:, k, tc * 512:(tc + 1) * 512],
                LNA.ap()[:, tc % 2, :]),
                deps=[av, f"{tag}_mq3_{DT - 1}"], label=f"{tag}_apm{i}")
            sc.add("v", lambda e, tc=tc, k=k: e.tensor_add(
                XLN.ap()[:, k, tc * 512:(tc + 1) * 512],
                XLN.ap()[:, k, tc * 512:(tc + 1) * 512],
                LNB.ap()[:, tc % 2, :]),
                deps=[f"{tag}_apm{i}", bv_], label=f"{tag}_ap{i}")
    return [f"{tag}_ap{i}" for i in range(4 * DT)]


_PROG_CACHE = {}


def _host_prep(inputs):
    f32 = np.float32
    ai = {k: np.asarray(v) for k, v in inputs.items()}
    x = np.ascontiguousarray(ai["L0_attn_x"], dtype=f32)
    common = {}
    common["w_in"] = np.ascontiguousarray(ai["L0_attn_w_in"], f32)
    common["b_in"] = np.ascontiguousarray(ai["L0_attn_b_in"], f32).reshape(1, CIN)
    common["w_gate"] = np.ascontiguousarray(ai["L0_attn_w_gate"], f32)
    common["b_gate"] = np.ascontiguousarray(ai["L0_attn_b_gate"], f32).reshape(1, CIN)
    common["w_out"] = np.ascontiguousarray(ai["L0_attn_w_out"], f32)
    common["b_out"] = np.ascontiguousarray(ai["L0_attn_b_out"], f32).reshape(1, D)
    common["pk_keysT"] = np.ascontiguousarray(
        np.asarray(ai["L0_ffn_keys"], f32).transpose(2, 0, 3, 1))
    common["pk_wq"] = np.ascontiguousarray(ai["L0_ffn_Wq"], f32)
    common["pk_w_inT"] = np.ascontiguousarray(
        np.asarray(ai["L0_ffn_W_in"], f32).T
        .reshape(DT, 128, DIN0).transpose(1, 0, 2))
    common["pk_w_outT"] = np.ascontiguousarray(
        np.asarray(ai["L0_ffn_W_out"], f32).T
        .reshape(DIN0 // 128, 128, DT, 128).transpose(2, 1, 0, 3))
    common["pk_in_a"] = np.ascontiguousarray(ai["L0_ffn_in_a"], f32)
    common["pk_in_b"] = np.ascontiguousarray(ai["L0_ffn_in_b"], f32)
    common["pk_out_a"] = np.ascontiguousarray(ai["L0_ffn_out_a"], f32)
    common["pk_out_b"] = np.ascontiguousarray(ai["L0_ffn_out_b"], f32)
    common["la_wqT"] = np.ascontiguousarray(np.asarray(ai["L1_attn_Wq"], f32).T)
    common["la_wkT"] = np.ascontiguousarray(np.asarray(ai["L1_attn_Wk"], f32).T)
    common["la_wvT"] = np.ascontiguousarray(np.asarray(ai["L1_attn_Wv"], f32).T)
    common["la_woT"] = np.ascontiguousarray(
        np.asarray(ai["L1_attn_Wo"], f32).T
        .reshape(DT, 128, DT, 128).transpose(2, 1, 0, 3))
    for bn in ("bq", "bk", "bv", "bo"):
        common["la_" + bn] = np.ascontiguousarray(
            ai["L1_attn_" + bn], f32).reshape(1, D)
    common["la_beta"] = np.ascontiguousarray(
        ai["L1_attn_beta"], f32).reshape(1, 16)
    common["mo_wmhT"] = np.ascontiguousarray(
        np.asarray(ai["L1_ffn_w_mh"], f32).T
        .reshape(DT, 128, D).transpose(1, 0, 2))
    common["mo_wrT"] = np.ascontiguousarray(
        np.asarray(ai["L1_ffn_w_router"], f32).T)
    common["mo_kfT"] = np.ascontiguousarray(
        np.asarray(ai["L1_ffn_k_ffwd"], f32)
        .reshape(MH, MES, MD).transpose(0, 2, 1))
    common["mo_vfT"] = np.ascontiguousarray(
        np.asarray(ai["L1_ffn_v_ffwd"], f32)
        .reshape(MH, MNE, 2, 128, 128).transpose(0, 3, 1, 2, 4)
        .reshape(MH, 128, 32, 128))
    common["mo_wmgT"] = np.ascontiguousarray(
        np.asarray(ai["L1_ffn_w_mg"], f32).T
        .reshape(DT, 128, DT, 128).transpose(2, 1, 0, 3))
    common["c_identf"] = np.eye(128, dtype=f32)
    common["c_onesf_r"] = np.ones((1, 512), f32)
    common["c_onesf_c"] = np.ones((128, 1), f32)
    bdo = np.zeros((128, 8), f32)
    for g in range(8):
        bdo[g * 16:(g + 1) * 16, g] = 1.0
    common["c_bdones"] = bdo
    A = np.ascontiguousarray(ai["L0_attn_A"], f32)
    Bs = np.ascontiguousarray(ai["L0_attn_B_ssm"], f32)
    Cs = np.ascontiguousarray(ai["L0_attn_C_ssm"], f32)
    in_maps = []
    for core in range(NCORE):
        m = dict(common)
        m["x_slice"] = np.ascontiguousarray(
            x[:, core * NLOC:(core + 1) * NLOC, :].reshape(T, D))
        m["core_id"] = np.array([[core]], dtype=np.uint32)
        cs0 = core * CSH
        m["A_shard"] = np.ascontiguousarray(A[cs0:cs0 + CSH])
        m["B_shard"] = np.ascontiguousarray(Bs[cs0:cs0 + CSH])
        m["C_shard"] = np.ascontiguousarray(Cs[cs0:cs0 + CSH])
        in_maps.append(m)
    return in_maps


def _run(dbg=None):
    key = dbg
    if key not in _PROG_CACHE:
        nc = build_program(dbg=dbg)
        mybir.codegen_inst_isa_subclasses(nc)
        _PROG_CACHE[key] = nc
    return _PROG_CACHE[key]


def kernel(**inputs):
    in_maps = _host_prep(inputs)
    nc = _run(None)
    res = run_bass_kernel_spmd(nc, in_maps, core_ids=list(range(NCORE)))
    outs = [res.results[c]["out"] for c in range(NCORE)]
    full = np.zeros((B, SEQ, D), np.float32)
    for c in range(NCORE):
        full[:, c * NLOC:(c + 1) * NLOC, :] = outs[c].reshape(B, NLOC, D)
    return full
